# version 01
# baseline (speedup 1.0000x reference)
import numpy as np
import ml_dtypes
from contextlib import ExitStack

import concourse.bass as bass
import concourse.mybir as mybir
from concourse.bass_utils import run_bass_kernel_spmd

F32 = mybir.dt.float32
BF16 = mybir.dt.bfloat16
AF = mybir.ActivationFunctionType
ALU = mybir.AluOpType
AXX = mybir.AxisListType.X

D = 2048
NK = 16
TOWN = 2048
THALO = 128
TT_ = TOWN + THALO
H, DK, DV = 16, 128, 128
CCH = 128
NCH = TOWN // CCH
LORA_ALPHA = 16.0

PHASES = ("p1", "p2", "p3", "p4", "p5", "p6", "p7")
DBG_OUT = ()
P3_SWEEP1_ONLY = False
P1_TRUNC = "full"
P1_SKIP_LORA = False

ENGINES = ("sync", "scalar", "tensor", "vector", "gpsimd")


class Rec:
    def __init__(self):
        self.items = []

    def __getattr__(self, name):
        if name.startswith("_"):
            raise AttributeError(name)
        def call(*a, **k):
            it = [name, a, k, None]
            self.items.append(it)
            class _P:
                def then_inc(_self, sem, n):
                    it[3] = (sem, n)
                    return _self
            return _P()
        return call

    def replay(self, eng):
        for name, a, k, inc in self.items:
            r = getattr(eng, name)(*a, **k)
            if inc is not None:
                r.then_inc(*inc)


class Sem:
    def __init__(self, handle, name):
        self.h = handle
        self.name = name
        self.n = 0


class B:
    def __init__(self, nc, stack):
        self.nc = nc
        self.stack = stack
        self.r = {e: Rec() for e in ENGINES}
        self.sems = {}

    def sem(self, name):
        s = Sem(self.stack.enter_context(self.nc.semaphore(name)), name)
        self.sems[name] = s
        return s

    def slot_sems(self, name, n):
        return [self.sem(f"{name}{i}") for i in range(n)]

    def dma(self, queue, out, in_, sem):
        self.r[queue].dma_start(out=out, in_=in_).then_inc(sem.h, 16)
        sem.n += 16
        return sem.n

    def wait(self, eng, sem, val=None):
        self.r[eng].wait_ge(sem.h, sem.n if val is None else val)

    def inc(self, eng, sem):
        it = self.r[eng].items[-1]
        assert it[3] is None
        sem.n += 1
        it[3] = (sem.h, 1)
        return sem.n

    def finalize_hazards(self):
        from concourse.bass import AP
        skip = {"wait_ge", "nop", "sem_inc", "dma_start", "dma_start_transpose"}
        for ename in ENGINES:
            rec = self.r[ename]
            if not rec.items:
                continue
            asem = self.sem("AUTO_" + ename)

            def outs_ins(name, a, k):
                outs, ins = set(), set()
                for key, v in list(enumerate(a)) + list(k.items()):
                    if isinstance(v, AP):
                        tn = v.tensor.name
                        if key == 0 or key in ("out", "accum_out", "out_max", "out_indices"):
                            outs.add(tn)
                        else:
                            ins.add(tn)
                return outs, ins

            new_items = []
            hist = []
            for it in rec.items:
                name, a, k, inc = it
                if name in skip:
                    new_items.append(it)
                    continue
                outs, ins = outs_ins(name, a, k)
                producer = None
                for pit, pouts in hist[-4:]:
                    if ins & pouts:
                        producer = pit
                if producer is not None:
                    if producer[3] is None:
                        producer[3] = (asem, "AUTOINC")
                    new_items.append(["wait_ge", ("EDGE", producer), {}, None])
                hist.append((it, outs))
                new_items.append(it)
            cnt = 0
            counts = {}
            prod_val = {}
            for it in new_items:
                if it[3] is not None and isinstance(it[3], tuple):
                    if it[3][1] == "AUTOINC":
                        cnt += 1
                        it[3] = (asem.h, 1)
                hnd, n = it[3] if it[3] is not None else (None, 0)
                if hnd is not None:
                    key = id(hnd)
                    c = counts.get(key, 0) + n
                    counts[key] = c
                    prod_val[id(it)] = (hnd, c)
            for it in new_items:
                if it[0] == "wait_ge" and len(it[1]) == 2 and it[1][0] == "EDGE":
                    hnd, val = prod_val[id(it[1][1])]
                    it[1] = (hnd, val)
            asem.n = cnt
            rec.items = new_items

    def barrier(self):
        bar = self.sems["BAR"]
        for name, sm in self.sems.items():
            if sm.n == 0 or name in ("TE", "VE", "SE", "GP", "BAR", "CC") \
                    or name.startswith("AUTO"):
                continue
            for eng in ("sync", "scalar", "gpsimd"):
                self.wait(eng, sm)
        target = bar.n + len(ENGINES)
        for e in ENGINES:
            self.r[e].nop().then_inc(bar.h, 1)
        bar.n = target
        for e in ENGINES:
            self.r[e].wait_ge(bar.h, target)


def build_program():
    nc = bass.Bass("TRN2", target_bir_lowering=False, debug=False,
                   detect_race_conditions=False)
    stack = ExitStack()
    b = B(nc, stack)
    ap = {}

    def inp(name, shape, dtype=F32):
        ap[name] = nc.dram_tensor(name, list(shape), dtype, kind="ExternalInput").ap()
        return ap[name]

    inp("x_own", (TOWN, D)); inp("x_halo", (THALO, D))
    for w in ("wq", "wk", "wv", "wo"):
        inp(w, (D, D))
    inp("conv_qT", (128, 64)); inp("conv_kT", (128, 64)); inp("conv_vT", (128, 64))
    inp("wfa", (D, 128)); inp("wfb", (128, 16)); inp("wb", (D, 16))
    inp("wga", (D, 128)); inp("wgb", (128, D))
    inp("a_log", (16, 1)); inp("dt_bias", (16, 1))
    inp("o_norm_w", (1, 128))
    inp("onw4", (4, 4, 128))
    inp("ln1_wT", (128, 16)); inp("ln2_wT", (128, 16))
    inp("w_gate", (D, 8192)); inp("w_up", (D, 8192)); inp("w_down", (8192, D))
    inp("fwT", (D, D)); inp("gate_wT", (D, 8))
    inp("A_wT", (D, 128)); inp("B_wRT", (128, D))
    inp("identF", (128, 128)); inp("identB", (128, 128), BF16)
    inp("onesColF", (128, 1)); inp("onesRowF", (1, 128)); inp("ones16", (16, CCH))
    inp("mUIT", (128, 128), BF16); inp("mLST", (128, 128), BF16)
    inp("selm4", (16, 4, 512))
    inp("Ib4", (128, 512), BF16); inp("twoIb4", (128, 512))
    inp("E8", (8, 128), BF16)
    inp("halo_mask", (128, 1)); inp("mA", (128, 1)); inp("mB", (128, 1))
    inp("sel16", (16, 2048))

    y = nc.dram_tensor("y", [TOWN, D], F32, kind="ExternalOutput").ap()

    scr = {}
    def scratch(name, shape, dtype=F32):
        scr[name] = nc.dram_tensor(name, list(shape), dtype, kind="Internal")
        return scr[name]

    scratch("x1scr", (D, TT_))
    scratch("xiscr", (D, TOWN))
    scratch("x2scr", (D, TOWN))
    scratch("qfm", (D, TOWN), BF16)
    scratch("kfm", (D, TOWN), BF16)
    scratch("ktok", (H, TOWN, DK), BF16)
    scratch("vtok", (H, TOWN, DV), BF16)
    scratch("ofm", (D, TOWN), BF16)
    scratch("uv_s", (NCH, 128, H * CCH), BF16)
    scratch("wiT_s", (NCH, 128, H * CCH), BF16)
    scratch("w2T_s", (NCH, 128, H * CCH), BF16)
    scratch("qh_s", (NCH, 128, H * CCH), BF16)
    scratch("kb_s", (NCH, 128, H * CCH), BF16)
    scratch("gcc_s", (NCH, 128, 16))
    scratch("mscr", (64, 128, TOWN), BF16)
    scratch("wdc", (64, 128, D), BF16)
    scratch("pcdbg", (128, NK, 520), BF16)
    scratch("scomb_dbg", (128, 2048))
    scratch("ma_dbg", (128, 1))
    scratch("g_dbg", (16, TOWN))
    scratch("beta_dbg", (16, TOWN))
    scratch("hga_dbg", (128, TOWN), BF16)
    scratch("cc_in", (128, H * DK))
    scratch("cc_out", (256, H * DK))

    dbg_outs = {}
    for name in DBG_OUT:
        t = scr[name]
        dbg_outs[name] = nc.dram_tensor("dbg_" + name, list(t.shape), t.dtype,
                                        kind="ExternalOutput").ap()

    E = stack.enter_context
    for s in ("LD", "CA", "ST", "TE", "VE", "SE", "GP", "BAR", "CC"):
        b.sem(s)
    LD, CA, ST = b.sems["LD"], b.sems["CA"], b.sems["ST"]
    TE, VE, SE, CC = b.sems["TE"], b.sems["VE"], b.sems["SE"], b.sems["CC"]
    SY, SC, TN, VR, GP = (b.r["sync"], b.r["scalar"], b.r["tensor"],
                          b.r["vector"], b.r["gpsimd"])

    identF = E(nc.sbuf_tensor("s_identF", [128, 128], F32))
    identB = E(nc.sbuf_tensor("s_identB", [128, 128], BF16))
    onesColF = E(nc.sbuf_tensor("s_onesColF", [128, 1], F32))
    onesRowF = E(nc.sbuf_tensor("s_onesRowF", [1, 128], F32))
    ones16 = E(nc.sbuf_tensor("s_ones16", [16, CCH], F32))
    mUIT = E(nc.sbuf_tensor("s_mUIT", [128, 128], BF16))
    mLST = E(nc.sbuf_tensor("s_mLST", [128, 128], BF16))
    selm4 = E(nc.sbuf_tensor("s_selm4", [16, 4, 512], F32))
    Ib4 = E(nc.sbuf_tensor("s_Ib4", [128, 512], BF16))
    twoIb4 = E(nc.sbuf_tensor("s_twoIb4", [128, 512], F32))
    E8 = E(nc.sbuf_tensor("s_E8", [8, 128], BF16))
    halo_m = E(nc.sbuf_tensor("s_halo_m", [128, 1], F32))
    mA = E(nc.sbuf_tensor("s_mA", [128, 1], F32))
    mB = E(nc.sbuf_tensor("s_mB", [128, 1], F32))
    gwt = E(nc.sbuf_tensor("s_gwt", [128, NK, 8], BF16))
    at = E(nc.sbuf_tensor("s_at", [128, NK, 128], BF16))
    sel16 = E(nc.sbuf_tensor("s_sel16", [16, 2048], F32))
    bt = E(nc.sbuf_tensor("s_bt", [128, NK, 128], BF16))

    for name, t in (("identF", identF), ("identB", identB), ("onesColF", onesColF),
                    ("onesRowF", onesRowF), ("ones16", ones16), ("mUIT", mUIT),
                    ("mLST", mLST), ("selm4", selm4), ("Ib4", Ib4), ("twoIb4", twoIb4), ("E8", E8),
                    ("halo_mask", halo_m), ("mA", mA), ("mB", mB), ("sel16", sel16)):
        b.dma("sync", t[:], ap[name], LD)
    b.wait("tensor", LD); b.wait("vector", LD); b.wait("scalar", LD); b.wait("gpsimd", LD)

    x1v = scr["x1scr"].ap().rearrange("(k r) t -> r k t", r=128)
    xiv = scr["xiscr"].ap().rearrange("(k r) t -> r k t", r=128)
    x2v = scr["x2scr"].ap().rearrange("(k r) t -> r k t", r=128)
    qfmv = scr["qfm"].ap().rearrange("(k r) t -> r k t", r=128)
    kfmv = scr["kfm"].ap().rearrange("(k r) t -> r k t", r=128)
    ofmv = scr["ofm"].ap().rearrange("(k r) t -> r k t", r=128)

    class TransposeHelper:
        def __init__(self, pbanks):
            self.p = pbanks
            self.last_ve = [0, 0]

        def run(self, items, ident):
            n = len(items)
            for base in range(0, n, 4):
                grp = items[base:base + 4]
                sel = (base // 4) % 2
                pp = self.p[sel]
                if self.last_ve[sel]:
                    b.wait("tensor", VE, self.last_ve[sel])
                for j, (sap, _) in enumerate(grp):
                    TN.transpose(pp[:, j * 128:(j + 1) * 128], sap, ident)
                t = b.inc("tensor", TE)
                b.wait("vector", TE, t)
                for j, (_, dap) in enumerate(grp):
                    VR.tensor_copy(dap, pp[:, j * 128:(j + 1) * 128])
                self.last_ve[sel] = b.inc("vector", VE)

    def ffn_setup():
        gwv = ap["gate_wT"].rearrange("(k r) e -> r k e", r=128)
        b.dma("gpsimd", gwt[:, :, :], gwv, CA)
        awv = ap["A_wT"].rearrange("(k r) e -> r k e", r=128)
        b.dma("gpsimd", at[:, :, :], awv, CA)
        bwv = ap["B_wRT"].rearrange("p (j c) -> p j c", j=16)
        b.dma("gpsimd", bt[:, :, :], bwv, CA)
        b.wait("tensor", CA)

    def ffn_phase(which):
        first = which == "p1"
        with ExitStack() as ps:
            PE = ps.enter_context
            xfm = PE(nc.sbuf_tensor("xfm" + which, [128, NK, 1024], F32))
            lnb = PE(nc.sbuf_tensor("lnb" + which, [128, NK, 1024], BF16))
            sq = PE(nc.sbuf_tensor("sq" + which, [128, 2, NK, 128], F32))
            xtm = PE(nc.sbuf_tensor("xtm" + which, [128, 2, D], F32))
            wt = PE(nc.sbuf_tensor("wt" + which, [128, 2, NK, 128], BF16))
            stat = PE(nc.sbuf_tensor("stat" + which, [128, 32], F32))
            colw = PE(nc.sbuf_tensor("colw" + which, [128, 16], F32))
            rowt = PE(nc.sbuf_tensor("rowt" + which, [16, 128], F32))
            cwT = PE(nc.sbuf_tensor("cwT" + which, [8, 1024], BF16))
            cwb = PE(nc.sbuf_tensor("cwb" + which, [128, 1024], BF16))
            tsc = PE(nc.sbuf_tensor("tsc" + which, [128, 1024], BF16))
            otm = PE(nc.sbuf_tensor("otm" + which, [128, 2, D], F32))
            pA = PE(nc.psum_tensor("pA" + which, [128, 1024], F32))
            pB = PE(nc.psum_tensor("pB" + which, [128, 1024], F32))
            pT = PE(nc.psum_tensor("pT" + which, [128, 512], F32))
            pT2 = PE(nc.psum_tensor("pT2" + which, [128, 512], F32))
            pTB = PE(nc.psum_tensor("pTB" + which, [128, 512], BF16))
            pSM = PE(nc.psum_tensor("pSM" + which, [128, 128], F32))
            pST = pSM

            if first:
                ffn_setup()
            fwv = ap["fwT"].rearrange("(k r) o -> r k o", r=128)
            XT = b.slot_sems("XT" + which, 2)
            WTS = b.slot_sems("WTS" + which, 2)
            OTS = b.slot_sems("OTS" + which, 2)
            xt_busy = [0, 0]
            wt_busy = [0, 0]
            ot_busy = [0, 0]

            thF = TransposeHelper([pT, pT2])
            panels = ([(THALO, "halo")] if first else []) + [(1024, 0), (1024, 1)]
            st_prev = 0
            for T, ptag in panels:
                halo = ptag == "halo"
                ntile = T // 128
                ngg = max(1, T // 512)
                wgg = min(512, T)
                b.wait("vector", ST, st_prev)
                if first:
                    src = ap["x_halo"] if halo else ap["x_own"]
                    off = 0 if halo else ptag * 1024
                    for r in range(ntile):
                        if xt_busy[r % 2]:
                            b.wait("sync", TE, xt_busy[r % 2])
                        b.dma("sync", xtm[:, r % 2, :],
                              src[off + r * 128: off + (r + 1) * 128, :], XT[r % 2])
                        b.wait("tensor", XT[r % 2], XT[r % 2].n)
                        items = [(xtm[:, r % 2, k * 128:(k + 1) * 128],
                                  xfm[:, k, r * 128:(r + 1) * 128]) for k in range(NK)]
                        thF.run(items, identF[:, :])
                        xt_busy[r % 2] = TE.n
                else:
                    b.wait("sync", TE, TE.n)
                    b.dma("sync", xfm[:, :, 0:1024], x2v[:, :, ptag * 1024:(ptag + 1) * 1024], LD)
                    b.wait("tensor", LD); b.wait("vector", LD); b.wait("scalar", LD)

                if first and P1_TRUNC == "xonly":
                    b.wait("scalar", VE)
                    c0 = 0 if halo else THALO + ptag * 1024
                    st_prev = b.dma("scalar", x1v[:, :, c0:c0 + T], xfm[:, :, 0:T], ST)
                    continue
                b.wait("scalar", VE)
                te_sq = {}
                for r in range(ntile):
                    if r >= 2:
                        b.wait("scalar", TE, te_sq[r - 2])
                    SC.activation(sq[:, r % 2, :, :], xfm[:, :, r * 128:(r + 1) * 128], AF.Square)
                    s = b.inc("scalar", SE)
                    b.wait("tensor", SE, s)
                    for k in range(NK):
                        TN.matmul(pST[:, r:r + 1], xfm[:, k, r * 128:(r + 1) * 128],
                                  onesColF[:], start=(r == 0 and k == 0), stop=False)
                        TN.matmul(pST[:, 8 + r:9 + r], sq[:, r % 2, k, :], onesColF[:],
                                  start=False, stop=(r == ntile - 1 and k == NK - 1))
                    te_sq[r] = b.inc("tensor", TE)
                t = TE.n
                b.wait("vector", TE, t)
                nt = ntile
                VR.memset(colw[:, :], 0.0)
                VR.tensor_copy(stat[:, 0:nt], pST[:, 0:nt])
                VR.tensor_copy(stat[:, 8:8 + nt], pST[:, 8:8 + nt])
                VR.tensor_scalar(colw[:, 0:nt], stat[:, 0:nt], 1.0 / D, None, ALU.mult)
                VR.tensor_mul(colw[:, 8:8 + nt], colw[:, 0:nt], colw[:, 0:nt])
                VR.scalar_tensor_tensor(colw[:, 8:8 + nt], stat[:, 8:8 + nt], 1.0 / D,
                                        colw[:, 8:8 + nt], ALU.mult, ALU.subtract)
                v = b.inc("vector", VE)
                b.wait("scalar", VE, v)
                SC.activation(colw[:, 8:8 + nt], colw[:, 8:8 + nt], AF.Sqrt, bias=np.float32(1e-5))
                s = b.inc("scalar", SE)
                b.wait("vector", SE, s)
                VR.reciprocal(colw[:, 8:8 + nt], colw[:, 8:8 + nt])
                VR.tensor_mul(colw[:, 0:nt], colw[:, 0:nt], colw[:, 8:8 + nt])
                VR.tensor_scalar(colw[:, 0:nt], colw[:, 0:nt], -1.0, None, ALU.mult)
                v = b.inc("vector", VE)
                b.wait("tensor", VE, v)
                TN.transpose(pSM[0:16, 0:128], colw[:, 0:16], identF[:, :])
                t = b.inc("tensor", TE)
                b.wait("vector", TE, t)
                VR.tensor_copy(rowt[:, :], pSM[0:16, 0:128])
                v = b.inc("vector", VE)
                b.wait("tensor", VE, v)
                for r in range(ntile):
                    TN.matmul(pA[:, r * 128:(r + 1) * 128],
                              sel16[:, (8 + r) * 128:(9 + r) * 128], rowt[:, :],
                              start=(r % 4 == 0), stop=(r % 4 == 3 or r == ntile - 1))
                    TN.matmul(pB[:, r * 128:(r + 1) * 128],
                              sel16[:, r * 128:(r + 1) * 128], rowt[:, :],
                              start=(r % 4 == 0), stop=(r % 4 == 3 or r == ntile - 1))
                t = b.inc("tensor", TE)
                b.wait("vector", TE, t)
                for k in range(NK):
                    VR.tensor_mul(lnb[:, k, 0:T], xfm[:, k, 0:T], pA[:, 0:T])
                    VR.tensor_add(lnb[:, k, 0:T], lnb[:, k, 0:T], pB[:, 0:T])
                v_lnb = b.inc("vector", VE)
                b.wait("tensor", VE, v_lnb)

                if first and P1_TRUNC == "ln":
                    qv_dbg = scr["qfm"].ap().rearrange("(k r) t -> r k t", r=128)
                    b.wait("scalar", VE)
                    if not halo:
                        b.dma("scalar", qv_dbg[:, :, ptag * 1024:ptag * 1024 + T],
                              lnb[:, :, 0:T], ST)
                    continue
                for r in range(ntile):
                    for k in range(NK):
                        TN.matmul(pSM[:, 0:8], lnb[:, k, r * 128:(r + 1) * 128],
                                  gwt[:, k, :], start=(k == 0), stop=(k == NK - 1))
                    t = b.inc("tensor", TE)
                    b.wait("vector", TE, t)
                    VR.reduce_max(stat[:, 0:1], pSM[:, 0:8], AXX)
                    VR.tensor_scalar(stat[:, 8:16], pSM[:, 0:8], stat[:, 0:1], 0.0,
                                     ALU.subtract, ALU.is_ge)
                    VR.scalar_tensor_tensor(stat[:, 16:24], stat[:, 8:16], -1e30, pSM[:, 0:8],
                                            ALU.mult, ALU.add)
                    VR.reduce_max(stat[:, 1:2], stat[:, 16:24], AXX)
                    VR.tensor_sub(stat[:, 2:3], stat[:, 1:2], stat[:, 0:1])
                    v = b.inc("vector", VE)
                    b.wait("scalar", VE, v)
                    SC.activation(stat[:, 3:4], stat[:, 2:3], AF.Sigmoid)
                    s = b.inc("scalar", SE)
                    b.wait("vector", SE, s)
                    VR.tensor_scalar(stat[:, 4:5], stat[:, 3:4], -1.0, 1.0, ALU.mult, ALU.add)
                    VR.tensor_scalar(stat[:, 24:32], stat[:, 16:24], stat[:, 1:2], 0.0,
                                     ALU.subtract, ALU.is_ge)
                    VR.tensor_scalar(stat[:, 8:16], stat[:, 8:16], stat[:, 4:5], None, ALU.mult)
                    VR.tensor_scalar(stat[:, 24:32], stat[:, 24:32], stat[:, 3:4], None, ALU.mult)
                    VR.tensor_add(stat[:, 24:32], stat[:, 24:32], stat[:, 8:16])
                    if first and P1_TRUNC == "lora" and not halo and ptag == 0:
                        VR.tensor_copy(tsc[:, r * 32:(r + 1) * 32], stat[:, 0:32])
                    v = b.inc("vector", VE)
                    b.wait("tensor", VE, v)
                    TN.transpose(pSM[0:8, 64:192] if False else pSM[0:8, 0:128],
                                 stat[:, 24:32], identF[:, :])
                    t = b.inc("tensor", TE)
                    b.wait("vector", TE, t)
                    VR.tensor_copy(cwT[:, r * 128:(r + 1) * 128], pSM[0:8, 0:128])
                    v = b.inc("vector", VE)
                    b.wait("tensor", VE, v)
                for r in range(ntile):
                    TN.matmul(pB[:, r * 128:(r + 1) * 128], E8[:], cwT[:, r * 128:(r + 1) * 128],
                              start=(r % 4 == 0), stop=(r % 4 == 3 or r == ntile - 1))
                for gg in range(ngg):
                    for k in range(NK):
                        TN.matmul(pA[:, gg * 512:gg * 512 + wgg], at[:, k, :],
                                  lnb[:, k, gg * 512:gg * 512 + wgg],
                                  start=(k == 0), stop=(k == NK - 1))
                t = b.inc("tensor", TE)
                b.wait("vector", TE, t)
                VR.tensor_copy(cwb[:, 0:T], pB[:, 0:T])
                if not (first and P1_TRUNC == "lora"):
                    for gg in range(ngg):
                        VR.scalar_tensor_tensor(tsc[:, gg * 512:gg * 512 + wgg],
                                                pA[:, gg * 512:gg * 512 + wgg], LORA_ALPHA,
                                                cwb[:, gg * 512:gg * 512 + wgg], ALU.mult, ALU.mult)
                v = b.inc("vector", VE)
                b.wait("tensor", VE, v)

                if first and P1_TRUNC == "lora" and not halo:
                    kv_dbg = scr["kfm"].ap().rearrange("(k r) t -> r k t", r=128)
                    b.wait("scalar", VE)
                    b.dma("scalar", kv_dbg[:, 0:1, ptag * 1024:ptag * 1024 + T]
                          .rearrange("r k t -> r (k t)"), tsc[:, 0:T], ST)
                    b.dma("scalar", kv_dbg[:, 1:2, ptag * 1024:ptag * 1024 + T]
                          .rearrange("r k t -> r (k t)"), cwb[:, 0:T], ST)
                if first and P1_TRUNC == "lora":
                    continue
                ve_done = {}
                te_mm = {}
                for m in range(NK):
                    bsel = m % 2
                    pO = pA if bsel == 0 else pB
                    if wt_busy[bsel]:
                        b.wait("gpsimd", TE, wt_busy[bsel])
                    b.dma("gpsimd", wt[:, bsel], fwv[:, :, m * 128:(m + 1) * 128], WTS[bsel])
                    b.wait("tensor", WTS[bsel], WTS[bsel].n)
                    if m >= 2:
                        b.wait("tensor", VE, ve_done[m - 2])
                    for gg in range(ngg):
                        for k in range(NK):
                            TN.matmul(pO[:, gg * 512:gg * 512 + wgg], wt[:, bsel, k, :],
                                      lnb[:, k, gg * 512:gg * 512 + wgg],
                                      start=(k == 0), stop=(P1_SKIP_LORA and k == NK - 1))
                        if P1_SKIP_LORA:
                            TN.matmul(pO[:, gg * 512:gg * 512 + wgg], bt[:, m, :],
                                      tsc[:, gg * 512:gg * 512 + wgg], start=False, stop=True,
                                      skip_group_check=True) if False else None
                        if not P1_SKIP_LORA:
                            TN.matmul(pO[:, gg * 512:gg * 512 + wgg], bt[:, m, :],
                                      tsc[:, gg * 512:gg * 512 + wgg], start=False, stop=True)
                    te_mm[m] = b.inc("tensor", TE)
                    wt_busy[bsel] = te_mm[m]
                    b.wait("vector", TE, te_mm[m])
                    for gg in range(ngg):
                        VR.scalar_tensor_tensor(xfm[:, m, gg * 512:gg * 512 + wgg],
                                                pO[:, gg * 512:gg * 512 + wgg], 0.5,
                                                xfm[:, m, gg * 512:gg * 512 + wgg],
                                                ALU.mult, ALU.add)
                    ve_done[m] = b.inc("vector", VE)
                b.wait("scalar", VE)
                if first:
                    c0 = 0 if halo else THALO + ptag * 1024
                    st_prev = b.dma("scalar", x1v[:, :, c0:c0 + T], xfm[:, :, 0:T], ST)
                else:
                    for r in range(ntile):
                        if ot_busy[r % 2]:
                            b.wait("vector", OTS[r % 2], ot_busy[r % 2])
                        items = [(xfm[:, k, r * 128:(r + 1) * 128],
                                  otm[:, r % 2, k * 128:(k + 1) * 128]) for k in range(NK)]
                        thF.run(items, identF[:, :])
                        b.wait("scalar", VE)
                        ot_busy[r % 2] = b.dma(
                            "scalar", y[ptag * 1024 + r * 128: ptag * 1024 + (r + 1) * 128, :],
                            otm[:, r % 2, :], OTS[r % 2])
            b.barrier()

    def proj_phase(g_res, beta_res, hga_res):
        with ExitStack() as ps:
            PE = ps.enter_context
            xp = PE(nc.sbuf_tensor("xp2", [128, NK, 512], F32))
            hb = PE(nc.sbuf_tensor("hb2", [128, NK, 512], BF16))
            sq2 = PE(nc.sbuf_tensor("sq2", [128, 2, NK, 128], F32))
            pc = PE(nc.sbuf_tensor("pc2", [128, NK, 520], BF16))
            carryW = PE(nc.sbuf_tensor("carryW2", [128, 3, NK, 3], BF16))
            wbuf = PE(nc.sbuf_tensor("wbuf2", [128, 2, 512], BF16))
            stat = PE(nc.sbuf_tensor("stat2", [128, 32], F32))
            colw = PE(nc.sbuf_tensor("colw2", [128, 16], F32))
            rowt = PE(nc.sbuf_tensor("rowt2", [16, 128], F32))
            cv = PE(nc.sbuf_tensor("cv2", [128, 512], BF16))
            qs = PE(nc.sbuf_tensor("qs2", [128, 512], BF16))
            sqq = PE(nc.sbuf_tensor("sqq2", [128, 512], F32))
            lcol = PE(nc.sbuf_tensor("lcol2", [128, 4], F32))
            tkb = PE(nc.sbuf_tensor("tkb2", [128, 2, 4, 128], BF16))
            fmb = PE(nc.sbuf_tensor("fmb2", [128, 2, 512], BF16))
            hfa = PE(nc.sbuf_tensor("hfa2", [128, 512], BF16))
            wfa_s = PE(nc.sbuf_tensor("wfa_s", [128, NK, 128], BF16))
            wga_s = PE(nc.sbuf_tensor("wga_s", [128, NK, 128], BF16))
            wb_s = PE(nc.sbuf_tensor("wb_s", [128, NK, 16], BF16))
            wfb_s = PE(nc.sbuf_tensor("wfb_s", [128, 16], BF16))
            cqT = PE(nc.sbuf_tensor("cqT", [128, 64], F32))
            ckT = PE(nc.sbuf_tensor("ckT", [128, 64], F32))
            cvT = PE(nc.sbuf_tensor("cvT", [128, 64], F32))
            ln1c = PE(nc.sbuf_tensor("ln1c", [128, 16], F32))
            acol = PE(nc.sbuf_tensor("acol", [16, 4], F32))
            zw = PE(nc.sbuf_tensor("zw2", [16, 4, 512], F32))
            pP = PE(nc.psum_tensor("pP2", [128, 2048], F32))
            pA = PE(nc.psum_tensor("pA2", [128, 512], F32))
            pB = PE(nc.psum_tensor("pB2", [128, 512], F32))
            pC = PE(nc.psum_tensor("pC2", [128, 512], F32))
            pTB = PE(nc.psum_tensor("pTB2", [128, 512], BF16))

            WSL = b.slot_sems("WSL2", 2)
            TKS = b.slot_sems("TKS2", 2)
            FMS = b.slot_sems("FMS2", 2)
            te_wslot = [0, 0]
            tk_busy = [0, 0]
            fm_busy = [0, 0]
            te_sqslot = [0, 0]
            ve_lcol = 0

            b.dma("gpsimd", wfa_s[:, :, :], ap["wfa"].rearrange("(k r) c -> r k c", r=128), CA)
            b.dma("gpsimd", wga_s[:, :, :], ap["wga"].rearrange("(k r) c -> r k c", r=128), CA)
            b.dma("gpsimd", wb_s[:, :, :], ap["wb"].rearrange("(k r) c -> r k c", r=128), CA)
            b.dma("gpsimd", wfb_s[:, :], ap["wfb"], CA)
            b.dma("sync", cqT[:, :], ap["conv_qT"], LD)
            b.dma("sync", ckT[:, :], ap["conv_kT"], LD)
            b.dma("sync", cvT[:, :], ap["conv_vT"], LD)
            b.dma("sync", ln1c[:, :], ap["ln1_wT"], LD)
            b.dma("sync", acol[0:16, 0:1], ap["a_log"], LD)
            b.dma("sync", acol[0:16, 1:2], ap["dt_bias"], LD)
            b.wait("scalar", LD)
            SC.activation(acol[0:16, 2:3], acol[0:16, 0:1], AF.Exp)
            se0 = b.inc("scalar", SE)
            b.wait("vector", SE, se0)
            VR.tensor_scalar(acol[0:16, 2:3], acol[0:16, 2:3], -1.0, None, ALU.mult)
            b.inc("vector", VE)
            b.wait("tensor", LD); b.wait("tensor", CA); b.wait("vector", LD)

            te_xp = 0
            ve_xp = 0
            panels = [(THALO, "halo")] + [(512, p) for p in range(4)]
            for T, ptag in panels:
                halo = ptag == "halo"
                nt = ntile = T // 128
                if te_xp:
                    b.wait("sync", TE, te_xp)
                    b.wait("sync", VE, ve_xp)
                c0 = 0 if halo else THALO + ptag * 512
                b.dma("sync", xp[:, :, 0:T], x1v[:, :, c0:c0 + T], LD)
                b.wait("scalar", LD); b.wait("tensor", LD); b.wait("vector", LD)
                b.wait("tensor", SE, SE.n)
                b.wait("tensor", VE, VE.n)
                for r in range(ntile):
                    if te_sqslot[r % 2]:
                        b.wait("scalar", TE, te_sqslot[r % 2])
                    SC.activation(sq2[:, r % 2, :, :], xp[:, :, r * 128:(r + 1) * 128], AF.Square)
                    s = b.inc("scalar", SE)
                    b.wait("tensor", SE, s)
                    for k in range(NK):
                        TN.matmul(pC[:, r:r + 1], xp[:, k, r * 128:(r + 1) * 128],
                                  onesColF[:], start=(r == 0 and k == 0), stop=False)
                        TN.matmul(pC[:, 8 + r:9 + r], sq2[:, r % 2, k, :], onesColF[:],
                                  start=False, stop=(r == ntile - 1 and k == NK - 1))
                    te_sqslot[r % 2] = b.inc("tensor", TE)
                b.wait("vector", TE, TE.n)
                VR.memset(colw[:, :], 0.0)
                VR.tensor_copy(stat[:, 0:nt], pC[:, 0:nt])
                VR.tensor_copy(stat[:, 8:8 + nt], pC[:, 8:8 + nt])
                VR.tensor_scalar(colw[:, 0:nt], stat[:, 0:nt], 1.0 / D, None, ALU.mult)
                VR.tensor_mul(colw[:, 8:8 + nt], colw[:, 0:nt], colw[:, 0:nt])
                VR.scalar_tensor_tensor(colw[:, 8:8 + nt], stat[:, 8:8 + nt], 1.0 / D,
                                        colw[:, 8:8 + nt], ALU.mult, ALU.subtract)
                v = b.inc("vector", VE)
                b.wait("scalar", VE, v)
                SC.activation(colw[:, 8:8 + nt], colw[:, 8:8 + nt], AF.Sqrt,
                              bias=np.float32(1e-11), scale=np.float32(1.0 + 1e-6))
                s = b.inc("scalar", SE)
                b.wait("vector", SE, s)
                VR.reciprocal(colw[:, 8:8 + nt], colw[:, 8:8 + nt])
                VR.tensor_mul(colw[:, 0:nt], colw[:, 0:nt], colw[:, 8:8 + nt])
                VR.tensor_scalar(colw[:, 0:nt], colw[:, 0:nt], -1.0, None, ALU.mult)
                v = b.inc("vector", VE)
                b.wait("tensor", VE, v)
                TN.transpose(pC[0:16, 0:128], colw[:, 0:16], identF[:, :])
                t = b.inc("tensor", TE)
                b.wait("vector", TE, t)
                VR.tensor_copy(rowt[:, :], pC[0:16, 0:128])
                v = b.inc("vector", VE)
                b.wait("tensor", VE, v)
                for r in range(ntile):
                    TN.matmul(pA[:, r * 128:(r + 1) * 128],
                              sel16[:, (8 + r) * 128:(9 + r) * 128], rowt[:, :],
                              start=(r == 0), stop=(r == ntile - 1))
                    TN.matmul(pB[:, r * 128:(r + 1) * 128],
                              sel16[:, r * 128:(r + 1) * 128], rowt[:, :],
                              start=(r == 0), stop=(r == ntile - 1))
                t = b.inc("tensor", TE)
                b.wait("vector", TE, t)
                for k in range(NK):
                    VR.scalar_tensor_tensor(hb[:, k, 0:T], xp[:, k, 0:T], ln1c[:, k:k + 1],
                                            pA[:, 0:T], ALU.mult, ALU.mult)
                    VR.scalar_tensor_tensor(hb[:, k, 0:T], pB[:, 0:T], ln1c[:, k:k + 1],
                                            hb[:, k, 0:T], ALU.mult, ALU.add)
                v_hb = b.inc("vector", VE)
                b.wait("tensor", VE, v_hb)

                for wi, (wname, convT, is_q, is_k) in enumerate(
                        (("wq", cqT, True, False), ("wk", ckT, False, True),
                         ("wv", cvT, False, False))):
                    wap = ap[wname]
                    is_v = not (is_q or is_k)
                    if not halo:
                        VR.tensor_copy(pc[:, :, 0:3], carryW[:, wi])
                        b.inc("vector", VE)
                        if DBG_OUT and ptag == 1 and wi == 1:
                            b.wait("scalar", VE, VE.n)
                            b.dma("scalar", scr["pcdbg"].ap(), pc[:, :, :], ST)
                            b.wait("scalar", ST)
                    for grp in range(4):
                        for k in range(NK):
                            ksl = k % 2
                            if te_wslot[ksl]:
                                b.wait("gpsimd", TE, te_wslot[ksl])
                            b.dma("gpsimd", wbuf[:, ksl, :],
                                  wap[k * 128:(k + 1) * 128, grp * 512:(grp + 1) * 512],
                                  WSL[ksl])
                            b.wait("tensor", WSL[ksl], WSL[ksl].n)
                            for mm_ in range(4):
                                TN.matmul(pP[:, mm_ * 512:mm_ * 512 + T],
                                          wbuf[:, ksl, mm_ * 128:(mm_ + 1) * 128],
                                          hb[:, k, 0:T], start=(k == 0), stop=(k == NK - 1))
                            te_wslot[ksl] = b.inc("tensor", TE)
                        b.wait("vector", TE, TE.n)
                        for mm_ in range(4):
                            m = grp * 4 + mm_
                            VR.tensor_copy(pc[:, m, 3:3 + T], pP[:, mm_ * 512:mm_ * 512 + T])
                        ve_pc = b.inc("vector", VE)
                        b.wait("tensor", VE, ve_pc)
                    if halo:
                        VR.tensor_scalar(carryW[:, wi], pc[:, :, T:T + 3],
                                         halo_m[:, 0:1], None, ALU.mult)
                        b.inc("vector", VE)
                        continue
                    VR.tensor_copy(carryW[:, wi], pc[:, :, T:T + 3])
                    b.inc("vector", VE)
                    for m in range(NK):
                        VR.tensor_scalar(cv[:, 0:T], pc[:, m, 3:3 + T],
                                         convT[:, 4 * m + 3:4 * m + 4], None, ALU.mult)
                        VR.scalar_tensor_tensor(cv[:, 0:T], pc[:, m, 2:2 + T],
                                                convT[:, 4 * m + 2:4 * m + 3], cv[:, 0:T],
                                                ALU.mult, ALU.add)
                        VR.scalar_tensor_tensor(cv[:, 0:T], pc[:, m, 1:1 + T],
                                                convT[:, 4 * m + 1:4 * m + 2], cv[:, 0:T],
                                                ALU.mult, ALU.add)
                        VR.scalar_tensor_tensor(cv[:, 0:T], pc[:, m, 0:T],
                                                convT[:, 4 * m:4 * m + 1], cv[:, 0:T],
                                                ALU.mult, ALU.add)
                        v = b.inc("vector", VE)
                        b.wait("scalar", VE, v)
                        SC.activation(qs[:, 0:T], cv[:, 0:T], AF.Silu)
                        s_si = b.inc("scalar", SE)
                        if not is_v:
                            SC.activation(sqq[:, 0:T], qs[:, 0:T], AF.Square)
                            s_sq = b.inc("scalar", SE)
                            b.wait("tensor", SE, s_sq)
                            if ve_lcol:
                                b.wait("tensor", VE, ve_lcol)
                            for r in range(4):
                                TN.matmul(pC[:, 16 + r:17 + r], sqq[:, r * 128:(r + 1) * 128],
                                          onesColF[:], start=(r == 0), stop=(r == 3))
                            t = b.inc("tensor", TE)
                            b.wait("vector", TE, t)
                            VR.tensor_copy(lcol[:, 0:4], pC[:, 16:20])
                            ve_lcol = b.inc("vector", VE)
                            b.wait("scalar", VE, ve_lcol)
                            if is_q:
                                SC.activation(lcol[:, 0:4], lcol[:, 0:4], AF.Sqrt,
                                              bias=np.float32(128e-6), scale=np.float32(128.0))
                            else:
                                SC.activation(lcol[:, 0:4], lcol[:, 0:4], AF.Sqrt,
                                              bias=np.float32(1e-6))
                            s = b.inc("scalar", SE)
                            b.wait("vector", SE, s)
                            VR.reciprocal(lcol[:, 0:4], lcol[:, 0:4])
                            ve_lcol = b.inc("vector", VE)
                        b.wait("tensor", SE, s_si)
                        tsl = m % 2
                        for r in range(4):
                            TN.transpose(pTB[:, r * 128:(r + 1) * 128],
                                         qs[:, r * 128:(r + 1) * 128], identB[:, :])
                        t = b.inc("tensor", TE)
                        b.wait("vector", TE, t)
                        if tk_busy[tsl]:
                            b.wait("vector", TKS[tsl], tk_busy[tsl])
                        if not is_v:
                            for r in range(4):
                                VR.tensor_scalar(tkb[:, tsl, r, :], pTB[:, r * 128:(r + 1) * 128],
                                                 lcol[:, r:r + 1], None, ALU.mult)
                        else:
                            VR.tensor_copy(tkb[:, tsl].rearrange("p a c -> p (a c)"),
                                           pTB[:, 0:512])
                        v_tk = b.inc("vector", VE)
                        if is_k or is_v:
                            dst = scr["ktok" if is_k else "vtok"].ap()[m]
                            b.wait("scalar", VE, v_tk)
                            tk_busy[tsl] = b.dma(
                                "scalar",
                                dst[ptag * 512:(ptag + 1) * 512, :]
                                .rearrange("(rr t) d -> t rr d", t=128),
                                tkb[:, tsl], TKS[tsl])
                        if not is_v:
                            b.wait("tensor", VE, v_tk)
                            fsl = m % 2
                            for r in range(4):
                                TN.transpose(pTB[:, r * 128:(r + 1) * 128],
                                             tkb[:, tsl, r, :], identB[:, :])
                            t = b.inc("tensor", TE)
                            b.wait("vector", TE, t)
                            if fm_busy[fsl]:
                                b.wait("vector", FMS[fsl], fm_busy[fsl])
                            VR.tensor_copy(fmb[:, fsl, :], pTB[:, 0:512])
                            v_fm = b.inc("vector", VE)
                            b.wait("scalar", VE, v_fm)
                            dstv = qfmv if is_q else kfmv
                            fm_busy[fsl] = b.dma(
                                "scalar", dstv[:, m, ptag * 512:(ptag + 1) * 512],
                                fmb[:, fsl, :], FMS[fsl])
                if halo:
                    te_xp = TE.n
                    ve_xp = VE.n
                    continue

                b.wait("tensor", VE, VE.n)
                for k in range(NK):
                    TN.matmul(pA[:, 0:T], wfa_s[:, k, :], hb[:, k, 0:T],
                              start=(k == 0), stop=(k == NK - 1))
                t = b.inc("tensor", TE)
                b.wait("vector", TE, t)
                VR.tensor_copy(hfa[:, 0:T], pA[:, 0:T])
                v = b.inc("vector", VE)
                b.wait("tensor", VE, v)
                TN.matmul(pB[0:16, 0:T], wfb_s[:, :], hfa[:, 0:T], start=True, stop=True)
                for k in range(NK):
                    TN.matmul(pC[0:16, 0:T], wb_s[:, k, :], hb[:, k, 0:T],
                              start=(k == 0), stop=(k == NK - 1))
                t = b.inc("tensor", TE)
                b.wait("scalar", TE, t)
                b.wait("vector", TE, t)
                VR.tensor_scalar(zw[:, 0, 0:T], pB[0:16, 0:T], acol[0:16, 1:2], None, ALU.add)
                v = b.inc("vector", VE)
                b.wait("scalar", VE, v)
                SC.activation(zw[:, 1, 0:T], zw[:, 0, 0:T], AF.Abs)
                SC.activation(zw[:, 2, 0:T], zw[:, 1, 0:T], AF.Exp, scale=np.float32(-1.0))
                SC.activation(zw[:, 2, 0:T], zw[:, 2, 0:T], AF.Ln, bias=np.float32(1.0))
                SC.activation(beta_res[:, ptag * 512:ptag * 512 + T], pC[0:16, 0:T], AF.Sigmoid)
                s = b.inc("scalar", SE)
                b.wait("vector", SE, s)
                VR.tensor_scalar_max(zw[:, 3, 0:T], zw[:, 0, 0:T], 0.0)
                VR.tensor_add(zw[:, 3, 0:T], zw[:, 3, 0:T], zw[:, 2, 0:T])
                VR.tensor_scalar(g_res[:, ptag * 512:ptag * 512 + T],
                                 zw[:, 3, 0:T], acol[0:16, 2:3], None, ALU.mult)
                v = b.inc("vector", VE)
                b.wait("tensor", VE, v)
                for k in range(NK):
                    TN.matmul(pA[:, 0:T], wga_s[:, k, :], hb[:, k, 0:T],
                              start=(k == 0), stop=(k == NK - 1))
                t = b.inc("tensor", TE)
                b.wait("vector", TE, t)
                VR.tensor_copy(hga_res[:, ptag * 512:ptag * 512 + T], pA[:, 0:T])
                b.inc("vector", VE)
                te_xp = TE.n
                ve_xp = VE.n
            b.barrier()

    def scan_phase(g_res, beta_res):
        with ExitStack() as ps:
            PE = ps.enter_context
            S = PE(nc.sbuf_tensor("S3", [128, H, DV], F32))
            S0b = PE(nc.sbuf_tensor("S0b3", [128, H, DV], BF16))
            qf = PE(nc.sbuf_tensor("qf3", [128, 2, H, CCH], BF16))
            kf = PE(nc.sbuf_tensor("kf3", [128, 2, H, CCH], BF16))
            kt = PE(nc.sbuf_tensor("kt3", [128, 2, H, DK], BF16))
            vt = PE(nc.sbuf_tensor("vt3", [128, 2, H, DV], BF16))
            Uv = PE(nc.sbuf_tensor("Uv3", [128, 2, H, DV], BF16))
            WiT = PE(nc.sbuf_tensor("WiT3", [128, 2, H, CCH], BF16))
            W2T = PE(nc.sbuf_tensor("W2T3", [128, 2, H, CCH], BF16))
            Qh = PE(nc.sbuf_tensor("Qh3", [128, 2, H, CCH], BF16))
            Kb = PE(nc.sbuf_tensor("Kb3", [128, 2, H, DK], BF16))
            Ua = PE(nc.sbuf_tensor("Ua3", [128, H, DV], BF16))
            ofmb = PE(nc.sbuf_tensor("ofmb3", [128, 2, H, CCH], BF16))
            gcc = PE(nc.sbuf_tensor("gcc3", [128, 2, 16], F32))
            DpT4 = PE(nc.sbuf_tensor("DpT4", [128, 512], BF16))
            DsL4 = PE(nc.sbuf_tensor("DsL4", [128, 512], BF16))
            Grow4 = PE(nc.sbuf_tensor("Grow4", [128, 512], BF16))
            MPOS4 = PE(nc.sbuf_tensor("MPOS4", [128, 512], BF16))
            MT4 = PE(nc.sbuf_tensor("MT4", [128, 512], BF16))
            A4 = PE(nc.sbuf_tensor("A4", [128, 512], BF16))
            W4 = PE(nc.sbuf_tensor("W4", [128, 512], BF16))
            V4 = PE(nc.sbuf_tensor("V4", [128, 512], BF16))
            G4 = PE(nc.sbuf_tensor("G4", [128, 512], BF16))
            Zv4 = PE(nc.sbuf_tensor("Zv4", [128, 512], BF16))
            Y4 = PE(nc.sbuf_tensor("Y4", [128, 512], BF16))
            ot4 = PE(nc.sbuf_tensor("ot4", [128, 512], BF16))
            cT = PE(nc.sbuf_tensor("cT3", [16, 128], F32))
            lnbT = PE(nc.sbuf_tensor("lnbT3", [16, 128], F32))
            cplusT = PE(nc.sbuf_tensor("cplusT3", [16, 128], F32))
            cT4 = PE(nc.sbuf_tensor("cT43", [16, 512], F32))
            msk4 = PE(nc.sbuf_tensor("msk43", [16, 512], F32))
            nmsk4 = PE(nc.sbuf_tensor("nmsk43", [16, 512], F32))
            cols = PE(nc.sbuf_tensor("cols3", [128, 64], F32))
            negcC = PE(nc.sbuf_tensor("negcC3", [128, 16], F32))
            ratC = PE(nc.sbuf_tensor("ratC3", [128, 16], F32))
            bgC = PE(nc.sbuf_tensor("bgC3", [128, 16], F32))
            gCcur = PE(nc.sbuf_tensor("gCcur3", [128, 16], F32))
            clast = PE(nc.sbuf_tensor("clast3", [1, 16], F32))
            s0buf = PE(nc.sbuf_tensor("s0buf3", [128, H, DV], F32))
            s1buf = PE(nc.sbuf_tensor("s1buf3", [128, H, DV], F32))
            pCR = PE(nc.psum_tensor("pCR3", [128, 512], F32))
            pMM = PE(nc.psum_tensor("pMM3", [128, 512], F32))
            pN1 = PE(nc.psum_tensor("pN13", [128, 512], F32))
            pN2 = PE(nc.psum_tensor("pN23", [128, 512], F32))
            pU = PE(nc.psum_tensor("pU3", [128, 512], F32))
            pO = PE(nc.psum_tensor("pO3", [128, 512], F32))
            pS = PE(nc.psum_tensor("pS3", [128, 512], F32))
            pOT = PE(nc.psum_tensor("pOT3", [128, 512], BF16))

            QKS = b.slot_sems("QKS3", 2)
            UVS = b.slot_sems("UVS3", 2)
            OFS = b.slot_sems("OFS3", 2)
            te_qk = [0, 0]
            wr_uv = [0, 0]
            se_uv = [0, 0]
            ve_of = [0, 0]

            ktokv = scr["ktok"].ap()
            vtokv = scr["vtok"].ap()

            def load_qkv(ch, sl):
                if te_qk[sl]:
                    b.wait("sync", TE, te_qk[sl])
                cs = slice(ch * CCH, (ch + 1) * CCH)
                b.dma("sync", qf[:, sl], qfmv[:, :, cs], QKS[sl])
                b.dma("sync", kf[:, sl], kfmv[:, :, cs], QKS[sl])
                b.dma("sync", kt[:, sl],
                      ktokv[:, cs, :].rearrange("h t d -> t h d"), QKS[sl])
                b.dma("sync", vt[:, sl],
                      vtokv[:, cs, :].rearrange("h t d -> t h d"), QKS[sl])

            def load_pass2(ch, sl):
                if wr_uv[sl]:
                    b.wait("sync", VE, wr_uv[sl])
                if te_qk[sl]:
                    b.wait("sync", TE, te_qk[sl])
                if uv_store[sl]:
                    b.wait("sync", UVS[sl], uv_store[sl])
                b.wait("sync", SE, se_uv[sl])
                b.dma("sync", Uv[:, sl], scr["uv_s"].ap()[ch].rearrange("p (h c) -> p h c", h=H), UVS[sl])
                b.dma("sync", WiT[:, sl], scr["wiT_s"].ap()[ch].rearrange("p (h c) -> p h c", h=H), UVS[sl])
                b.dma("sync", W2T[:, sl], scr["w2T_s"].ap()[ch].rearrange("p (h c) -> p h c", h=H), UVS[sl])
                b.dma("sync", Qh[:, sl], scr["qh_s"].ap()[ch].rearrange("p (h c) -> p h c", h=H), UVS[sl])
                b.dma("sync", Kb[:, sl], scr["kb_s"].ap()[ch].rearrange("p (h c) -> p h c", h=H), UVS[sl])
                b.dma("sync", gcc[:, sl], scr["gcc_s"].ap()[ch], UVS[sl])

            def sweep(pass2):
                for ch in range(NCH):
                    sl = ch % 2
                    if not pass2:
                        if ch == 0:
                            load_qkv(0, 0)
                        if ch + 1 < NCH:
                            load_qkv(ch + 1, (ch + 1) % 2)
                        b.wait("vector", QKS[sl], QKS[sl].n)
                        b.wait("tensor", QKS[sl], QKS[sl].n)
                        if uv_store[sl]:
                            b.wait("vector", UVS[sl], uv_store[sl])
                            b.wait("scalar", UVS[sl], uv_store[sl])
                    else:
                        if ch == 0:
                            load_pass2(0, 0)
                        if ch + 1 < NCH:
                            load_pass2(ch + 1, (ch + 1) % 2)
                        b.wait("tensor", UVS[sl], UVS[sl].n)
                        b.wait("vector", UVS[sl], UVS[sl].n)
                    cs = slice(ch * CCH, (ch + 1) * CCH)
                    if not pass2:
                        VR.tensor_tensor_scan(cT[:, :], ones16[:, 0:CCH],
                                              g_res[:, cs], 0.0, ALU.mult, ALU.add)
                        v = b.inc("vector", VE)
                        b.wait("scalar", VE, v)
                        SC.activation(lnbT[:, :], beta_res[:, cs], AF.Ln)
                        sln = b.inc("scalar", SE)
                        b.wait("vector", SE, sln)
                        VR.tensor_add(cplusT[:, :], cT[:, :], lnbT[:, :])
                        for j in range(4):
                            VR.tensor_copy(cT4[:, j * 128:(j + 1) * 128], cT[:, :])
                        v = b.inc("vector", VE)
                        b.wait("tensor", VE, v)
                        TN.transpose(pCR[0:128, 0:16], cT[:, :], identF[0:16, 0:16])
                        TN.transpose(pCR[0:128, 16:32], cplusT[:, :], identF[0:16, 0:16])
                        t = b.inc("tensor", TE)
                        b.wait("vector", TE, t)
                        VR.tensor_copy(cols[:, 0:32], pCR[:, 0:32])
                        v = b.inc("vector", VE)
                        b.wait("tensor", VE, v)
                        TN.transpose(pCR[0:128, 32:48], beta_res[:, cs], identF[0:16, 0:16])
                        TN.transpose(pCR[0:1, 48:64], cT[:, 127:128], identF[0:16, 0:16])
                        t = b.inc("tensor", TE)
                        b.wait("vector", TE, t)
                        VR.tensor_copy(cols[:, 32:48], pCR[:, 32:48])
                        VR.tensor_copy(clast[:, :], pCR[0:1, 48:64])
                        VR.tensor_scalar(negcC[:, :], cols[:, 0:16], -1.0, None, ALU.mult)
                        v = b.inc("vector", VE)
                        b.wait("tensor", VE, v)
                        TN.matmul(pCR[:, 0:16], onesRowF[:], clast[:, :], start=True, stop=True)
                        t = b.inc("tensor", TE)
                        b.wait("vector", TE, t)
                        VR.tensor_sub(cols[:, 48:64], pCR[:, 0:16], cols[:, 0:16])
                        v = b.inc("vector", VE)
                        b.wait("scalar", VE, v)
                        SC.activation(ratC[:, :], cols[:, 48:64], AF.Exp)
                        SC.activation(gCcur[:, :], pCR[:, 0:16], AF.Exp)
                        SC.activation(bgC[:, :], cols[:, 16:32], AF.Exp)
                        sexp = b.inc("scalar", SE)
                        b.wait("vector", SE, sexp)
                        b.wait("tensor", SE, sexp)
                        VR.tensor_copy(gcc[:, sl], gCcur[:, :])
                        v = b.inc("vector", VE)
                        b.wait("scalar", VE, v)
                        b.dma("scalar", scr["gcc_s"].ap()[ch], gcc[:, sl], UVS[sl])
                    else:
                        pass

                    if DBG_OUT and pass2 and ch == 0 and "scomb_dbg" in DBG_OUT:
                        b.wait("scalar", VE, VE.n)
                        b.wait("scalar", UVS[sl], UVS[sl].n)
                        pd = scr["pcdbg"].ap().rearrange("p a c -> p (a c)")
                        b.dma("scalar", pd[:, 0:2048],
                              S0b[:, :, :].rearrange("p a c -> p (a c)"), ST)
                        b.dma("scalar", pd[:, 2048:4096],
                              Uv[:, sl].rearrange("p a c -> p (a c)"), ST)
                        b.dma("scalar", pd[:, 4096:6144],
                              WiT[:, sl].rearrange("p a c -> p (a c)"), ST)
                        b.dma("scalar", pd[:, 6144:8192],
                              W2T[:, sl].rearrange("p a c -> p (a c)"), ST)
                        for e_ in ENGINES:
                            b.wait(e_, ST)
                    for g in range(4):
                        hs = slice(4 * g, 4 * g + 4)
                        if not pass2:
                            VR.tensor_mul(msk4[:, :], cT4[:, :], selm4[:, g, :])
                            VR.tensor_scalar(nmsk4[:, :], msk4[:, :], -1.0, None, ALU.mult)
                            v = b.inc("vector", VE)
                            b.wait("tensor", VE, v)
                            TN.matmul(pCR[:, 0:512], ones16[:, 0:128], msk4[:, :],
                                      start=True, stop=False)
                            t = b.inc("tensor", TE)
                            b.wait("scalar", TE, t)
                            SC.activation(Grow4[:, :], pCR[:, 0:512], AF.Exp)
                            sg_ = b.inc("scalar", SE)
                            b.wait("tensor", SE, sg_)
                            TN.matmul(pCR[:, 0:512], mUIT[:, :], Ib4[:, :],
                                      start=False, stop=True)
                            t = b.inc("tensor", TE)
                            b.wait("scalar", TE, t)
                            for j in range(4):
                                SC.activation(DpT4[:, j * 128:(j + 1) * 128],
                                              pCR[:, j * 128:(j + 1) * 128], AF.Exp,
                                              bias=negcC[:, 4 * g + j:4 * g + j + 1])
                            s_ = b.inc("scalar", SE)
                            b.wait("tensor", SE, s_)
                            TN.matmul(pN1[:, 0:512], ones16[:, 0:128], nmsk4[:, :],
                                      start=True, stop=False)
                            TN.matmul(pN1[:, 0:512], mLST[:, :], Ib4[:, :],
                                      start=False, stop=True)
                            t = b.inc("tensor", TE)
                            b.wait("scalar", TE, t)
                            for j in range(4):
                                SC.activation(DsL4[:, j * 128:(j + 1) * 128],
                                              pN1[:, j * 128:(j + 1) * 128], AF.Exp,
                                              bias=cols[:, 16 + 4 * g + j:17 + 4 * g + j])
                            s_ = b.inc("scalar", SE)
                            b.wait("tensor", SE, s_)
                            for j in range(4):
                                TN.matmul(pMM[:, j * 128:(j + 1) * 128],
                                          kf[:, sl, 4 * g + j, :], kf[:, sl, 4 * g + j, :],
                                          start=(j == 0), stop=(j == 3))
                            t = b.inc("tensor", TE)
                            b.wait("vector", TE, t)
                            VR.tensor_mul(MPOS4[:, :], DsL4[:, :], pMM[:, 0:512])
                            v = b.inc("vector", VE)
                            b.wait("tensor", VE, v)
                            for j in range(4):
                                TN.transpose(pOT[:, j * 128:(j + 1) * 128],
                                             MPOS4[:, j * 128:(j + 1) * 128], identB[:, :])
                            t = b.inc("tensor", TE)
                            b.wait("scalar", TE, t)
                            SC.copy(MT4[:, :], pOT[:, 0:512])
                            s_ = b.inc("scalar", SE)
                            b.wait("vector", SE, s_)
                            VR.tensor_add(A4[:, :], MPOS4[:, :], Ib4[:, :])
                            VR.scalar_tensor_tensor(W4[:, :], MT4[:, :], -1.0, Ib4[:, :],
                                                    ALU.mult, ALU.add)
                            VR.scalar_tensor_tensor(V4[:, :], MPOS4[:, :], -1.0, Ib4[:, :],
                                                    ALU.mult, ALU.add)
                            v = b.inc("vector", VE)
                            b.wait("tensor", VE, v)
                            for j in range(4):
                                TN.matmul(pMM[:, j * 128:(j + 1) * 128],
                                          kf[:, sl, 4 * g + j, :], qf[:, sl, 4 * g + j, :],
                                          start=(j == 0), stop=(j == 3))
                            t = b.inc("tensor", TE)
                            b.wait("vector", TE, t)
                            VR.tensor_mul(W2T[:, sl, hs, :].rearrange("p a c -> p (a c)"),
                                          DpT4[:, :], pMM[:, 0:512])
                            VR.tensor_mul(Qh[:, sl, hs, :].rearrange("p a c -> p (a c)"),
                                          qf[:, sl, hs, :].rearrange("p a c -> p (a c)"),
                                          Grow4[:, :])
                            for j in range(4):
                                VR.tensor_scalar(Zv4[:, j * 128:(j + 1) * 128],
                                                 vt[:, sl, 4 * g + j, :],
                                                 cols[:, 32 + 4 * g + j:33 + 4 * g + j],
                                                 None, ALU.mult)
                                VR.tensor_scalar(Y4[:, j * 128:(j + 1) * 128],
                                                 kt[:, sl, 4 * g + j, :],
                                                 bgC[:, 4 * g + j:4 * g + j + 1],
                                                 None, ALU.mult)
                                VR.tensor_scalar(Kb[:, sl, 4 * g + j, :],
                                                 kt[:, sl, 4 * g + j, :],
                                                 ratC[:, 4 * g + j:4 * g + j + 1],
                                                 None, ALU.mult)
                            v = b.inc("vector", VE)
                            b.wait("tensor", VE, v)
                            for it_ in range(6):
                                for j in range(4):
                                    TN.matmul(pN1[:, j * 128:(j + 1) * 128],
                                              A4[:, j * 128:(j + 1) * 128],
                                              W4[:, j * 128:(j + 1) * 128],
                                              start=(j == 0), stop=(j == 3))
                                t = b.inc("tensor", TE)
                                b.wait("vector", TE, t)
                                VR.scalar_tensor_tensor(G4[:, :], pN1[:, 0:512], -1.0,
                                                        twoIb4[:, :], ALU.mult, ALU.add)
                                v = b.inc("vector", VE)
                                b.wait("tensor", VE, v)
                                for j in range(4):
                                    TN.matmul(pN2[:, j * 128:(j + 1) * 128],
                                              V4[:, j * 128:(j + 1) * 128],
                                              G4[:, j * 128:(j + 1) * 128],
                                              start=(j == 0), stop=(j == 3))
                                for j in range(4):
                                    TN.matmul(pN1[:, j * 128:(j + 1) * 128],
                                              G4[:, j * 128:(j + 1) * 128],
                                              V4[:, j * 128:(j + 1) * 128],
                                              start=(j == 0), stop=(j == 3))
                                t = b.inc("tensor", TE)
                                b.wait("vector", TE, t)
                                b.wait("scalar", TE, t)
                                SC.copy(W4[:, :], pN2[:, 0:512])
                                s_ = b.inc("scalar", SE)
                                if it_ < 5:
                                    VR.tensor_copy(V4[:, :], pN1[:, 0:512])
                                    v = b.inc("vector", VE)
                                    b.wait("tensor", VE, v)
                                b.wait("tensor", SE, s_)
                                b.wait("vector", SE, s_)
                            for j in range(4):
                                TN.matmul(pN2[:, j * 128:(j + 1) * 128],
                                          W4[:, j * 128:(j + 1) * 128],
                                          Zv4[:, j * 128:(j + 1) * 128],
                                          start=(j == 0), stop=(j == 3))
                            for j in range(4):
                                TN.matmul(pN1[:, j * 128:(j + 1) * 128],
                                          Y4[:, j * 128:(j + 1) * 128],
                                          W4[:, j * 128:(j + 1) * 128],
                                          start=(j == 0), stop=(j == 3))
                            t = b.inc("tensor", TE)
                            b.wait("vector", TE, t)
                            b.wait("scalar", TE, t)
                            VR.tensor_copy(Uv[:, sl, hs, :].rearrange("p a c -> p (a c)"),
                                           pN2[:, 0:512])
                            SC.copy(WiT[:, sl, hs, :].rearrange("p a c -> p (a c)"),
                                    pN1[:, 0:512])
                            v = b.inc("vector", VE)
                            s_ = b.inc("scalar", SE)
                            b.wait("tensor", VE, v)
                            b.wait("tensor", SE, s_)
                        for j in range(4):
                            TN.matmul(pU[:, j * 128:(j + 1) * 128],
                                      WiT[:, sl, 4 * g + j, :], S0b[:, 4 * g + j, :],
                                      start=(j == 0), stop=(j == 3))
                        t = b.inc("tensor", TE)
                        b.wait("vector", TE, t)
                        VR.scalar_tensor_tensor(Ua[:, hs, :].rearrange("p a c -> p (a c)"),
                                                pU[:, 0:512], -1.0,
                                                Uv[:, sl, hs, :].rearrange("p a c -> p (a c)"),
                                                ALU.mult, ALU.add)
                        v = b.inc("vector", VE)
                        b.wait("tensor", VE, v)
                        for j in range(4):
                            TN.matmul(pO[:, j * 128:(j + 1) * 128],
                                      Qh[:, sl, 4 * g + j, :], S0b[:, 4 * g + j, :],
                                      start=(j == 0), stop=False)
                            TN.matmul(pO[:, j * 128:(j + 1) * 128],
                                      W2T[:, sl, 4 * g + j, :], Ua[:, 4 * g + j, :],
                                      start=False, stop=(j == 3))
                        t = b.inc("tensor", TE)
                        b.wait("vector", TE, t)
                        VR.tensor_copy(ot4[:, :], pO[:, 0:512])
                        v = b.inc("vector", VE)
                        b.wait("tensor", VE, v)
                        for j in range(4):
                            TN.transpose(pOT[:, j * 128:(j + 1) * 128],
                                         ot4[:, j * 128:(j + 1) * 128], identB[:, :])
                        for j in range(4):
                            TN.matmul(pS[:, j * 128:(j + 1) * 128],
                                      Kb[:, sl, 4 * g + j, :], Ua[:, 4 * g + j, :],
                                      start=(j == 0), stop=(j == 3))
                        t = b.inc("tensor", TE)
                        b.wait("scalar", TE, t)
                        b.wait("vector", TE, t)
                        if ve_of[sl] and g == 0:
                            b.wait("scalar", OFS[sl], ve_of[sl])
                        SC.copy(ofmb[:, sl, hs, :].rearrange("p a c -> p (a c)"),
                                pOT[:, 0:512])
                        s_ = b.inc("scalar", SE)
                        for j in range(4):
                            VR.tensor_scalar(S[:, 4 * g + j, :], S[:, 4 * g + j, :],
                                             gCcur[:, 4 * g + j:4 * g + j + 1] if not pass2
                                             else gcc[:, sl, 4 * g + j:4 * g + j + 1],
                                             None, ALU.mult)
                        VR.tensor_add(S[:, hs, :].rearrange("p a c -> p (a c)"),
                                      S[:, hs, :].rearrange("p a c -> p (a c)"),
                                      pS[:, 0:512])
                        v = b.inc("vector", VE)
                        b.wait("tensor", VE, v)
                        b.wait("scalar", VE, v)
                    if DBG_OUT and pass2 and ch == 0 and g == 3:
                        b.wait("scalar", VE, VE.n)
                        b.dma("scalar", scr["pcdbg"].ap().rearrange("p a c -> p (a c)")[:, 0:2048]
                              .rearrange("p (a c) -> p a c", a=16),
                              Ua[:, :, :], ST)
                        b.dma("scalar", scr["pcdbg"].ap().rearrange("p a c -> p (a c)")[:, 2048:4096]
                              .rearrange("p (a c) -> p a c", a=16),
                              S0b[:, :, :], ST)
                        b.wait("scalar", ST)
                    VR.tensor_copy(S0b[:, :, :], S[:, :, :])
                    v_s0 = b.inc("vector", VE)
                    b.wait("tensor", VE, v_s0)
                    b.wait("scalar", SE, SE.n)
                    ve_of[sl] = b.dma("scalar", ofmv[:, :, cs], ofmb[:, sl], OFS[sl])
                    if not pass2:
                        b.wait("scalar", VE, VE.n)
                        b.wait("scalar", SE, SE.n)
                        for nm, tns in (("uv_s", Uv), ("wiT_s", WiT), ("w2T_s", W2T),
                                        ("qh_s", Qh), ("kb_s", Kb)):
                            b.dma("scalar", scr[nm].ap()[ch],
                                  tns[:, sl].rearrange("p a c -> p (a c)"), UVS[sl])
                        uv_store[sl] = UVS[sl].n
                        wr_uv[sl] = VE.n
                        te_qk[sl] = TE.n
                        se_uv[sl] = SE.n
                    else:
                        wr_uv[sl] = VE.n
                        te_qk[sl] = TE.n
                        se_uv[sl] = SE.n

            uv_store = [0, 0]
            VR.memset(S[:, :, :], 0.0)
            v0 = b.inc("vector", VE)
            b.wait("vector", VE, v0)
            VR.tensor_copy(S0b[:, :, :], S[:, :, :])
            b.inc("vector", VE)
            b.wait("tensor", VE, VE.n)
            b.wait("scalar", VE, VE.n)
            sweep(False)
            b.wait("scalar", VE, VE.n)
            b.dma("scalar", scr["cc_in"].ap(), S[:, :, :].rearrange("p a c -> p (a c)"), ST)
            b.wait("gpsimd", ST)
            GP.collective_compute(
                "AllGather", ALU.bypass,
                replica_groups=[[0, 1], [2, 3], [4, 5], [6, 7]],
                ins=[scr["cc_in"].ap()], outs=[scr["cc_out"].ap()],
            )
            b.inc("gpsimd", b.sems["CC"])
            b.wait("gpsimd", b.sems["CC"], b.sems["CC"].n)
            b.dma("gpsimd", s0buf[:, :, :],
                  scr["cc_out"].ap()[0:128].rearrange("p (a c) -> p a c", a=H), CA)
            b.dma("gpsimd", s1buf[:, :, :],
                  scr["cc_out"].ap()[128:256].rearrange("p (a c) -> p a c", a=H), CA)
            b.wait("vector", CA)
            b.wait("vector", OFS[0]); b.wait("vector", OFS[1])
            b.wait("vector", UVS[0]); b.wait("vector", UVS[1])
            VR.tensor_scalar(S[:, :, :], s0buf[:, :, :], mA[:, 0:1], None, ALU.mult)
            VR.tensor_scalar(s1buf[:, :, :], s1buf[:, :, :], mB[:, 0:1], None, ALU.mult)
            VR.tensor_add(S[:, :, :], S[:, :, :], s1buf[:, :, :])
            VR.tensor_copy(S0b[:, :, :], S[:, :, :])
            v = b.inc("vector", VE)
            b.wait("tensor", VE, v)
            b.wait("scalar", VE, v)
            if DBG_OUT:
                b.dma("scalar", scr["scomb_dbg"].ap(),
                      S[:, :, :].rearrange("p a c -> p (a c)"), ST)
                b.wait("scalar", ST)
            if not P3_SWEEP1_ONLY:
                sweep(True)
            b.barrier()

    def owo_phase(hga_res):
        with ExitStack() as ps:
            PE = ps.enter_context
            op_ = PE(nc.sbuf_tensor("op4", [128, NK, 512], BF16))
            x1p = PE(nc.sbuf_tensor("x1p4", [128, NK, 512], F32))
            ap_ = PE(nc.sbuf_tensor("ap4", [128, NK, 512], F32))
            sq4 = PE(nc.sbuf_tensor("sq4", [128, 2, NK, 128], F32))
            ogb = PE(nc.sbuf_tensor("ogb4", [128, NK, 512], BF16))
            sg4 = PE(nc.sbuf_tensor("sg4", [128, 512], BF16))
            sqo = PE(nc.sbuf_tensor("sqo4", [128, 512], F32))
            wgb_s = PE(nc.sbuf_tensor("wgb_s4", [128, D], BF16))
            onw4 = PE(nc.sbuf_tensor("onw44", [4, 4, 128], F32))
            wob = PE(nc.sbuf_tensor("wob4", [128, 2, 512], BF16))
            stat = PE(nc.sbuf_tensor("stat4", [128, 32], F32))
            colw = PE(nc.sbuf_tensor("colw4", [128, 16], F32))
            rowt = PE(nc.sbuf_tensor("rowt4", [16, 128], F32))
            lrow = PE(nc.sbuf_tensor("lrow4", [4, 128], F32))
            pA = PE(nc.psum_tensor("pA4", [128, 512], F32))
            pB = PE(nc.psum_tensor("pB4", [128, 512], F32))
            pC = PE(nc.psum_tensor("pC4", [128, 512], F32))
            pD = PE(nc.psum_tensor("pD4", [128, 512], F32))
            pE = PE(nc.psum_tensor("pE4", [128, 2048], F32))

            WOS = b.slot_sems("WOS4", 2)
            te_wo = [0, 0]
            te_sq4 = [0, 0]
            b.dma("gpsimd", wgb_s[:, :], ap["wgb"], CA)
            b.dma("sync", onw4[:, :, :], ap["onw4"], LD)
            b.wait("tensor", CA); b.wait("tensor", LD)

            st_prev = 0
            for ptag in range(4):
                T = 512
                ntile = nt = 4
                b.wait("vector", ST, st_prev)
                b.wait("scalar", ST, st_prev)
                b.wait("sync", TE, TE.n); b.wait("sync", VE, VE.n)
                cs = slice(ptag * 512, (ptag + 1) * 512)
                b.dma("sync", op_[:, :, :], ofmv[:, :, cs], LD)
                b.dma("sync", x1p[:, :, :], x1v[:, :, THALO + ptag * 512:THALO + (ptag + 1) * 512], LD)
                b.wait("scalar", LD); b.wait("tensor", LD); b.wait("vector", LD)
                b.wait("tensor", SE, SE.n)
                b.wait("tensor", VE, VE.n)
                for r in range(ntile):
                    if te_sq4[r % 2]:
                        b.wait("scalar", TE, te_sq4[r % 2])
                    SC.activation(sq4[:, r % 2, :, :], x1p[:, :, r * 128:(r + 1) * 128], AF.Square)
                    s = b.inc("scalar", SE)
                    b.wait("tensor", SE, s)
                    for k in range(NK):
                        TN.matmul(pC[:, r:r + 1], x1p[:, k, r * 128:(r + 1) * 128],
                                  onesColF[:], start=(r == 0 and k == 0), stop=False)
                        TN.matmul(pC[:, 8 + r:9 + r], sq4[:, r % 2, k, :], onesColF[:],
                                  start=False, stop=(r == ntile - 1 and k == NK - 1))
                    te_sq4[r % 2] = b.inc("tensor", TE)
                b.wait("vector", TE, TE.n)
                VR.memset(colw[:, :], 0.0)
                VR.tensor_copy(stat[:, 0:nt], pC[:, 0:nt])
                VR.tensor_copy(stat[:, 8:8 + nt], pC[:, 8:8 + nt])
                VR.tensor_scalar(colw[:, 0:nt], stat[:, 0:nt], 1.0 / D, None, ALU.mult)
                VR.tensor_mul(colw[:, 8:8 + nt], colw[:, 0:nt], colw[:, 0:nt])
                VR.scalar_tensor_tensor(colw[:, 8:8 + nt], stat[:, 8:8 + nt], 1.0 / D,
                                        colw[:, 8:8 + nt], ALU.mult, ALU.subtract)
                v = b.inc("vector", VE)
                b.wait("scalar", VE, v)
                SC.activation(colw[:, 8:8 + nt], colw[:, 8:8 + nt], AF.Sqrt, bias=np.float32(1e-5))
                s = b.inc("scalar", SE)
                b.wait("vector", SE, s)
                VR.reciprocal(colw[:, 8:8 + nt], colw[:, 8:8 + nt])
                VR.tensor_mul(colw[:, 0:nt], colw[:, 0:nt], colw[:, 8:8 + nt])
                VR.tensor_scalar(colw[:, 0:nt], colw[:, 0:nt], -1.0, None, ALU.mult)
                v = b.inc("vector", VE)
                b.wait("tensor", VE, v)
                TN.transpose(pC[0:16, 0:128], colw[:, 0:16], identF[:, :])
                t = b.inc("tensor", TE)
                b.wait("vector", TE, t)
                VR.tensor_copy(rowt[:, :], pC[0:16, 0:128])
                v = b.inc("vector", VE)
                b.wait("tensor", VE, v)
                for r in range(ntile):
                    TN.matmul(pA[:, r * 128:(r + 1) * 128],
                              sel16[:, (8 + r) * 128:(9 + r) * 128], rowt[:, :],
                              start=(r == 0), stop=(r == ntile - 1))
                    TN.matmul(pB[:, r * 128:(r + 1) * 128],
                              sel16[:, r * 128:(r + 1) * 128], rowt[:, :],
                              start=(r == 0), stop=(r == ntile - 1))
                t = b.inc("tensor", TE)
                b.wait("vector", TE, t)
                for k in range(NK):
                    VR.tensor_mul(ap_[:, k, :], x1p[:, k, :], pA[:, 0:512])
                    VR.tensor_add(ap_[:, k, :], ap_[:, k, :], pB[:, 0:512])
                v_a = b.inc("vector", VE)
                b.wait("tensor", VE, v_a)
                for m in range(NK):
                    TN.matmul(pA[:, 0:512], wgb_s[:, m * 128:(m + 1) * 128],
                              hga_res[:, cs], start=True, stop=True)
                    t = b.inc("tensor", TE)
                    b.wait("scalar", TE, t)
                    SC.activation(sg4[:, :], pA[:, 0:512], AF.Silu)
                    SC.activation(sqo[:, :], op_[:, m, :], AF.Square)
                    s = b.inc("scalar", SE)
                    b.wait("tensor", SE, s)
                    for r in range(4):
                        TN.matmul(pB[:, 16 + r:17 + r], sqo[:, r * 128:(r + 1) * 128],
                                  onesColF[:], start=(r == 0), stop=(r == 3))
                    t = b.inc("tensor", TE)
                    b.wait("vector", TE, t)
                    VR.tensor_copy(colw[:, 0:4], pB[:, 16:20])
                    v = b.inc("vector", VE)
                    b.wait("scalar", VE, v)
                    SC.activation(colw[:, 0:4], colw[:, 0:4], AF.Sqrt,
                                  bias=np.float32(1e-6), scale=np.float32(1.0 / 128.0))
                    s = b.inc("scalar", SE)
                    b.wait("vector", SE, s)
                    VR.reciprocal(colw[:, 0:4], colw[:, 0:4])
                    v = b.inc("vector", VE)
                    b.wait("tensor", VE, v)
                    TN.transpose(pB[0:4, 32:160], colw[:, 0:4], identF[:, :])
                    t = b.inc("tensor", TE)
                    b.wait("vector", TE, t)
                    VR.tensor_copy(lrow[:, :], pB[0:4, 32:160])
                    v = b.inc("vector", VE)
                    b.wait("tensor", VE, v)
                    for r in range(4):
                        TN.matmul(pC[:, r * 128:(r + 1) * 128], onw4[:, r, :], lrow[:, :],
                                  start=(r == 0), stop=(r == 3))
                    t = b.inc("tensor", TE)
                    b.wait("vector", TE, t)
                    VR.tensor_mul(ogb[:, m, :], op_[:, m, :], pC[:, 0:512])
                    VR.tensor_mul(ogb[:, m, :], ogb[:, m, :], sg4[:, :])
                    v = b.inc("vector", VE)
                    b.wait("tensor", VE, v)
                    b.wait("scalar", VE, v)
                for dq in range(4):
                    for k in range(NK):
                        ksl = k % 2
                        if te_wo[ksl]:
                            b.wait("gpsimd", TE, te_wo[ksl])
                        b.dma("gpsimd", wob[:, ksl, :],
                              ap["wo"][k * 128:(k + 1) * 128, dq * 512:(dq + 1) * 512],
                              WOS[ksl])
                        b.wait("tensor", WOS[ksl], WOS[ksl].n)
                        for dd in range(4):
                            TN.matmul(pE[:, dd * 512:dd * 512 + 512],
                                      wob[:, ksl, dd * 128:(dd + 1) * 128],
                                      ogb[:, k, :], start=(k == 0), stop=(k == NK - 1))
                        te_wo[ksl] = b.inc("tensor", TE)
                    b.wait("vector", TE, TE.n)
                    for dd in range(4):
                        d_ = dq * 4 + dd
                        VR.scalar_tensor_tensor(ap_[:, d_, :], pE[:, dd * 512:dd * 512 + 512],
                                                1.0, ap_[:, d_, :], ALU.mult, ALU.add)
                    ve_pe = b.inc("vector", VE)
                    b.wait("tensor", VE, ve_pe)
                b.wait("scalar", VE, VE.n)
                st_prev = b.dma("scalar", xiv[:, :, cs], ap_[:, :, :], ST)
            b.barrier()

    def mlp_up_phase():
        with ExitStack() as ps:
            PE = ps.enter_context
            h2f = PE(nc.sbuf_tensor("h2f5", [128, NK, TOWN], BF16))
            xip = PE(nc.sbuf_tensor("xip5", [128, NK, 512], F32))
            sq5 = PE(nc.sbuf_tensor("sq5", [128, 2, NK, 128], F32))
            wgb = PE(nc.sbuf_tensor("wgb5", [128, 2, NK, 128], BF16))
            wub = PE(nc.sbuf_tensor("wub5", [128, 2, NK, 128], BF16))
            sgl = PE(nc.sbuf_tensor("sgl5", [128, 2048], BF16))
            mb = PE(nc.sbuf_tensor("mb5", [128, 2, 2048], BF16))
            stat = PE(nc.sbuf_tensor("stat5", [128, 32], F32))
            colw = PE(nc.sbuf_tensor("colw5", [128, 16], F32))
            rowt = PE(nc.sbuf_tensor("rowt5", [16, 128], F32))
            ln2c = PE(nc.sbuf_tensor("ln2c5", [128, 16], F32))
            pG = PE(nc.psum_tensor("pG5", [128, 2048], F32))
            pU = PE(nc.psum_tensor("pU5", [128, 2048], F32))
            WG5 = b.slot_sems("WG5", 2)
            MS5 = b.slot_sems("MS5", 2)
            te_w5 = [0, 0]
            ve_m5 = [0, 0]
            b.dma("sync", ln2c[:, :], ap["ln2_wT"], LD)
            b.wait("vector", LD)
            for ptag in range(4):
                nt = 4
                b.wait("sync", TE, TE.n); b.wait("sync", VE, VE.n)
                b.dma("sync", xip[:, :, :], xiv[:, :, ptag * 512:(ptag + 1) * 512], LD)
                b.wait("scalar", LD); b.wait("tensor", LD); b.wait("vector", LD)
                b.wait("tensor", SE, SE.n); b.wait("tensor", VE, VE.n)
                te_sq5 = {}
                for r in range(4):
                    if r >= 2:
                        b.wait("scalar", TE, te_sq5[r - 2])
                    SC.activation(sq5[:, r % 2, :, :], xip[:, :, r * 128:(r + 1) * 128], AF.Square)
                    s = b.inc("scalar", SE)
                    b.wait("tensor", SE, s)
                    for k in range(NK):
                        TN.matmul(pG[:, 8 + r:9 + r], sq5[:, r % 2, k, :], onesColF[:],
                                  start=(r == 0 and k == 0),
                                  stop=(r == 3 and k == NK - 1))
                    te_sq5[r] = b.inc("tensor", TE)
                b.wait("vector", TE, TE.n)
                VR.memset(colw[:, :], 0.0)
                VR.tensor_copy(stat[:, 8:8 + nt], pG[:, 8:8 + nt])
                v = b.inc("vector", VE)
                b.wait("scalar", VE, v)
                SC.activation(colw[:, 8:8 + nt], stat[:, 8:8 + nt], AF.Sqrt,
                              bias=np.float32(1e-6), scale=np.float32(1.0 / D))
                s = b.inc("scalar", SE)
                b.wait("vector", SE, s)
                VR.reciprocal(colw[:, 8:8 + nt], colw[:, 8:8 + nt])
                v = b.inc("vector", VE)
                b.wait("tensor", VE, v)
                TN.transpose(pG[0:16, 0:128], colw[:, 0:16], identF[:, :])
                t = b.inc("tensor", TE)
                b.wait("vector", TE, t)
                VR.tensor_copy(rowt[:, :], pG[0:16, 0:128])
                v = b.inc("vector", VE)
                b.wait("tensor", VE, v)
                for r in range(4):
                    TN.matmul(pU[:, r * 128:(r + 1) * 128],
                              sel16[:, (8 + r) * 128:(9 + r) * 128], rowt[:, :],
                              start=(r == 0), stop=(r == 3))
                t = b.inc("tensor", TE)
                b.wait("vector", TE, t)
                for k in range(NK):
                    VR.scalar_tensor_tensor(h2f[:, k, ptag * 512:(ptag + 1) * 512],
                                            xip[:, k, :], ln2c[:, k:k + 1],
                                            pU[:, 0:512], ALU.mult, ALU.mult)
                v = b.inc("vector", VE)
                b.wait("tensor", VE, v)
            wgv = ap["w_gate"].rearrange("(k r) i -> r k i", r=128)
            wuv = ap["w_up"].rearrange("(k r) i -> r k i", r=128)
            for i in range(64):
                isl = i % 2
                if te_w5[isl]:
                    b.wait("gpsimd", TE, te_w5[isl])
                b.dma("gpsimd", wgb[:, isl], wgv[:, :, i * 128:(i + 1) * 128], WG5[isl])
                b.dma("gpsimd", wub[:, isl], wuv[:, :, i * 128:(i + 1) * 128], WG5[isl])
                b.wait("tensor", WG5[isl], WG5[isl].n)
                for tg in range(4):
                    for k in range(NK):
                        TN.matmul(pG[:, tg * 512:(tg + 1) * 512], wgb[:, isl, k, :],
                                  h2f[:, k, tg * 512:(tg + 1) * 512],
                                  start=(k == 0), stop=(k == NK - 1))
                for tg in range(4):
                    for k in range(NK):
                        TN.matmul(pU[:, tg * 512:(tg + 1) * 512], wub[:, isl, k, :],
                                  h2f[:, k, tg * 512:(tg + 1) * 512],
                                  start=(k == 0), stop=(k == NK - 1))
                te_w5[isl] = b.inc("tensor", TE)
                b.wait("scalar", TE, te_w5[isl])
                SC.activation(sgl[:, :], pG[:, 0:2048], AF.Silu)
                s = b.inc("scalar", SE)
                b.wait("vector", SE, s)
                if ve_m5[isl]:
                    b.wait("vector", MS5[isl], ve_m5[isl])
                VR.tensor_mul(mb[:, isl, :], sgl[:, :], pU[:, 0:2048])
                v = b.inc("vector", VE)
                b.wait("tensor", VE, v)
                b.wait("scalar", VE, v)
                ve_m5[isl] = b.dma("scalar", scr["mscr"].ap()[i], mb[:, isl, :], MS5[isl])
            b.barrier()

    def mlp_down_phase():
        with ExitStack() as ps:
            PE = ps.enter_context
            mb6 = PE(nc.sbuf_tensor("mb65", [128, 64, 1024], BF16))
            wdb = PE(nc.sbuf_tensor("wdb6", [128, 2, 32, 128], BF16))
            x1t = PE(nc.sbuf_tensor("x1t6", [128, 2, 1024], F32))
            x2b = PE(nc.sbuf_tensor("x2b6", [128, 2, 1024], F32))
            pD1 = PE(nc.psum_tensor("pD16", [128, 1024], F32))
            pD2 = PE(nc.psum_tensor("pD26", [128, 1024], F32))
            MT6 = b.slot_sems("MT6", 1)
            WD6 = b.slot_sems("WD6", 2)
            XT6 = b.slot_sems("XT6", 2)
            X26 = b.slot_sems("X26", 2)
            te_wd = [0, 0]
            ve_add = [0, 0]
            st_x2 = [0, 0]
            wdv = ap["w_down"].rearrange("(i r) d -> r i d", r=128)
            mscv = scr["mscr"].ap()
            for tg in range(2):
                b.wait("sync", TE, TE.n)
                b.dma("sync", mb6[:, :, :],
                      mscv[:, :, tg * 1024:(tg + 1) * 1024].rearrange("i p t -> p i t"),
                      MT6[0])
                b.wait("tensor", MT6[0], MT6[0].n)
                for d_ in range(16):
                    dsl = d_ % 2
                    pDx = pD1 if dsl == 0 else pD2
                    if ve_add[dsl]:
                        b.wait("gpsimd", VE, ve_add[dsl])
                    b.dma("gpsimd", x1t[:, dsl, :],
                          x1v[:, d_, THALO + tg * 1024:THALO + (tg + 1) * 1024], XT6[dsl])
                    b.wait("gpsimd", XT6[dsl], XT6[dsl].n)
                    GP.dma_start(out=x1t[:, dsl, :],
                                 in_=xiv[:, d_, tg * 1024:(tg + 1) * 1024],
                                 accum_op=ALU.add).then_inc(XT6[dsl].h, 16)
                    XT6[dsl].n += 16
                    for ih in range(2):
                        wsl = (2 * d_ + ih) % 2
                        if te_wd[wsl]:
                            b.wait("gpsimd", TE, te_wd[wsl])
                        b.dma("gpsimd", wdb[:, wsl],
                              wdv[:, ih * 32:(ih + 1) * 32, d_ * 128:(d_ + 1) * 128],
                              WD6[wsl])
                        b.wait("tensor", WD6[wsl], WD6[wsl].n)
                        if ih == 0 and ve_add[dsl]:
                            b.wait("tensor", VE, ve_add[dsl])
                        for i in range(32):
                            for gg_ in range(2):
                                TN.matmul(pDx[:, gg_ * 512:(gg_ + 1) * 512],
                                          wdb[:, wsl, i, :],
                                          mb6[:, ih * 32 + i, gg_ * 512:(gg_ + 1) * 512],
                                          start=(ih == 0 and i == 0),
                                          stop=(ih == 1 and i == 31))
                        te_wd[wsl] = b.inc("tensor", TE)
                    b.wait("vector", TE, te_wd[(2 * d_ + 1) % 2])
                    b.wait("vector", XT6[dsl], XT6[dsl].n)
                    if st_x2[dsl]:
                        b.wait("vector", X26[dsl], st_x2[dsl])
                    VR.tensor_add(x2b[:, dsl, :], x1t[:, dsl, :], pDx[:, 0:1024])
                    ve_add[dsl] = b.inc("vector", VE)
                    b.wait("scalar", VE, ve_add[dsl])
                    st_x2[dsl] = b.dma("scalar", x2v[:, d_, tg * 1024:(tg + 1) * 1024],
                                       x2b[:, dsl, :], X26[dsl])
            b.barrier()

    if "p1" in PHASES:
        ffn_phase("p1")
    if "p2" in PHASES or "p3" in PHASES or "p4" in PHASES:
        with ExitStack() as p24:
            g_res = p24.enter_context(nc.sbuf_tensor("g_res", [16, TOWN], F32))
            beta_res = p24.enter_context(nc.sbuf_tensor("beta_res", [16, TOWN], F32))
            hga_res = p24.enter_context(nc.sbuf_tensor("hga_res", [128, TOWN], BF16))
            if "p2" in PHASES:
                proj_phase(g_res, beta_res, hga_res)
            if "p3" in PHASES:
                scan_phase(g_res, beta_res)
            if "p4" in PHASES:
                owo_phase(hga_res)
                if DBG_OUT:
                    b.dma("scalar", scr["g_dbg"].ap(), g_res[:, :], ST)
                    b.dma("scalar", scr["beta_dbg"].ap(), beta_res[:, :], ST)
                    b.dma("scalar", scr["hga_dbg"].ap(), hga_res[:, :], ST)
                    b.wait("scalar", ST)
    if "p5" in PHASES:
        mlp_up_phase()
    if "p6" in PHASES:
        mlp_down_phase()
    if "p7" in PHASES:
        ffn_phase("p7")

    b.wait("gpsimd", ST)
    for name, out_ap in dbg_outs.items():
        b.dma("gpsimd", out_ap, scr[name].ap(), CA)
    b.wait("gpsimd", CA)
    b.wait("scalar", ST)

    b.finalize_hazards()
    with nc.Block() as block:
        block.sync(lambda e: b.r["sync"].replay(e))
        block.scalar(lambda e: b.r["scalar"].replay(e))
        block.tensor(lambda e: b.r["tensor"].replay(e))
        block.vector(lambda e: b.r["vector"].replay(e))
        block.gpsimd(lambda e: b.r["gpsimd"].replay(e))

    stack.close()
    return nc


def _consts():
    c = {}
    c["identF"] = np.eye(128, dtype=np.float32)
    c["identB"] = np.eye(128, dtype=np.float32).astype(ml_dtypes.bfloat16)
    c["onesColF"] = np.ones((128, 1), np.float32)
    c["onesRowF"] = np.ones((1, 128), np.float32)
    c["ones16"] = np.ones((16, CCH), np.float32)
    s = np.arange(128)[:, None]
    t = np.arange(128)[None, :]
    c["mUIT"] = np.where(t <= s, 0.0, -1e9).astype(ml_dtypes.bfloat16)
    c["mLST"] = np.where(s < t, 0.0, -1e9).astype(ml_dtypes.bfloat16)
    sm = np.zeros((16, 4, 512), np.float32)
    for g in range(4):
        for j in range(4):
            sm[4 * g + j, g, 128 * j:128 * (j + 1)] = 1.0
    c["selm4"] = sm
    c["Ib4"] = np.tile(np.eye(128, dtype=np.float32), (1, 4)).astype(ml_dtypes.bfloat16)
    c["twoIb4"] = np.tile(2.0 * np.eye(128, dtype=np.float32), (1, 4)).copy()
    e8 = np.zeros((8, 128), np.float32)
    for e in range(8):
        e8[e, 16 * e:16 * e + 16] = 1.0
    c["E8"] = e8.astype(ml_dtypes.bfloat16)
    c["sel16"] = np.kron(np.eye(16, dtype=np.float32), np.ones((1, 128), np.float32))
    return c


_NC_CACHE = {}


def kernel(**inputs):
    key = (tuple(PHASES), tuple(DBG_OUT))
    if key not in _NC_CACHE:
        _NC_CACHE[key] = build_program()
    nc = _NC_CACHE[key]

    x = np.ascontiguousarray(np.asarray(inputs["L0_attn_x"], np.float32))
    Bsz, T, _ = x.shape
    common = _consts()
    ren = {
        "wq": "L0_attn_wq", "wk": "L0_attn_wk", "wv": "L0_attn_wv", "wo": "L0_attn_wo",
        "wfa": "L0_attn_wfa", "wfb": "L0_attn_wfb", "wb": "L0_attn_wb",
        "wga": "L0_attn_wga", "wgb": "L0_attn_wgb",
        "w_gate": "L0_attn_w_gate", "w_up": "L0_attn_w_up", "w_down": "L0_attn_w_down",
    }
    for k, src in ren.items():
        common[k] = np.ascontiguousarray(np.asarray(inputs[src], np.float32))
    common["a_log"] = np.asarray(inputs["L0_attn_a_log"], np.float32).reshape(16, 1).copy()
    common["dt_bias"] = np.asarray(inputs["L0_attn_dt_bias"], np.float32).reshape(16, 1).copy()
    common["o_norm_w"] = np.asarray(inputs["L0_attn_o_norm_w"], np.float32).reshape(1, 128).copy()
    onw4 = np.zeros((4, 4, 128), np.float32)
    for r in range(4):
        onw4[r, r, :] = common["o_norm_w"][0]
    common["onw4"] = onw4
    common["ln1_wT"] = np.ascontiguousarray(
        np.asarray(inputs["L0_attn_ln1_w"], np.float32).reshape(16, 128).T)
    common["ln2_wT"] = np.ascontiguousarray(
        np.asarray(inputs["L0_attn_ln2_w"], np.float32).reshape(16, 128).T)
    for cn, src in (("conv_qT", "L0_attn_conv_q"), ("conv_kT", "L0_attn_conv_k"),
                    ("conv_vT", "L0_attn_conv_v")):
        cw = np.asarray(inputs[src], np.float32)
        common[cn] = np.ascontiguousarray(
            cw.reshape(4, 16, 128).transpose(2, 1, 0).reshape(128, 64))
    common["fwT"] = np.ascontiguousarray(np.asarray(inputs["L0_ffn_weight"], np.float32).T)
    common["gate_wT"] = np.ascontiguousarray(np.asarray(inputs["L0_ffn_gate_w"], np.float32).T)
    common["A_wT"] = np.ascontiguousarray(
        np.asarray(inputs["L0_ffn_A_w"], np.float32).reshape(128, D).T)
    common["B_wRT"] = np.ascontiguousarray(
        np.asarray(inputs["L0_ffn_B_w"], np.float32).transpose(0, 2, 1).reshape(128, D))

    in_maps = []
    for c in range(8):
        bidx, half = c >> 1, c & 1
        m = dict(common)
        m["x_own"] = np.ascontiguousarray(x[bidx, half * TOWN:(half + 1) * TOWN])
        m["x_halo"] = np.ascontiguousarray(
            x[bidx, TOWN - THALO:TOWN] if half == 1 else x[bidx, 0:THALO])
        m["halo_mask"] = np.full((128, 1), float(half), np.float32)
        m["mA"] = np.full((128, 1), float(half), np.float32)
        m["mB"] = np.zeros((128, 1), np.float32)
        in_maps.append(m)

    res = run_bass_kernel_spmd(nc, in_maps, core_ids=list(range(8)))
    out = np.empty((Bsz, T, D), np.float32)
    for c in range(8):
        bidx, half = c >> 1, c & 1
        out[bidx, half * TOWN:(half + 1) * TOWN] = res.results[c]["y"]
    if DBG_OUT:
        kernel.dbg = [{name: r["dbg_" + name] for name in DBG_OUT} for r in res.results]
    return out


# version 02
# speedup vs baseline: 1.0394x; 1.0394x over previous
import numpy as np
import ml_dtypes
from contextlib import ExitStack

import concourse.bass as bass
import concourse.mybir as mybir
from concourse.bass_utils import run_bass_kernel_spmd

F32 = mybir.dt.float32
BF16 = mybir.dt.bfloat16
AF = mybir.ActivationFunctionType
ALU = mybir.AluOpType
AXX = mybir.AxisListType.X

D = 2048
NK = 16
TOWN = 2048
THALO = 128
TT_ = TOWN + THALO
H, DK, DV = 16, 128, 128
CCH = 128
NCH = TOWN // CCH
LORA_ALPHA = 16.0

PHASES = ("p1", "p2", "p3", "p4", "p5", "p6", "p7")
DBG_OUT = ()
P3_SWEEP1_ONLY = False
P1_TRUNC = "full"
P1_SKIP_LORA = False

ENGINES = ("sync", "scalar", "tensor", "vector", "gpsimd")


class Rec:
    def __init__(self):
        self.items = []

    def __getattr__(self, name):
        if name.startswith("_"):
            raise AttributeError(name)
        def call(*a, **k):
            it = [name, a, k, None]
            self.items.append(it)
            class _P:
                def then_inc(_self, sem, n):
                    it[3] = (sem, n)
                    return _self
            return _P()
        return call

    def replay(self, eng):
        for name, a, k, inc in self.items:
            r = getattr(eng, name)(*a, **k)
            if inc is not None:
                r.then_inc(*inc)


class Sem:
    def __init__(self, handle, name):
        self.h = handle
        self.name = name
        self.n = 0


class B:
    def __init__(self, nc, stack):
        self.nc = nc
        self.stack = stack
        self.r = {e: Rec() for e in ENGINES}
        self.sems = {}

    def sem(self, name):
        s = Sem(self.stack.enter_context(self.nc.semaphore(name)), name)
        self.sems[name] = s
        return s

    def slot_sems(self, name, n):
        return [self.sem(f"{name}{i}") for i in range(n)]

    def dma(self, queue, out, in_, sem):
        self.r[queue].dma_start(out=out, in_=in_).then_inc(sem.h, 16)
        sem.n += 16
        return sem.n

    def wait(self, eng, sem, val=None):
        self.r[eng].wait_ge(sem.h, sem.n if val is None else val)

    def inc(self, eng, sem):
        it = self.r[eng].items[-1]
        assert it[3] is None
        sem.n += 1
        it[3] = (sem.h, 1)
        return sem.n

    def finalize_hazards(self):
        from concourse.bass import AP
        skip = {"wait_ge", "nop", "sem_inc", "dma_start", "dma_start_transpose"}
        for ename in ENGINES:
            rec = self.r[ename]
            if not rec.items:
                continue
            asem = self.sem("AUTO_" + ename)

            def outs_ins(name, a, k):
                outs, ins = set(), set()
                for key, v in list(enumerate(a)) + list(k.items()):
                    if isinstance(v, AP):
                        tn = v.tensor.name
                        if key == 0 or key in ("out", "accum_out", "out_max", "out_indices"):
                            outs.add(tn)
                        else:
                            ins.add(tn)
                return outs, ins

            new_items = []
            hist = []
            for it in rec.items:
                name, a, k, inc = it
                if name in skip:
                    new_items.append(it)
                    continue
                outs, ins = outs_ins(name, a, k)
                producer = None
                for pit, pouts in hist[-4:]:
                    if ins & pouts:
                        producer = pit
                if producer is not None:
                    if producer[3] is None:
                        producer[3] = (asem, "AUTOINC")
                    new_items.append(["wait_ge", ("EDGE", producer), {}, None])
                hist.append((it, outs))
                new_items.append(it)
            cnt = 0
            counts = {}
            prod_val = {}
            for it in new_items:
                if it[3] is not None and isinstance(it[3], tuple):
                    if it[3][1] == "AUTOINC":
                        cnt += 1
                        it[3] = (asem.h, 1)
                hnd, n = it[3] if it[3] is not None else (None, 0)
                if hnd is not None:
                    key = id(hnd)
                    c = counts.get(key, 0) + n
                    counts[key] = c
                    prod_val[id(it)] = (hnd, c)
            for it in new_items:
                if it[0] == "wait_ge" and len(it[1]) == 2 and it[1][0] == "EDGE":
                    hnd, val = prod_val[id(it[1][1])]
                    it[1] = (hnd, val)
            asem.n = cnt
            rec.items = new_items

    def barrier(self):
        bar = self.sems["BAR"]
        for name, sm in self.sems.items():
            if sm.n == 0 or name in ("TE", "VE", "SE", "GP", "BAR", "CC") \
                    or name.startswith("AUTO"):
                continue
            for eng in ("sync", "scalar", "gpsimd"):
                self.wait(eng, sm)
        target = bar.n + len(ENGINES)
        for e in ENGINES:
            self.r[e].nop().then_inc(bar.h, 1)
        bar.n = target
        for e in ENGINES:
            self.r[e].wait_ge(bar.h, target)


def build_program():
    nc = bass.Bass("TRN2", target_bir_lowering=False, debug=False,
                   detect_race_conditions=False)
    stack = ExitStack()
    b = B(nc, stack)
    ap = {}

    def inp(name, shape, dtype=F32):
        ap[name] = nc.dram_tensor(name, list(shape), dtype, kind="ExternalInput").ap()
        return ap[name]

    inp("x_own", (TOWN, D)); inp("x_halo", (THALO, D))
    for w in ("wq", "wk", "wv", "wo"):
        inp(w, (D, D))
    inp("conv_qT", (128, 64)); inp("conv_kT", (128, 64)); inp("conv_vT", (128, 64))
    inp("wfa", (D, 128)); inp("wfb", (128, 16)); inp("wb", (D, 16))
    inp("wga", (D, 128)); inp("wgb", (128, D))
    inp("a_log", (16, 1)); inp("dt_bias", (16, 1))
    inp("o_norm_w", (1, 128))
    inp("onw4", (4, 4, 128))
    inp("ln1_wT", (128, 16)); inp("ln2_wT", (128, 16))
    inp("w_gate", (D, 8192)); inp("w_up", (D, 8192)); inp("w_down", (8192, D))
    inp("fwT", (D, D)); inp("gate_wT", (D, 8))
    inp("A_wT", (D, 128)); inp("B_wRT", (128, D))
    inp("identF", (128, 128)); inp("identB", (128, 128), BF16)
    inp("onesColF", (128, 1)); inp("onesRowF", (1, 128)); inp("ones16", (16, CCH))
    inp("mUIT", (128, 128), BF16); inp("mLST", (128, 128), BF16)
    inp("selm4", (16, 4, 512))
    inp("Ib4", (128, 512), BF16); inp("twoIb4", (128, 512))
    inp("E8", (8, 128), BF16)
    inp("halo_mask", (128, 1)); inp("mA", (128, 1)); inp("mB", (128, 1))
    inp("sel16", (16, 2048))

    y = nc.dram_tensor("y", [TOWN, D], F32, kind="ExternalOutput").ap()

    scr = {}
    def scratch(name, shape, dtype=F32):
        scr[name] = nc.dram_tensor(name, list(shape), dtype, kind="Internal")
        return scr[name]

    scratch("x1scr", (D, TT_))
    scratch("xiscr", (D, TOWN))
    scratch("x2scr", (D, TOWN))
    scratch("qfm", (D, TOWN), BF16)
    scratch("kfm", (D, TOWN), BF16)
    scratch("ktok", (H, TOWN, DK), BF16)
    scratch("vtok", (H, TOWN, DV), BF16)
    scratch("ofm", (D, TOWN), BF16)
    scratch("uv_s", (NCH, 128, H * CCH), BF16)
    scratch("wiT_s", (NCH, 128, H * CCH), BF16)
    scratch("w2T_s", (NCH, 128, H * CCH), BF16)
    scratch("qh_s", (NCH, 128, H * CCH), BF16)
    scratch("kb_s", (NCH, 128, H * CCH), BF16)
    scratch("gcc_s", (NCH, 128, 16))
    scratch("mscr", (64, 128, TOWN), BF16)
    scratch("wdc", (64, 128, D), BF16)
    scratch("pcdbg", (128, NK, 520), BF16)
    scratch("scomb_dbg", (128, 2048))
    scratch("ma_dbg", (128, 1))
    scratch("g_dbg", (16, TOWN))
    scratch("beta_dbg", (16, TOWN))
    scratch("hga_dbg", (128, TOWN), BF16)
    scratch("cc_in", (128, H * DK))
    scratch("cc_out", (256, H * DK))

    dbg_outs = {}
    for name in DBG_OUT:
        t = scr[name]
        dbg_outs[name] = nc.dram_tensor("dbg_" + name, list(t.shape), t.dtype,
                                        kind="ExternalOutput").ap()

    E = stack.enter_context
    for s in ("LD", "CA", "ST", "TE", "VE", "SE", "GP", "BAR", "CC"):
        b.sem(s)
    LD, CA, ST = b.sems["LD"], b.sems["CA"], b.sems["ST"]
    TE, VE, SE, CC = b.sems["TE"], b.sems["VE"], b.sems["SE"], b.sems["CC"]
    SY, SC, TN, VR, GP = (b.r["sync"], b.r["scalar"], b.r["tensor"],
                          b.r["vector"], b.r["gpsimd"])

    identF = E(nc.sbuf_tensor("s_identF", [128, 128], F32))
    identB = E(nc.sbuf_tensor("s_identB", [128, 128], BF16))
    onesColF = E(nc.sbuf_tensor("s_onesColF", [128, 1], F32))
    onesRowF = E(nc.sbuf_tensor("s_onesRowF", [1, 128], F32))
    ones16 = E(nc.sbuf_tensor("s_ones16", [16, CCH], F32))
    mUIT = E(nc.sbuf_tensor("s_mUIT", [128, 128], BF16))
    mLST = E(nc.sbuf_tensor("s_mLST", [128, 128], BF16))
    selm4 = E(nc.sbuf_tensor("s_selm4", [16, 4, 512], F32))
    Ib4 = E(nc.sbuf_tensor("s_Ib4", [128, 512], BF16))
    twoIb4 = E(nc.sbuf_tensor("s_twoIb4", [128, 512], F32))
    E8 = E(nc.sbuf_tensor("s_E8", [8, 128], BF16))
    halo_m = E(nc.sbuf_tensor("s_halo_m", [128, 1], F32))
    mA = E(nc.sbuf_tensor("s_mA", [128, 1], F32))
    mB = E(nc.sbuf_tensor("s_mB", [128, 1], F32))
    gwt = E(nc.sbuf_tensor("s_gwt", [128, NK, 8], BF16))
    at = E(nc.sbuf_tensor("s_at", [128, NK, 128], BF16))
    sel16 = E(nc.sbuf_tensor("s_sel16", [16, 2048], F32))
    bt = E(nc.sbuf_tensor("s_bt", [128, NK, 128], BF16))

    for name, t in (("identF", identF), ("identB", identB), ("onesColF", onesColF),
                    ("onesRowF", onesRowF), ("ones16", ones16), ("mUIT", mUIT),
                    ("mLST", mLST), ("selm4", selm4), ("Ib4", Ib4), ("twoIb4", twoIb4), ("E8", E8),
                    ("halo_mask", halo_m), ("mA", mA), ("mB", mB), ("sel16", sel16)):
        b.dma("sync", t[:], ap[name], LD)
    b.wait("tensor", LD); b.wait("vector", LD); b.wait("scalar", LD); b.wait("gpsimd", LD)

    x1v = scr["x1scr"].ap().rearrange("(k r) t -> r k t", r=128)
    xiv = scr["xiscr"].ap().rearrange("(k r) t -> r k t", r=128)
    x2v = scr["x2scr"].ap().rearrange("(k r) t -> r k t", r=128)
    qfmv = scr["qfm"].ap().rearrange("(k r) t -> r k t", r=128)
    kfmv = scr["kfm"].ap().rearrange("(k r) t -> r k t", r=128)
    ofmv = scr["ofm"].ap().rearrange("(k r) t -> r k t", r=128)

    class TransposeHelper:
        def __init__(self, pbanks):
            self.p = pbanks
            self.last_ve = [0, 0]

        def run(self, items, ident):
            n = len(items)
            for base in range(0, n, 4):
                grp = items[base:base + 4]
                sel = (base // 4) % 2
                pp = self.p[sel]
                if self.last_ve[sel]:
                    b.wait("tensor", VE, self.last_ve[sel])
                for j, (sap, _) in enumerate(grp):
                    TN.transpose(pp[:, j * 128:(j + 1) * 128], sap, ident)
                t = b.inc("tensor", TE)
                b.wait("vector", TE, t)
                for j, (_, dap) in enumerate(grp):
                    VR.tensor_copy(dap, pp[:, j * 128:(j + 1) * 128])
                self.last_ve[sel] = b.inc("vector", VE)

    def ffn_setup():
        gwv = ap["gate_wT"].rearrange("(k r) e -> r k e", r=128)
        b.dma("gpsimd", gwt[:, :, :], gwv, CA)
        awv = ap["A_wT"].rearrange("(k r) e -> r k e", r=128)
        b.dma("gpsimd", at[:, :, :], awv, CA)
        bwv = ap["B_wRT"].rearrange("p (j c) -> p j c", j=16)
        b.dma("gpsimd", bt[:, :, :], bwv, CA)
        b.wait("tensor", CA)

    def ffn_phase(which):
        first = which == "p1"
        with ExitStack() as ps:
            PE = ps.enter_context
            xfm = PE(nc.sbuf_tensor("xfm" + which, [128, NK, 1024], F32))
            lnb = PE(nc.sbuf_tensor("lnb" + which, [128, NK, 1024], BF16))
            sq = PE(nc.sbuf_tensor("sq" + which, [128, 2, NK, 128], F32))
            xtm = PE(nc.sbuf_tensor("xtm" + which, [128, 2, D], F32))
            wt = PE(nc.sbuf_tensor("wt" + which, [128, 2, NK, 128], BF16))
            stat = PE(nc.sbuf_tensor("stat" + which, [128, 32], F32))
            colw = PE(nc.sbuf_tensor("colw" + which, [128, 16], F32))
            rowt = PE(nc.sbuf_tensor("rowt" + which, [16, 128], F32))
            cwT = PE(nc.sbuf_tensor("cwT" + which, [8, 1024], BF16))
            cwb = PE(nc.sbuf_tensor("cwb" + which, [128, 1024], BF16))
            tsc = PE(nc.sbuf_tensor("tsc" + which, [128, 1024], BF16))
            otm = PE(nc.sbuf_tensor("otm" + which, [128, 2, D], F32))
            pA = PE(nc.psum_tensor("pA" + which, [128, 1024], F32))
            pB = PE(nc.psum_tensor("pB" + which, [128, 1024], F32))
            pT = PE(nc.psum_tensor("pT" + which, [128, 512], F32))
            pT2 = PE(nc.psum_tensor("pT2" + which, [128, 512], F32))
            pTB = PE(nc.psum_tensor("pTB" + which, [128, 512], BF16))
            pSM = PE(nc.psum_tensor("pSM" + which, [128, 128], F32))
            pST = pSM

            if first:
                ffn_setup()
            fwv = ap["fwT"].rearrange("(k r) o -> r k o", r=128)
            XT = b.slot_sems("XT" + which, 2)
            WTS = b.slot_sems("WTS" + which, 2)
            OTS = b.slot_sems("OTS" + which, 2)
            xt_busy = [0, 0]
            wt_busy = [0, 0]
            ot_busy = [0, 0]

            thF = TransposeHelper([pT, pT2])
            panels = ([(THALO, "halo")] if first else []) + [(1024, 0), (1024, 1)]
            st_prev = 0
            for T, ptag in panels:
                halo = ptag == "halo"
                ntile = T // 128
                ngg = max(1, T // 512)
                wgg = min(512, T)
                b.wait("vector", ST, st_prev)
                if first:
                    src = ap["x_halo"] if halo else ap["x_own"]
                    off = 0 if halo else ptag * 1024
                    for r in range(ntile):
                        if xt_busy[r % 2]:
                            b.wait("sync", TE, xt_busy[r % 2])
                        b.dma("sync", xtm[:, r % 2, :],
                              src[off + r * 128: off + (r + 1) * 128, :], XT[r % 2])
                        b.wait("tensor", XT[r % 2], XT[r % 2].n)
                        items = [(xtm[:, r % 2, k * 128:(k + 1) * 128],
                                  xfm[:, k, r * 128:(r + 1) * 128]) for k in range(NK)]
                        thF.run(items, identF[:, :])
                        xt_busy[r % 2] = TE.n
                else:
                    b.wait("sync", TE, TE.n)
                    b.dma("sync", xfm[:, :, 0:1024], x2v[:, :, ptag * 1024:(ptag + 1) * 1024], LD)
                    b.wait("tensor", LD); b.wait("vector", LD); b.wait("scalar", LD)

                if first and P1_TRUNC == "xonly":
                    b.wait("scalar", VE)
                    c0 = 0 if halo else THALO + ptag * 1024
                    st_prev = b.dma("scalar", x1v[:, :, c0:c0 + T], xfm[:, :, 0:T], ST)
                    continue
                b.wait("scalar", VE)
                te_sq = {}
                for r in range(ntile):
                    if r >= 2:
                        b.wait("scalar", TE, te_sq[r - 2])
                    SC.activation(sq[:, r % 2, :, :], xfm[:, :, r * 128:(r + 1) * 128], AF.Square)
                    s = b.inc("scalar", SE)
                    b.wait("tensor", SE, s)
                    for k in range(NK):
                        TN.matmul(pST[:, r:r + 1], xfm[:, k, r * 128:(r + 1) * 128],
                                  onesColF[:], start=(r == 0 and k == 0), stop=False)
                        TN.matmul(pST[:, 8 + r:9 + r], sq[:, r % 2, k, :], onesColF[:],
                                  start=False, stop=(r == ntile - 1 and k == NK - 1))
                    te_sq[r] = b.inc("tensor", TE)
                t = TE.n
                b.wait("vector", TE, t)
                nt = ntile
                VR.memset(colw[:, :], 0.0)
                VR.tensor_copy(stat[:, 0:nt], pST[:, 0:nt])
                VR.tensor_copy(stat[:, 8:8 + nt], pST[:, 8:8 + nt])
                VR.tensor_scalar(colw[:, 0:nt], stat[:, 0:nt], 1.0 / D, None, ALU.mult)
                VR.tensor_mul(colw[:, 8:8 + nt], colw[:, 0:nt], colw[:, 0:nt])
                VR.scalar_tensor_tensor(colw[:, 8:8 + nt], stat[:, 8:8 + nt], 1.0 / D,
                                        colw[:, 8:8 + nt], ALU.mult, ALU.subtract)
                v = b.inc("vector", VE)
                b.wait("scalar", VE, v)
                SC.activation(colw[:, 8:8 + nt], colw[:, 8:8 + nt], AF.Sqrt, bias=np.float32(1e-5))
                s = b.inc("scalar", SE)
                b.wait("vector", SE, s)
                VR.reciprocal(colw[:, 8:8 + nt], colw[:, 8:8 + nt])
                VR.tensor_mul(colw[:, 0:nt], colw[:, 0:nt], colw[:, 8:8 + nt])
                VR.tensor_scalar(colw[:, 0:nt], colw[:, 0:nt], -1.0, None, ALU.mult)
                v = b.inc("vector", VE)
                b.wait("tensor", VE, v)
                TN.transpose(pSM[0:16, 0:128], colw[:, 0:16], identF[:, :])
                t = b.inc("tensor", TE)
                b.wait("vector", TE, t)
                VR.tensor_copy(rowt[:, :], pSM[0:16, 0:128])
                v = b.inc("vector", VE)
                b.wait("tensor", VE, v)
                for r in range(ntile):
                    TN.matmul(pA[:, r * 128:(r + 1) * 128],
                              sel16[:, (8 + r) * 128:(9 + r) * 128], rowt[:, :],
                              start=(r % 4 == 0), stop=(r % 4 == 3 or r == ntile - 1))
                    TN.matmul(pB[:, r * 128:(r + 1) * 128],
                              sel16[:, r * 128:(r + 1) * 128], rowt[:, :],
                              start=(r % 4 == 0), stop=(r % 4 == 3 or r == ntile - 1))
                t = b.inc("tensor", TE)
                b.wait("vector", TE, t)
                for k in range(NK):
                    VR.tensor_mul(lnb[:, k, 0:T], xfm[:, k, 0:T], pA[:, 0:T])
                    VR.tensor_add(lnb[:, k, 0:T], lnb[:, k, 0:T], pB[:, 0:T])
                v_lnb = b.inc("vector", VE)
                b.wait("tensor", VE, v_lnb)

                if first and P1_TRUNC == "ln":
                    qv_dbg = scr["qfm"].ap().rearrange("(k r) t -> r k t", r=128)
                    b.wait("scalar", VE)
                    if not halo:
                        b.dma("scalar", qv_dbg[:, :, ptag * 1024:ptag * 1024 + T],
                              lnb[:, :, 0:T], ST)
                    continue
                for r in range(ntile):
                    for k in range(NK):
                        TN.matmul(pSM[:, 0:8], lnb[:, k, r * 128:(r + 1) * 128],
                                  gwt[:, k, :], start=(k == 0), stop=(k == NK - 1))
                    t = b.inc("tensor", TE)
                    b.wait("vector", TE, t)
                    VR.reduce_max(stat[:, 0:1], pSM[:, 0:8], AXX)
                    VR.tensor_scalar(stat[:, 8:16], pSM[:, 0:8], stat[:, 0:1], 0.0,
                                     ALU.subtract, ALU.is_ge)
                    VR.scalar_tensor_tensor(stat[:, 16:24], stat[:, 8:16], -1e30, pSM[:, 0:8],
                                            ALU.mult, ALU.add)
                    VR.reduce_max(stat[:, 1:2], stat[:, 16:24], AXX)
                    VR.tensor_sub(stat[:, 2:3], stat[:, 1:2], stat[:, 0:1])
                    v = b.inc("vector", VE)
                    b.wait("scalar", VE, v)
                    SC.activation(stat[:, 3:4], stat[:, 2:3], AF.Sigmoid)
                    s = b.inc("scalar", SE)
                    b.wait("vector", SE, s)
                    VR.tensor_scalar(stat[:, 4:5], stat[:, 3:4], -1.0, 1.0, ALU.mult, ALU.add)
                    VR.tensor_scalar(stat[:, 24:32], stat[:, 16:24], stat[:, 1:2], 0.0,
                                     ALU.subtract, ALU.is_ge)
                    VR.tensor_scalar(stat[:, 8:16], stat[:, 8:16], stat[:, 4:5], None, ALU.mult)
                    VR.tensor_scalar(stat[:, 24:32], stat[:, 24:32], stat[:, 3:4], None, ALU.mult)
                    VR.tensor_add(stat[:, 24:32], stat[:, 24:32], stat[:, 8:16])
                    if first and P1_TRUNC == "lora" and not halo and ptag == 0:
                        VR.tensor_copy(tsc[:, r * 32:(r + 1) * 32], stat[:, 0:32])
                    v = b.inc("vector", VE)
                    b.wait("tensor", VE, v)
                    TN.transpose(pSM[0:8, 64:192] if False else pSM[0:8, 0:128],
                                 stat[:, 24:32], identF[:, :])
                    t = b.inc("tensor", TE)
                    b.wait("vector", TE, t)
                    VR.tensor_copy(cwT[:, r * 128:(r + 1) * 128], pSM[0:8, 0:128])
                    v = b.inc("vector", VE)
                    b.wait("tensor", VE, v)
                for r in range(ntile):
                    TN.matmul(pB[:, r * 128:(r + 1) * 128], E8[:], cwT[:, r * 128:(r + 1) * 128],
                              start=(r % 4 == 0), stop=(r % 4 == 3 or r == ntile - 1))
                for gg in range(ngg):
                    for k in range(NK):
                        TN.matmul(pA[:, gg * 512:gg * 512 + wgg], at[:, k, :],
                                  lnb[:, k, gg * 512:gg * 512 + wgg],
                                  start=(k == 0), stop=(k == NK - 1))
                t = b.inc("tensor", TE)
                b.wait("vector", TE, t)
                VR.tensor_copy(cwb[:, 0:T], pB[:, 0:T])
                if not (first and P1_TRUNC == "lora"):
                    for gg in range(ngg):
                        VR.scalar_tensor_tensor(tsc[:, gg * 512:gg * 512 + wgg],
                                                pA[:, gg * 512:gg * 512 + wgg], LORA_ALPHA,
                                                cwb[:, gg * 512:gg * 512 + wgg], ALU.mult, ALU.mult)
                v = b.inc("vector", VE)
                b.wait("tensor", VE, v)

                if first and P1_TRUNC == "lora" and not halo:
                    kv_dbg = scr["kfm"].ap().rearrange("(k r) t -> r k t", r=128)
                    b.wait("scalar", VE)
                    b.dma("scalar", kv_dbg[:, 0:1, ptag * 1024:ptag * 1024 + T]
                          .rearrange("r k t -> r (k t)"), tsc[:, 0:T], ST)
                    b.dma("scalar", kv_dbg[:, 1:2, ptag * 1024:ptag * 1024 + T]
                          .rearrange("r k t -> r (k t)"), cwb[:, 0:T], ST)
                if first and P1_TRUNC == "lora":
                    continue
                ve_done = {}
                te_mm = {}
                for m in range(NK):
                    bsel = m % 2
                    pO = pA if bsel == 0 else pB
                    if wt_busy[bsel]:
                        b.wait("gpsimd", TE, wt_busy[bsel])
                    b.dma("gpsimd", wt[:, bsel], fwv[:, :, m * 128:(m + 1) * 128], WTS[bsel])
                    b.wait("tensor", WTS[bsel], WTS[bsel].n)
                    if m >= 2:
                        b.wait("tensor", VE, ve_done[m - 2])
                    for gg in range(ngg):
                        for k in range(NK):
                            TN.matmul(pO[:, gg * 512:gg * 512 + wgg], wt[:, bsel, k, :],
                                      lnb[:, k, gg * 512:gg * 512 + wgg],
                                      start=(k == 0), stop=(P1_SKIP_LORA and k == NK - 1))
                        if P1_SKIP_LORA:
                            TN.matmul(pO[:, gg * 512:gg * 512 + wgg], bt[:, m, :],
                                      tsc[:, gg * 512:gg * 512 + wgg], start=False, stop=True,
                                      skip_group_check=True) if False else None
                        if not P1_SKIP_LORA:
                            TN.matmul(pO[:, gg * 512:gg * 512 + wgg], bt[:, m, :],
                                      tsc[:, gg * 512:gg * 512 + wgg], start=False, stop=True)
                    te_mm[m] = b.inc("tensor", TE)
                    wt_busy[bsel] = te_mm[m]
                    b.wait("vector", TE, te_mm[m])
                    for gg in range(ngg):
                        VR.scalar_tensor_tensor(xfm[:, m, gg * 512:gg * 512 + wgg],
                                                pO[:, gg * 512:gg * 512 + wgg], 0.5,
                                                xfm[:, m, gg * 512:gg * 512 + wgg],
                                                ALU.mult, ALU.add)
                    ve_done[m] = b.inc("vector", VE)
                b.wait("scalar", VE)
                if first:
                    c0 = 0 if halo else THALO + ptag * 1024
                    st_prev = b.dma("scalar", x1v[:, :, c0:c0 + T], xfm[:, :, 0:T], ST)
                else:
                    for r in range(ntile):
                        if ot_busy[r % 2]:
                            b.wait("vector", OTS[r % 2], ot_busy[r % 2])
                        items = [(xfm[:, k, r * 128:(r + 1) * 128],
                                  otm[:, r % 2, k * 128:(k + 1) * 128]) for k in range(NK)]
                        thF.run(items, identF[:, :])
                        b.wait("scalar", VE)
                        ot_busy[r % 2] = b.dma(
                            "scalar", y[ptag * 1024 + r * 128: ptag * 1024 + (r + 1) * 128, :],
                            otm[:, r % 2, :], OTS[r % 2])
            b.barrier()

    def proj_phase(g_res, beta_res, hga_res):
        with ExitStack() as ps:
            PE = ps.enter_context
            xp = PE(nc.sbuf_tensor("xp2", [128, NK, 512], F32))
            hb = PE(nc.sbuf_tensor("hb2", [128, NK, 512], BF16))
            sq2 = PE(nc.sbuf_tensor("sq2", [128, 2, NK, 128], F32))
            pc = PE(nc.sbuf_tensor("pc2", [128, NK, 520], BF16))
            carryW = PE(nc.sbuf_tensor("carryW2", [128, 3, NK, 3], BF16))
            wbuf = PE(nc.sbuf_tensor("wbuf2", [128, 2, 512], BF16))
            stat = PE(nc.sbuf_tensor("stat2", [128, 32], F32))
            colw = PE(nc.sbuf_tensor("colw2", [128, 16], F32))
            rowt = PE(nc.sbuf_tensor("rowt2", [16, 128], F32))
            cv = PE(nc.sbuf_tensor("cv2", [128, 512], BF16))
            qs = PE(nc.sbuf_tensor("qs2", [128, 512], BF16))
            sqq = PE(nc.sbuf_tensor("sqq2", [128, 512], F32))
            lcol = PE(nc.sbuf_tensor("lcol2", [128, 4], F32))
            tkb = PE(nc.sbuf_tensor("tkb2", [128, 2, 4, 128], BF16))
            fmb = PE(nc.sbuf_tensor("fmb2", [128, 2, 512], BF16))
            hfa = PE(nc.sbuf_tensor("hfa2", [128, 512], BF16))
            wfa_s = PE(nc.sbuf_tensor("wfa_s", [128, NK, 128], BF16))
            wga_s = PE(nc.sbuf_tensor("wga_s", [128, NK, 128], BF16))
            wb_s = PE(nc.sbuf_tensor("wb_s", [128, NK, 16], BF16))
            wfb_s = PE(nc.sbuf_tensor("wfb_s", [128, 16], BF16))
            cqT = PE(nc.sbuf_tensor("cqT", [128, 64], F32))
            ckT = PE(nc.sbuf_tensor("ckT", [128, 64], F32))
            cvT = PE(nc.sbuf_tensor("cvT", [128, 64], F32))
            ln1c = PE(nc.sbuf_tensor("ln1c", [128, 16], F32))
            acol = PE(nc.sbuf_tensor("acol", [16, 4], F32))
            zw = PE(nc.sbuf_tensor("zw2", [16, 4, 512], F32))
            pP = PE(nc.psum_tensor("pP2", [128, 2048], F32))
            pA = PE(nc.psum_tensor("pA2", [128, 512], F32))
            pB = PE(nc.psum_tensor("pB2", [128, 512], F32))
            pC = PE(nc.psum_tensor("pC2", [128, 512], F32))
            pTB = PE(nc.psum_tensor("pTB2", [128, 512], BF16))

            WSL = b.slot_sems("WSL2", 2)
            TKS = b.slot_sems("TKS2", 2)
            FMS = b.slot_sems("FMS2", 2)
            te_wslot = [0, 0]
            tk_busy = [0, 0]
            fm_busy = [0, 0]
            te_sqslot = [0, 0]
            ve_lcol = 0

            b.dma("gpsimd", wfa_s[:, :, :], ap["wfa"].rearrange("(k r) c -> r k c", r=128), CA)
            b.dma("gpsimd", wga_s[:, :, :], ap["wga"].rearrange("(k r) c -> r k c", r=128), CA)
            b.dma("gpsimd", wb_s[:, :, :], ap["wb"].rearrange("(k r) c -> r k c", r=128), CA)
            b.dma("gpsimd", wfb_s[:, :], ap["wfb"], CA)
            b.dma("sync", cqT[:, :], ap["conv_qT"], LD)
            b.dma("sync", ckT[:, :], ap["conv_kT"], LD)
            b.dma("sync", cvT[:, :], ap["conv_vT"], LD)
            b.dma("sync", ln1c[:, :], ap["ln1_wT"], LD)
            b.dma("sync", acol[0:16, 0:1], ap["a_log"], LD)
            b.dma("sync", acol[0:16, 1:2], ap["dt_bias"], LD)
            b.wait("scalar", LD)
            SC.activation(acol[0:16, 2:3], acol[0:16, 0:1], AF.Exp)
            se0 = b.inc("scalar", SE)
            b.wait("vector", SE, se0)
            VR.tensor_scalar(acol[0:16, 2:3], acol[0:16, 2:3], -1.0, None, ALU.mult)
            b.inc("vector", VE)
            b.wait("tensor", LD); b.wait("tensor", CA); b.wait("vector", LD)

            te_xp = 0
            ve_xp = 0
            panels = [(THALO, "halo")] + [(512, p) for p in range(4)]
            for T, ptag in panels:
                halo = ptag == "halo"
                nt = ntile = T // 128
                if te_xp:
                    b.wait("sync", TE, te_xp)
                    b.wait("sync", VE, ve_xp)
                c0 = 0 if halo else THALO + ptag * 512
                b.dma("sync", xp[:, :, 0:T], x1v[:, :, c0:c0 + T], LD)
                b.wait("scalar", LD); b.wait("tensor", LD); b.wait("vector", LD)
                b.wait("tensor", SE, SE.n)
                b.wait("tensor", VE, VE.n)
                for r in range(ntile):
                    if te_sqslot[r % 2]:
                        b.wait("scalar", TE, te_sqslot[r % 2])
                    SC.activation(sq2[:, r % 2, :, :], xp[:, :, r * 128:(r + 1) * 128], AF.Square)
                    s = b.inc("scalar", SE)
                    b.wait("tensor", SE, s)
                    for k in range(NK):
                        TN.matmul(pC[:, r:r + 1], xp[:, k, r * 128:(r + 1) * 128],
                                  onesColF[:], start=(r == 0 and k == 0), stop=False)
                        TN.matmul(pC[:, 8 + r:9 + r], sq2[:, r % 2, k, :], onesColF[:],
                                  start=False, stop=(r == ntile - 1 and k == NK - 1))
                    te_sqslot[r % 2] = b.inc("tensor", TE)
                b.wait("vector", TE, TE.n)
                VR.memset(colw[:, :], 0.0)
                VR.tensor_copy(stat[:, 0:nt], pC[:, 0:nt])
                VR.tensor_copy(stat[:, 8:8 + nt], pC[:, 8:8 + nt])
                VR.tensor_scalar(colw[:, 0:nt], stat[:, 0:nt], 1.0 / D, None, ALU.mult)
                VR.tensor_mul(colw[:, 8:8 + nt], colw[:, 0:nt], colw[:, 0:nt])
                VR.scalar_tensor_tensor(colw[:, 8:8 + nt], stat[:, 8:8 + nt], 1.0 / D,
                                        colw[:, 8:8 + nt], ALU.mult, ALU.subtract)
                v = b.inc("vector", VE)
                b.wait("scalar", VE, v)
                SC.activation(colw[:, 8:8 + nt], colw[:, 8:8 + nt], AF.Sqrt,
                              bias=np.float32(1e-11), scale=np.float32(1.0 + 1e-6))
                s = b.inc("scalar", SE)
                b.wait("vector", SE, s)
                VR.reciprocal(colw[:, 8:8 + nt], colw[:, 8:8 + nt])
                VR.tensor_mul(colw[:, 0:nt], colw[:, 0:nt], colw[:, 8:8 + nt])
                VR.tensor_scalar(colw[:, 0:nt], colw[:, 0:nt], -1.0, None, ALU.mult)
                v = b.inc("vector", VE)
                b.wait("tensor", VE, v)
                TN.transpose(pC[0:16, 0:128], colw[:, 0:16], identF[:, :])
                t = b.inc("tensor", TE)
                b.wait("vector", TE, t)
                VR.tensor_copy(rowt[:, :], pC[0:16, 0:128])
                v = b.inc("vector", VE)
                b.wait("tensor", VE, v)
                for r in range(ntile):
                    TN.matmul(pA[:, r * 128:(r + 1) * 128],
                              sel16[:, (8 + r) * 128:(9 + r) * 128], rowt[:, :],
                              start=(r == 0), stop=(r == ntile - 1))
                    TN.matmul(pB[:, r * 128:(r + 1) * 128],
                              sel16[:, r * 128:(r + 1) * 128], rowt[:, :],
                              start=(r == 0), stop=(r == ntile - 1))
                t = b.inc("tensor", TE)
                b.wait("vector", TE, t)
                for k in range(NK):
                    VR.scalar_tensor_tensor(hb[:, k, 0:T], xp[:, k, 0:T], ln1c[:, k:k + 1],
                                            pA[:, 0:T], ALU.mult, ALU.mult)
                    VR.scalar_tensor_tensor(hb[:, k, 0:T], pB[:, 0:T], ln1c[:, k:k + 1],
                                            hb[:, k, 0:T], ALU.mult, ALU.add)
                v_hb = b.inc("vector", VE)
                b.wait("tensor", VE, v_hb)

                for wi, (wname, convT, is_q, is_k) in enumerate(
                        (("wq", cqT, True, False), ("wk", ckT, False, True),
                         ("wv", cvT, False, False))):
                    wap = ap[wname]
                    is_v = not (is_q or is_k)
                    if not halo:
                        VR.tensor_copy(pc[:, :, 0:3], carryW[:, wi])
                        b.inc("vector", VE)
                        if DBG_OUT and ptag == 1 and wi == 1:
                            b.wait("scalar", VE, VE.n)
                            b.dma("scalar", scr["pcdbg"].ap(), pc[:, :, :], ST)
                            b.wait("scalar", ST)
                    for grp in range(4):
                        for k in range(NK):
                            ksl = k % 2
                            if te_wslot[ksl]:
                                b.wait("gpsimd", TE, te_wslot[ksl])
                            b.dma("gpsimd", wbuf[:, ksl, :],
                                  wap[k * 128:(k + 1) * 128, grp * 512:(grp + 1) * 512],
                                  WSL[ksl])
                            b.wait("tensor", WSL[ksl], WSL[ksl].n)
                            for mm_ in range(4):
                                TN.matmul(pP[:, mm_ * 512:mm_ * 512 + T],
                                          wbuf[:, ksl, mm_ * 128:(mm_ + 1) * 128],
                                          hb[:, k, 0:T], start=(k == 0), stop=(k == NK - 1))
                            te_wslot[ksl] = b.inc("tensor", TE)
                        b.wait("vector", TE, TE.n)
                        for mm_ in range(4):
                            m = grp * 4 + mm_
                            VR.tensor_copy(pc[:, m, 3:3 + T], pP[:, mm_ * 512:mm_ * 512 + T])
                        ve_pc = b.inc("vector", VE)
                        b.wait("tensor", VE, ve_pc)
                    if halo:
                        VR.tensor_scalar(carryW[:, wi], pc[:, :, T:T + 3],
                                         halo_m[:, 0:1], None, ALU.mult)
                        b.inc("vector", VE)
                        continue
                    VR.tensor_copy(carryW[:, wi], pc[:, :, T:T + 3])
                    b.inc("vector", VE)
                    for m in range(NK):
                        VR.tensor_scalar(cv[:, 0:T], pc[:, m, 3:3 + T],
                                         convT[:, 4 * m + 3:4 * m + 4], None, ALU.mult)
                        VR.scalar_tensor_tensor(cv[:, 0:T], pc[:, m, 2:2 + T],
                                                convT[:, 4 * m + 2:4 * m + 3], cv[:, 0:T],
                                                ALU.mult, ALU.add)
                        VR.scalar_tensor_tensor(cv[:, 0:T], pc[:, m, 1:1 + T],
                                                convT[:, 4 * m + 1:4 * m + 2], cv[:, 0:T],
                                                ALU.mult, ALU.add)
                        VR.scalar_tensor_tensor(cv[:, 0:T], pc[:, m, 0:T],
                                                convT[:, 4 * m:4 * m + 1], cv[:, 0:T],
                                                ALU.mult, ALU.add)
                        v = b.inc("vector", VE)
                        b.wait("scalar", VE, v)
                        SC.activation(qs[:, 0:T], cv[:, 0:T], AF.Silu)
                        s_si = b.inc("scalar", SE)
                        if not is_v:
                            SC.activation(sqq[:, 0:T], qs[:, 0:T], AF.Square)
                            s_sq = b.inc("scalar", SE)
                            b.wait("tensor", SE, s_sq)
                            if ve_lcol:
                                b.wait("tensor", VE, ve_lcol)
                            for r in range(4):
                                TN.matmul(pC[:, 16 + r:17 + r], sqq[:, r * 128:(r + 1) * 128],
                                          onesColF[:], start=(r == 0), stop=(r == 3))
                            t = b.inc("tensor", TE)
                            b.wait("vector", TE, t)
                            VR.tensor_copy(lcol[:, 0:4], pC[:, 16:20])
                            ve_lcol = b.inc("vector", VE)
                            b.wait("scalar", VE, ve_lcol)
                            if is_q:
                                SC.activation(lcol[:, 0:4], lcol[:, 0:4], AF.Sqrt,
                                              bias=np.float32(128e-6), scale=np.float32(128.0))
                            else:
                                SC.activation(lcol[:, 0:4], lcol[:, 0:4], AF.Sqrt,
                                              bias=np.float32(1e-6))
                            s = b.inc("scalar", SE)
                            b.wait("vector", SE, s)
                            VR.reciprocal(lcol[:, 0:4], lcol[:, 0:4])
                            ve_lcol = b.inc("vector", VE)
                        b.wait("tensor", SE, s_si)
                        tsl = m % 2
                        for r in range(4):
                            TN.transpose(pTB[:, r * 128:(r + 1) * 128],
                                         qs[:, r * 128:(r + 1) * 128], identB[:, :])
                        t = b.inc("tensor", TE)
                        b.wait("vector", TE, t)
                        if tk_busy[tsl]:
                            b.wait("vector", TKS[tsl], tk_busy[tsl])
                        if not is_v:
                            for r in range(4):
                                VR.tensor_scalar(tkb[:, tsl, r, :], pTB[:, r * 128:(r + 1) * 128],
                                                 lcol[:, r:r + 1], None, ALU.mult)
                        else:
                            VR.tensor_copy(tkb[:, tsl].rearrange("p a c -> p (a c)"),
                                           pTB[:, 0:512])
                        v_tk = b.inc("vector", VE)
                        if is_k or is_v:
                            dst = scr["ktok" if is_k else "vtok"].ap()[m]
                            b.wait("scalar", VE, v_tk)
                            tk_busy[tsl] = b.dma(
                                "scalar",
                                dst[ptag * 512:(ptag + 1) * 512, :]
                                .rearrange("(rr t) d -> t rr d", t=128),
                                tkb[:, tsl], TKS[tsl])
                        if not is_v:
                            b.wait("tensor", VE, v_tk)
                            fsl = m % 2
                            for r in range(4):
                                TN.transpose(pTB[:, r * 128:(r + 1) * 128],
                                             tkb[:, tsl, r, :], identB[:, :])
                            t = b.inc("tensor", TE)
                            b.wait("vector", TE, t)
                            if fm_busy[fsl]:
                                b.wait("vector", FMS[fsl], fm_busy[fsl])
                            VR.tensor_copy(fmb[:, fsl, :], pTB[:, 0:512])
                            v_fm = b.inc("vector", VE)
                            b.wait("scalar", VE, v_fm)
                            dstv = qfmv if is_q else kfmv
                            fm_busy[fsl] = b.dma(
                                "scalar", dstv[:, m, ptag * 512:(ptag + 1) * 512],
                                fmb[:, fsl, :], FMS[fsl])
                if halo:
                    te_xp = TE.n
                    ve_xp = VE.n
                    continue

                b.wait("tensor", VE, VE.n)
                for k in range(NK):
                    TN.matmul(pA[:, 0:T], wfa_s[:, k, :], hb[:, k, 0:T],
                              start=(k == 0), stop=(k == NK - 1))
                t = b.inc("tensor", TE)
                b.wait("vector", TE, t)
                VR.tensor_copy(hfa[:, 0:T], pA[:, 0:T])
                v = b.inc("vector", VE)
                b.wait("tensor", VE, v)
                TN.matmul(pB[0:16, 0:T], wfb_s[:, :], hfa[:, 0:T], start=True, stop=True)
                for k in range(NK):
                    TN.matmul(pC[0:16, 0:T], wb_s[:, k, :], hb[:, k, 0:T],
                              start=(k == 0), stop=(k == NK - 1))
                t = b.inc("tensor", TE)
                b.wait("scalar", TE, t)
                b.wait("vector", TE, t)
                VR.tensor_scalar(zw[:, 0, 0:T], pB[0:16, 0:T], acol[0:16, 1:2], None, ALU.add)
                v = b.inc("vector", VE)
                b.wait("scalar", VE, v)
                SC.activation(zw[:, 1, 0:T], zw[:, 0, 0:T], AF.Abs)
                SC.activation(zw[:, 2, 0:T], zw[:, 1, 0:T], AF.Exp, scale=np.float32(-1.0))
                SC.activation(zw[:, 2, 0:T], zw[:, 2, 0:T], AF.Ln, bias=np.float32(1.0))
                SC.activation(beta_res[:, ptag * 512:ptag * 512 + T], pC[0:16, 0:T], AF.Sigmoid)
                s = b.inc("scalar", SE)
                b.wait("vector", SE, s)
                VR.tensor_scalar_max(zw[:, 3, 0:T], zw[:, 0, 0:T], 0.0)
                VR.tensor_add(zw[:, 3, 0:T], zw[:, 3, 0:T], zw[:, 2, 0:T])
                VR.tensor_scalar(g_res[:, ptag * 512:ptag * 512 + T],
                                 zw[:, 3, 0:T], acol[0:16, 2:3], None, ALU.mult)
                v = b.inc("vector", VE)
                b.wait("tensor", VE, v)
                for k in range(NK):
                    TN.matmul(pA[:, 0:T], wga_s[:, k, :], hb[:, k, 0:T],
                              start=(k == 0), stop=(k == NK - 1))
                t = b.inc("tensor", TE)
                b.wait("vector", TE, t)
                VR.tensor_copy(hga_res[:, ptag * 512:ptag * 512 + T], pA[:, 0:T])
                b.inc("vector", VE)
                te_xp = TE.n
                ve_xp = VE.n
            b.barrier()

    def scan_phase(g_res, beta_res):
        with ExitStack() as ps:
            PE = ps.enter_context
            S = PE(nc.sbuf_tensor("S3", [128, H, DV], F32))
            S0b = PE(nc.sbuf_tensor("S0b3", [128, H, DV], BF16))
            qf = PE(nc.sbuf_tensor("qf3", [128, 2, H, CCH], BF16))
            kf = PE(nc.sbuf_tensor("kf3", [128, 2, H, CCH], BF16))
            kt = PE(nc.sbuf_tensor("kt3", [128, 2, H, DK], BF16))
            vt = PE(nc.sbuf_tensor("vt3", [128, 2, H, DV], BF16))
            Uv = PE(nc.sbuf_tensor("Uv3", [128, 2, H, DV], BF16))
            WiT = PE(nc.sbuf_tensor("WiT3", [128, 2, H, CCH], BF16))
            W2T = PE(nc.sbuf_tensor("W2T3", [128, 2, H, CCH], BF16))
            Qh = PE(nc.sbuf_tensor("Qh3", [128, 2, H, CCH], BF16))
            Kb = PE(nc.sbuf_tensor("Kb3", [128, 2, H, DK], BF16))
            Ua = PE(nc.sbuf_tensor("Ua3", [128, H, DV], BF16))
            ofmb = PE(nc.sbuf_tensor("ofmb3", [128, 2, H, CCH], BF16))
            gcc = PE(nc.sbuf_tensor("gcc3", [128, 2, 16], F32))
            DpT4 = PE(nc.sbuf_tensor("DpT4", [128, 512], BF16))
            DsL4 = PE(nc.sbuf_tensor("DsL4", [128, 512], BF16))
            Grow4 = PE(nc.sbuf_tensor("Grow4", [128, 512], BF16))
            MPOS4 = PE(nc.sbuf_tensor("MPOS4", [128, 512], BF16))
            MT4 = PE(nc.sbuf_tensor("MT4", [128, 512], BF16))
            A4 = PE(nc.sbuf_tensor("A4", [128, 512], BF16))
            W4 = PE(nc.sbuf_tensor("W4", [128, 512], BF16))
            V4 = PE(nc.sbuf_tensor("V4", [128, 512], BF16))
            G4 = PE(nc.sbuf_tensor("G4", [128, 512], BF16))
            Zv4 = PE(nc.sbuf_tensor("Zv4", [128, 512], BF16))
            Y4 = PE(nc.sbuf_tensor("Y4", [128, 512], BF16))
            ot4 = PE(nc.sbuf_tensor("ot4", [128, 512], BF16))
            cT = PE(nc.sbuf_tensor("cT3", [16, 128], F32))
            lnbT = PE(nc.sbuf_tensor("lnbT3", [16, 128], F32))
            cplusT = PE(nc.sbuf_tensor("cplusT3", [16, 128], F32))
            cT4 = PE(nc.sbuf_tensor("cT43", [16, 512], F32))
            msk4 = PE(nc.sbuf_tensor("msk43", [16, 512], F32))
            nmsk4 = PE(nc.sbuf_tensor("nmsk43", [16, 512], F32))
            cols = PE(nc.sbuf_tensor("cols3", [128, 64], F32))
            negcC = PE(nc.sbuf_tensor("negcC3", [128, 16], F32))
            ratC = PE(nc.sbuf_tensor("ratC3", [128, 16], F32))
            bgC = PE(nc.sbuf_tensor("bgC3", [128, 16], F32))
            gCcur = PE(nc.sbuf_tensor("gCcur3", [128, 16], F32))
            clast = PE(nc.sbuf_tensor("clast3", [1, 16], F32))
            s0buf = PE(nc.sbuf_tensor("s0buf3", [128, H, DV], F32))
            s1buf = PE(nc.sbuf_tensor("s1buf3", [128, H, DV], F32))
            pCR = PE(nc.psum_tensor("pCR3", [128, 512], F32))
            pMM = PE(nc.psum_tensor("pMM3", [128, 512], F32))
            pN1 = PE(nc.psum_tensor("pN13", [128, 512], F32))
            pN2 = PE(nc.psum_tensor("pN23", [128, 512], F32))
            pU = PE(nc.psum_tensor("pU3", [128, 512], F32))
            pO = PE(nc.psum_tensor("pO3", [128, 512], F32))
            pS = PE(nc.psum_tensor("pS3", [128, 512], F32))
            pOT = PE(nc.psum_tensor("pOT3", [128, 512], BF16))

            QKS = b.slot_sems("QKS3", 2)
            UVS = b.slot_sems("UVS3", 2)
            OFS = b.slot_sems("OFS3", 2)
            te_qk = [0, 0]
            wr_uv = [0, 0]
            se_uv = [0, 0]
            ve_of = [0, 0]

            ktokv = scr["ktok"].ap()
            vtokv = scr["vtok"].ap()

            def load_qkv(ch, sl):
                if te_qk[sl]:
                    b.wait("sync", TE, te_qk[sl])
                cs = slice(ch * CCH, (ch + 1) * CCH)
                b.dma("sync", qf[:, sl], qfmv[:, :, cs], QKS[sl])
                b.dma("sync", kf[:, sl], kfmv[:, :, cs], QKS[sl])
                b.dma("sync", kt[:, sl],
                      ktokv[:, cs, :].rearrange("h t d -> t h d"), QKS[sl])
                b.dma("sync", vt[:, sl],
                      vtokv[:, cs, :].rearrange("h t d -> t h d"), QKS[sl])

            def load_pass2(ch, sl):
                if wr_uv[sl]:
                    b.wait("sync", VE, wr_uv[sl])
                if te_qk[sl]:
                    b.wait("sync", TE, te_qk[sl])
                if uv_store[sl]:
                    b.wait("sync", UVS[sl], uv_store[sl])
                b.wait("sync", SE, se_uv[sl])
                b.dma("sync", Uv[:, sl], scr["uv_s"].ap()[ch].rearrange("p (h c) -> p h c", h=H), UVS[sl])
                b.dma("sync", WiT[:, sl], scr["wiT_s"].ap()[ch].rearrange("p (h c) -> p h c", h=H), UVS[sl])
                b.dma("sync", W2T[:, sl], scr["w2T_s"].ap()[ch].rearrange("p (h c) -> p h c", h=H), UVS[sl])
                b.dma("sync", Qh[:, sl], scr["qh_s"].ap()[ch].rearrange("p (h c) -> p h c", h=H), UVS[sl])
                b.dma("sync", Kb[:, sl], scr["kb_s"].ap()[ch].rearrange("p (h c) -> p h c", h=H), UVS[sl])
                b.dma("sync", gcc[:, sl], scr["gcc_s"].ap()[ch], UVS[sl])

            def sweep(pass2):
                for ch in range(NCH):
                    sl = ch % 2
                    if not pass2:
                        if ch == 0:
                            load_qkv(0, 0)
                        if ch + 1 < NCH:
                            load_qkv(ch + 1, (ch + 1) % 2)
                        b.wait("vector", QKS[sl], QKS[sl].n)
                        b.wait("tensor", QKS[sl], QKS[sl].n)
                        if uv_store[sl]:
                            b.wait("vector", UVS[sl], uv_store[sl])
                            b.wait("scalar", UVS[sl], uv_store[sl])
                    else:
                        if ch == 0:
                            load_pass2(0, 0)
                        if ch + 1 < NCH:
                            load_pass2(ch + 1, (ch + 1) % 2)
                        b.wait("tensor", UVS[sl], UVS[sl].n)
                        b.wait("vector", UVS[sl], UVS[sl].n)
                    cs = slice(ch * CCH, (ch + 1) * CCH)
                    if not pass2:
                        VR.tensor_tensor_scan(cT[:, :], ones16[:, 0:CCH],
                                              g_res[:, cs], 0.0, ALU.mult, ALU.add)
                        v = b.inc("vector", VE)
                        b.wait("scalar", VE, v)
                        SC.activation(lnbT[:, :], beta_res[:, cs], AF.Ln)
                        sln = b.inc("scalar", SE)
                        b.wait("vector", SE, sln)
                        VR.tensor_add(cplusT[:, :], cT[:, :], lnbT[:, :])
                        for j in range(4):
                            VR.tensor_copy(cT4[:, j * 128:(j + 1) * 128], cT[:, :])
                        v = b.inc("vector", VE)
                        b.wait("tensor", VE, v)
                        TN.transpose(pCR[0:128, 0:16], cT[:, :], identF[0:16, 0:16])
                        TN.transpose(pCR[0:128, 16:32], cplusT[:, :], identF[0:16, 0:16])
                        t = b.inc("tensor", TE)
                        b.wait("vector", TE, t)
                        VR.tensor_copy(cols[:, 0:32], pCR[:, 0:32])
                        v = b.inc("vector", VE)
                        b.wait("tensor", VE, v)
                        TN.transpose(pCR[0:128, 32:48], beta_res[:, cs], identF[0:16, 0:16])
                        TN.transpose(pCR[0:1, 48:64], cT[:, 127:128], identF[0:16, 0:16])
                        t = b.inc("tensor", TE)
                        b.wait("vector", TE, t)
                        VR.tensor_copy(cols[:, 32:48], pCR[:, 32:48])
                        VR.tensor_copy(clast[:, :], pCR[0:1, 48:64])
                        VR.tensor_scalar(negcC[:, :], cols[:, 0:16], -1.0, None, ALU.mult)
                        v = b.inc("vector", VE)
                        b.wait("tensor", VE, v)
                        TN.matmul(pCR[:, 0:16], onesRowF[:], clast[:, :], start=True, stop=True)
                        t = b.inc("tensor", TE)
                        b.wait("vector", TE, t)
                        VR.tensor_sub(cols[:, 48:64], pCR[:, 0:16], cols[:, 0:16])
                        v = b.inc("vector", VE)
                        b.wait("scalar", VE, v)
                        SC.activation(ratC[:, :], cols[:, 48:64], AF.Exp)
                        SC.activation(gCcur[:, :], pCR[:, 0:16], AF.Exp)
                        SC.activation(bgC[:, :], cols[:, 16:32], AF.Exp)
                        sexp = b.inc("scalar", SE)
                        b.wait("vector", SE, sexp)
                        b.wait("tensor", SE, sexp)
                        VR.tensor_copy(gcc[:, sl], gCcur[:, :])
                        v = b.inc("vector", VE)
                        b.wait("scalar", VE, v)
                        b.dma("scalar", scr["gcc_s"].ap()[ch], gcc[:, sl], UVS[sl])
                    else:
                        pass

                    if DBG_OUT and pass2 and ch == 0 and "scomb_dbg" in DBG_OUT:
                        b.wait("scalar", VE, VE.n)
                        b.wait("scalar", UVS[sl], UVS[sl].n)
                        pd = scr["pcdbg"].ap().rearrange("p a c -> p (a c)")
                        b.dma("scalar", pd[:, 0:2048],
                              S0b[:, :, :].rearrange("p a c -> p (a c)"), ST)
                        b.dma("scalar", pd[:, 2048:4096],
                              Uv[:, sl].rearrange("p a c -> p (a c)"), ST)
                        b.dma("scalar", pd[:, 4096:6144],
                              WiT[:, sl].rearrange("p a c -> p (a c)"), ST)
                        b.dma("scalar", pd[:, 6144:8192],
                              W2T[:, sl].rearrange("p a c -> p (a c)"), ST)
                        for e_ in ENGINES:
                            b.wait(e_, ST)
                    for g in range(4):
                        hs = slice(4 * g, 4 * g + 4)
                        if not pass2:
                            VR.tensor_mul(msk4[:, :], cT4[:, :], selm4[:, g, :])
                            VR.tensor_scalar(nmsk4[:, :], msk4[:, :], -1.0, None, ALU.mult)
                            v = b.inc("vector", VE)
                            b.wait("tensor", VE, v)
                            TN.matmul(pCR[:, 0:512], ones16[:, 0:128], msk4[:, :],
                                      start=True, stop=False)
                            t = b.inc("tensor", TE)
                            b.wait("scalar", TE, t)
                            SC.activation(Grow4[:, :], pCR[:, 0:512], AF.Exp)
                            sg_ = b.inc("scalar", SE)
                            b.wait("tensor", SE, sg_)
                            TN.matmul(pCR[:, 0:512], mUIT[:, :], Ib4[:, :],
                                      start=False, stop=True)
                            t = b.inc("tensor", TE)
                            b.wait("scalar", TE, t)
                            for j in range(4):
                                SC.activation(DpT4[:, j * 128:(j + 1) * 128],
                                              pCR[:, j * 128:(j + 1) * 128], AF.Exp,
                                              bias=negcC[:, 4 * g + j:4 * g + j + 1])
                            s_ = b.inc("scalar", SE)
                            b.wait("tensor", SE, s_)
                            TN.matmul(pN1[:, 0:512], ones16[:, 0:128], nmsk4[:, :],
                                      start=True, stop=False)
                            TN.matmul(pN1[:, 0:512], mLST[:, :], Ib4[:, :],
                                      start=False, stop=True)
                            t = b.inc("tensor", TE)
                            b.wait("scalar", TE, t)
                            for j in range(4):
                                SC.activation(DsL4[:, j * 128:(j + 1) * 128],
                                              pN1[:, j * 128:(j + 1) * 128], AF.Exp,
                                              bias=cols[:, 16 + 4 * g + j:17 + 4 * g + j])
                            s_ = b.inc("scalar", SE)
                            b.wait("tensor", SE, s_)
                            for j in range(4):
                                TN.matmul(pMM[:, j * 128:(j + 1) * 128],
                                          kf[:, sl, 4 * g + j, :], kf[:, sl, 4 * g + j, :],
                                          start=(j == 0), stop=(j == 3))
                            t = b.inc("tensor", TE)
                            b.wait("vector", TE, t)
                            VR.tensor_mul(MPOS4[:, :], DsL4[:, :], pMM[:, 0:512])
                            v = b.inc("vector", VE)
                            b.wait("tensor", VE, v)
                            for j in range(4):
                                TN.transpose(pOT[:, j * 128:(j + 1) * 128],
                                             MPOS4[:, j * 128:(j + 1) * 128], identB[:, :])
                            t = b.inc("tensor", TE)
                            b.wait("scalar", TE, t)
                            SC.copy(MT4[:, :], pOT[:, 0:512])
                            s_ = b.inc("scalar", SE)
                            b.wait("vector", SE, s_)
                            VR.tensor_add(A4[:, :], MPOS4[:, :], Ib4[:, :])
                            VR.scalar_tensor_tensor(W4[:, :], MT4[:, :], -1.0, Ib4[:, :],
                                                    ALU.mult, ALU.add)
                            VR.scalar_tensor_tensor(V4[:, :], MPOS4[:, :], -1.0, Ib4[:, :],
                                                    ALU.mult, ALU.add)
                            v = b.inc("vector", VE)
                            b.wait("tensor", VE, v)
                            for j in range(4):
                                TN.matmul(pMM[:, j * 128:(j + 1) * 128],
                                          kf[:, sl, 4 * g + j, :], qf[:, sl, 4 * g + j, :],
                                          start=(j == 0), stop=(j == 3))
                            t = b.inc("tensor", TE)
                            b.wait("vector", TE, t)
                            VR.tensor_mul(W2T[:, sl, hs, :].rearrange("p a c -> p (a c)"),
                                          DpT4[:, :], pMM[:, 0:512])
                            VR.tensor_mul(Qh[:, sl, hs, :].rearrange("p a c -> p (a c)"),
                                          qf[:, sl, hs, :].rearrange("p a c -> p (a c)"),
                                          Grow4[:, :])
                            for j in range(4):
                                VR.tensor_scalar(Zv4[:, j * 128:(j + 1) * 128],
                                                 vt[:, sl, 4 * g + j, :],
                                                 cols[:, 32 + 4 * g + j:33 + 4 * g + j],
                                                 None, ALU.mult)
                                VR.tensor_scalar(Y4[:, j * 128:(j + 1) * 128],
                                                 kt[:, sl, 4 * g + j, :],
                                                 bgC[:, 4 * g + j:4 * g + j + 1],
                                                 None, ALU.mult)
                                VR.tensor_scalar(Kb[:, sl, 4 * g + j, :],
                                                 kt[:, sl, 4 * g + j, :],
                                                 ratC[:, 4 * g + j:4 * g + j + 1],
                                                 None, ALU.mult)
                            v = b.inc("vector", VE)
                            b.wait("tensor", VE, v)
                            for it_ in range(6):
                                for j in range(4):
                                    TN.matmul(pN1[:, j * 128:(j + 1) * 128],
                                              A4[:, j * 128:(j + 1) * 128],
                                              W4[:, j * 128:(j + 1) * 128],
                                              start=(j == 0), stop=(j == 3))
                                t = b.inc("tensor", TE)
                                b.wait("vector", TE, t)
                                VR.scalar_tensor_tensor(G4[:, :], pN1[:, 0:512], -1.0,
                                                        twoIb4[:, :], ALU.mult, ALU.add)
                                v = b.inc("vector", VE)
                                b.wait("tensor", VE, v)
                                for j in range(4):
                                    TN.matmul(pN2[:, j * 128:(j + 1) * 128],
                                              V4[:, j * 128:(j + 1) * 128],
                                              G4[:, j * 128:(j + 1) * 128],
                                              start=(j == 0), stop=(j == 3))
                                for j in range(4):
                                    TN.matmul(pN1[:, j * 128:(j + 1) * 128],
                                              G4[:, j * 128:(j + 1) * 128],
                                              V4[:, j * 128:(j + 1) * 128],
                                              start=(j == 0), stop=(j == 3))
                                t = b.inc("tensor", TE)
                                b.wait("vector", TE, t)
                                b.wait("scalar", TE, t)
                                SC.copy(W4[:, :], pN2[:, 0:512])
                                s_ = b.inc("scalar", SE)
                                if it_ < 5:
                                    VR.tensor_copy(V4[:, :], pN1[:, 0:512])
                                    v = b.inc("vector", VE)
                                    b.wait("tensor", VE, v)
                                b.wait("tensor", SE, s_)
                                b.wait("vector", SE, s_)
                            for j in range(4):
                                TN.matmul(pN2[:, j * 128:(j + 1) * 128],
                                          W4[:, j * 128:(j + 1) * 128],
                                          Zv4[:, j * 128:(j + 1) * 128],
                                          start=(j == 0), stop=(j == 3))
                            for j in range(4):
                                TN.matmul(pN1[:, j * 128:(j + 1) * 128],
                                          Y4[:, j * 128:(j + 1) * 128],
                                          W4[:, j * 128:(j + 1) * 128],
                                          start=(j == 0), stop=(j == 3))
                            t = b.inc("tensor", TE)
                            b.wait("vector", TE, t)
                            b.wait("scalar", TE, t)
                            VR.tensor_copy(Uv[:, sl, hs, :].rearrange("p a c -> p (a c)"),
                                           pN2[:, 0:512])
                            SC.copy(WiT[:, sl, hs, :].rearrange("p a c -> p (a c)"),
                                    pN1[:, 0:512])
                            v = b.inc("vector", VE)
                            s_ = b.inc("scalar", SE)
                            b.wait("tensor", VE, v)
                            b.wait("tensor", SE, s_)
                        for j in range(4):
                            TN.matmul(pU[:, j * 128:(j + 1) * 128],
                                      WiT[:, sl, 4 * g + j, :], S0b[:, 4 * g + j, :],
                                      start=(j == 0), stop=(j == 3))
                        t = b.inc("tensor", TE)
                        b.wait("vector", TE, t)
                        VR.scalar_tensor_tensor(Ua[:, hs, :].rearrange("p a c -> p (a c)"),
                                                pU[:, 0:512], -1.0,
                                                Uv[:, sl, hs, :].rearrange("p a c -> p (a c)"),
                                                ALU.mult, ALU.add)
                        v = b.inc("vector", VE)
                        b.wait("tensor", VE, v)
                        for j in range(4):
                            TN.matmul(pO[:, j * 128:(j + 1) * 128],
                                      Qh[:, sl, 4 * g + j, :], S0b[:, 4 * g + j, :],
                                      start=(j == 0), stop=False)
                            TN.matmul(pO[:, j * 128:(j + 1) * 128],
                                      W2T[:, sl, 4 * g + j, :], Ua[:, 4 * g + j, :],
                                      start=False, stop=(j == 3))
                        t = b.inc("tensor", TE)
                        b.wait("vector", TE, t)
                        VR.tensor_copy(ot4[:, :], pO[:, 0:512])
                        v = b.inc("vector", VE)
                        b.wait("tensor", VE, v)
                        for j in range(4):
                            TN.transpose(pOT[:, j * 128:(j + 1) * 128],
                                         ot4[:, j * 128:(j + 1) * 128], identB[:, :])
                        for j in range(4):
                            TN.matmul(pS[:, j * 128:(j + 1) * 128],
                                      Kb[:, sl, 4 * g + j, :], Ua[:, 4 * g + j, :],
                                      start=(j == 0), stop=(j == 3))
                        t = b.inc("tensor", TE)
                        b.wait("scalar", TE, t)
                        b.wait("vector", TE, t)
                        if ve_of[sl] and g == 0:
                            b.wait("scalar", OFS[sl], ve_of[sl])
                        SC.copy(ofmb[:, sl, hs, :].rearrange("p a c -> p (a c)"),
                                pOT[:, 0:512])
                        s_ = b.inc("scalar", SE)
                        for j in range(4):
                            VR.tensor_scalar(S[:, 4 * g + j, :], S[:, 4 * g + j, :],
                                             gCcur[:, 4 * g + j:4 * g + j + 1] if not pass2
                                             else gcc[:, sl, 4 * g + j:4 * g + j + 1],
                                             None, ALU.mult)
                        VR.tensor_add(S[:, hs, :].rearrange("p a c -> p (a c)"),
                                      S[:, hs, :].rearrange("p a c -> p (a c)"),
                                      pS[:, 0:512])
                        v = b.inc("vector", VE)
                        b.wait("tensor", VE, v)
                        b.wait("scalar", VE, v)
                    if DBG_OUT and pass2 and ch == 0 and g == 3:
                        b.wait("scalar", VE, VE.n)
                        b.dma("scalar", scr["pcdbg"].ap().rearrange("p a c -> p (a c)")[:, 0:2048]
                              .rearrange("p (a c) -> p a c", a=16),
                              Ua[:, :, :], ST)
                        b.dma("scalar", scr["pcdbg"].ap().rearrange("p a c -> p (a c)")[:, 2048:4096]
                              .rearrange("p (a c) -> p a c", a=16),
                              S0b[:, :, :], ST)
                        b.wait("scalar", ST)
                    VR.tensor_copy(S0b[:, :, :], S[:, :, :])
                    v_s0 = b.inc("vector", VE)
                    b.wait("tensor", VE, v_s0)
                    b.wait("scalar", SE, SE.n)
                    ve_of[sl] = b.dma("scalar", ofmv[:, :, cs], ofmb[:, sl], OFS[sl])
                    if not pass2:
                        b.wait("scalar", VE, VE.n)
                        b.wait("scalar", SE, SE.n)
                        for nm, tns in (("uv_s", Uv), ("wiT_s", WiT), ("w2T_s", W2T),
                                        ("qh_s", Qh), ("kb_s", Kb)):
                            b.dma("scalar", scr[nm].ap()[ch],
                                  tns[:, sl].rearrange("p a c -> p (a c)"), UVS[sl])
                        uv_store[sl] = UVS[sl].n
                        wr_uv[sl] = VE.n
                        te_qk[sl] = TE.n
                        se_uv[sl] = SE.n
                    else:
                        wr_uv[sl] = VE.n
                        te_qk[sl] = TE.n
                        se_uv[sl] = SE.n

            uv_store = [0, 0]
            VR.memset(S[:, :, :], 0.0)
            v0 = b.inc("vector", VE)
            b.wait("vector", VE, v0)
            VR.tensor_copy(S0b[:, :, :], S[:, :, :])
            b.inc("vector", VE)
            b.wait("tensor", VE, VE.n)
            b.wait("scalar", VE, VE.n)
            sweep(False)
            b.wait("scalar", VE, VE.n)
            b.dma("scalar", scr["cc_in"].ap(), S[:, :, :].rearrange("p a c -> p (a c)"), ST)
            b.wait("gpsimd", ST)
            GP.collective_compute(
                "AllGather", ALU.bypass,
                replica_groups=[[0, 1], [2, 3], [4, 5], [6, 7]],
                ins=[scr["cc_in"].ap()], outs=[scr["cc_out"].ap()],
            )
            b.inc("gpsimd", b.sems["CC"])
            b.wait("gpsimd", b.sems["CC"], b.sems["CC"].n)
            b.dma("gpsimd", s0buf[:, :, :],
                  scr["cc_out"].ap()[0:128].rearrange("p (a c) -> p a c", a=H), CA)
            b.dma("gpsimd", s1buf[:, :, :],
                  scr["cc_out"].ap()[128:256].rearrange("p (a c) -> p a c", a=H), CA)
            b.wait("vector", CA)
            b.wait("vector", OFS[0]); b.wait("vector", OFS[1])
            b.wait("vector", UVS[0]); b.wait("vector", UVS[1])
            VR.tensor_scalar(S[:, :, :], s0buf[:, :, :], mA[:, 0:1], None, ALU.mult)
            VR.tensor_scalar(s1buf[:, :, :], s1buf[:, :, :], mB[:, 0:1], None, ALU.mult)
            VR.tensor_add(S[:, :, :], S[:, :, :], s1buf[:, :, :])
            VR.tensor_copy(S0b[:, :, :], S[:, :, :])
            v = b.inc("vector", VE)
            b.wait("tensor", VE, v)
            b.wait("scalar", VE, v)
            if DBG_OUT:
                b.dma("scalar", scr["scomb_dbg"].ap(),
                      S[:, :, :].rearrange("p a c -> p (a c)"), ST)
                b.wait("scalar", ST)
            if not P3_SWEEP1_ONLY:
                sweep(True)
            b.barrier()

    def owo_phase(hga_res):
        with ExitStack() as ps:
            PE = ps.enter_context
            op_ = PE(nc.sbuf_tensor("op4", [128, NK, 512], BF16))
            x1p = PE(nc.sbuf_tensor("x1p4", [128, NK, 512], F32))
            ap_ = PE(nc.sbuf_tensor("ap4", [128, NK, 512], F32))
            sq4 = PE(nc.sbuf_tensor("sq4", [128, 2, NK, 128], F32))
            ogb = PE(nc.sbuf_tensor("ogb4", [128, NK, 512], BF16))
            sgall = PE(nc.sbuf_tensor("sgall4", [128, NK, 512], BF16))
            sqo = PE(nc.sbuf_tensor("sqo4", [128, 4, 512], F32))
            wgb_s = PE(nc.sbuf_tensor("wgb_s4", [128, D], BF16))
            colw64 = PE(nc.sbuf_tensor("colw644", [128, 64], F32))
            lrow4 = PE(nc.sbuf_tensor("lrow44", [4, 2, 128], F32))
            onw4 = PE(nc.sbuf_tensor("onw44", [4, 4, 128], F32))
            wob = PE(nc.sbuf_tensor("wob4", [128, 2, 512], BF16))
            stat = PE(nc.sbuf_tensor("stat4", [128, 32], F32))
            colw = PE(nc.sbuf_tensor("colw4", [128, 16], F32))
            rowt = PE(nc.sbuf_tensor("rowt4", [16, 128], F32))
            pA = PE(nc.psum_tensor("pA4", [128, 512], F32))
            pB = PE(nc.psum_tensor("pB4", [128, 512], F32))
            pC = PE(nc.psum_tensor("pC4", [128, 512], F32))
            pD = PE(nc.psum_tensor("pD4", [128, 512], F32))
            pE = PE(nc.psum_tensor("pE4", [128, 2048], F32))

            WOS = b.slot_sems("WOS4", 2)
            te_wo = [0, 0]
            te_sq4 = [0, 0]
            b.dma("gpsimd", wgb_s[:, :], ap["wgb"], CA)
            b.dma("sync", onw4[:, :, :], ap["onw4"], LD)
            b.wait("tensor", CA); b.wait("tensor", LD)
            se_g = {}
            ve_og = {}
            te_sqo = [0, 0, 0, 0]

            st_prev = 0
            for ptag in range(4):
                T = 512
                ntile = nt = 4
                b.wait("vector", ST, st_prev)
                b.wait("scalar", ST, st_prev)
                b.wait("sync", TE, TE.n); b.wait("sync", VE, VE.n)
                cs = slice(ptag * 512, (ptag + 1) * 512)
                b.dma("sync", op_[:, :, :], ofmv[:, :, cs], LD)
                b.dma("sync", x1p[:, :, :], x1v[:, :, THALO + ptag * 512:THALO + (ptag + 1) * 512], LD)
                b.wait("scalar", LD); b.wait("tensor", LD); b.wait("vector", LD)
                b.wait("tensor", SE, SE.n)
                b.wait("tensor", VE, VE.n)
                for r in range(ntile):
                    if te_sq4[r % 2]:
                        b.wait("scalar", TE, te_sq4[r % 2])
                    SC.activation(sq4[:, r % 2, :, :], x1p[:, :, r * 128:(r + 1) * 128], AF.Square)
                    s = b.inc("scalar", SE)
                    b.wait("tensor", SE, s)
                    for k in range(NK):
                        TN.matmul(pC[:, r:r + 1], x1p[:, k, r * 128:(r + 1) * 128],
                                  onesColF[:], start=(r == 0 and k == 0), stop=False)
                        TN.matmul(pC[:, 8 + r:9 + r], sq4[:, r % 2, k, :], onesColF[:],
                                  start=False, stop=(r == ntile - 1 and k == NK - 1))
                    te_sq4[r % 2] = b.inc("tensor", TE)
                b.wait("vector", TE, TE.n)
                VR.memset(colw[:, :], 0.0)
                VR.tensor_copy(stat[:, 0:nt], pC[:, 0:nt])
                VR.tensor_copy(stat[:, 8:8 + nt], pC[:, 8:8 + nt])
                VR.tensor_scalar(colw[:, 0:nt], stat[:, 0:nt], 1.0 / D, None, ALU.mult)
                VR.tensor_mul(colw[:, 8:8 + nt], colw[:, 0:nt], colw[:, 0:nt])
                VR.scalar_tensor_tensor(colw[:, 8:8 + nt], stat[:, 8:8 + nt], 1.0 / D,
                                        colw[:, 8:8 + nt], ALU.mult, ALU.subtract)
                v = b.inc("vector", VE)
                b.wait("scalar", VE, v)
                SC.activation(colw[:, 8:8 + nt], colw[:, 8:8 + nt], AF.Sqrt, bias=np.float32(1e-5))
                s = b.inc("scalar", SE)
                b.wait("vector", SE, s)
                VR.reciprocal(colw[:, 8:8 + nt], colw[:, 8:8 + nt])
                VR.tensor_mul(colw[:, 0:nt], colw[:, 0:nt], colw[:, 8:8 + nt])
                VR.tensor_scalar(colw[:, 0:nt], colw[:, 0:nt], -1.0, None, ALU.mult)
                v = b.inc("vector", VE)
                b.wait("tensor", VE, v)
                TN.transpose(pC[0:16, 0:128], colw[:, 0:16], identF[:, :])
                t = b.inc("tensor", TE)
                b.wait("vector", TE, t)
                VR.tensor_copy(rowt[:, :], pC[0:16, 0:128])
                v = b.inc("vector", VE)
                b.wait("tensor", VE, v)
                for r in range(ntile):
                    TN.matmul(pA[:, r * 128:(r + 1) * 128],
                              sel16[:, (8 + r) * 128:(9 + r) * 128], rowt[:, :],
                              start=(r == 0), stop=(r == ntile - 1))
                    TN.matmul(pB[:, r * 128:(r + 1) * 128],
                              sel16[:, r * 128:(r + 1) * 128], rowt[:, :],
                              start=(r == 0), stop=(r == ntile - 1))
                t = b.inc("tensor", TE)
                b.wait("vector", TE, t)
                for k in range(NK):
                    VR.tensor_mul(ap_[:, k, :], x1p[:, k, :], pA[:, 0:512])
                    VR.tensor_add(ap_[:, k, :], ap_[:, k, :], pB[:, 0:512])
                v_a = b.inc("vector", VE)
                b.wait("tensor", VE, v_a)
                for m in range(NK):
                    pX = pA if m % 2 == 0 else pB
                    if m >= 2:
                        b.wait("tensor", SE, se_g[m - 2])
                    TN.matmul(pX[:, 0:512], wgb_s[:, m * 128:(m + 1) * 128],
                              hga_res[:, cs], start=True, stop=True)
                    t = b.inc("tensor", TE)
                    b.wait("scalar", TE, t)
                    SC.activation(sgall[:, m, :], pX[:, 0:512], AF.Silu)
                    se_g[m] = b.inc("scalar", SE)
                for m in range(NK):
                    if m >= 4 and te_sqo[m % 4]:
                        b.wait("scalar", TE, te_sqo[m % 4])
                    SC.activation(sqo[:, m % 4, :], op_[:, m, :], AF.Square)
                    sse = b.inc("scalar", SE)
                    b.wait("tensor", SE, sse)
                    for r in range(4):
                        TN.matmul(pC[:, 4 * m + r:4 * m + r + 1],
                                  sqo[:, m % 4, r * 128:(r + 1) * 128], onesColF[:],
                                  start=(m == 0 and r == 0),
                                  stop=(m == NK - 1 and r == 3))
                    te_rms = te_sqo[m % 4] = b.inc("tensor", TE)
                b.wait("vector", TE, te_rms)
                VR.tensor_copy(colw64[:, :], pC[:, 0:64])
                v = b.inc("vector", VE)
                b.wait("scalar", VE, v)
                SC.activation(colw64[:, :], colw64[:, :], AF.Sqrt,
                              bias=np.float32(1e-6), scale=np.float32(1.0 / 128.0))
                srms = b.inc("scalar", SE)
                b.wait("vector", SE, srms)
                VR.reciprocal(colw64[:, :], colw64[:, :])
                v = b.inc("vector", VE)
                b.wait("tensor", VE, v)
                b.wait("tensor", SE, SE.n)
                for m in range(NK):
                    pX = pA if m % 2 == 0 else pB
                    TN.transpose(pD[0:4, (m % 2) * 128:(m % 2) * 128 + 128],
                                 colw64[:, 4 * m:4 * m + 4], identF[:, :])
                    t = b.inc("tensor", TE)
                    b.wait("vector", TE, t)
                    VR.tensor_copy(lrow4[:, m % 2, :], pD[0:4, (m % 2) * 128:(m % 2) * 128 + 128])
                    vlr = b.inc("vector", VE)
                    b.wait("tensor", VE, vlr)
                    if m >= 2:
                        b.wait("tensor", VE, ve_og[m - 2])
                    for r in range(4):
                        TN.matmul(pX[:, r * 128:(r + 1) * 128],
                                  onw4[:, r, :], lrow4[:, m % 2, :],
                                  start=(r == 0), stop=(r == 3))
                    t = b.inc("tensor", TE)
                    b.wait("vector", TE, t)
                    VR.tensor_mul(ogb[:, m, :], op_[:, m, :], pX[:, 0:512])
                    VR.tensor_mul(ogb[:, m, :], ogb[:, m, :], sgall[:, m, :])
                    ve_og[m] = b.inc("vector", VE)
                b.wait("tensor", VE, VE.n)
                for dq in range(4):
                    for k in range(NK):
                        ksl = k % 2
                        if te_wo[ksl]:
                            b.wait("gpsimd", TE, te_wo[ksl])
                        b.dma("gpsimd", wob[:, ksl, :],
                              ap["wo"][k * 128:(k + 1) * 128, dq * 512:(dq + 1) * 512],
                              WOS[ksl])
                        b.wait("tensor", WOS[ksl], WOS[ksl].n)
                        for dd in range(4):
                            TN.matmul(pE[:, dd * 512:dd * 512 + 512],
                                      wob[:, ksl, dd * 128:(dd + 1) * 128],
                                      ogb[:, k, :], start=(k == 0), stop=(k == NK - 1))
                        te_wo[ksl] = b.inc("tensor", TE)
                    b.wait("vector", TE, TE.n)
                    for dd in range(4):
                        d_ = dq * 4 + dd
                        VR.scalar_tensor_tensor(ap_[:, d_, :], pE[:, dd * 512:dd * 512 + 512],
                                                1.0, ap_[:, d_, :], ALU.mult, ALU.add)
                    ve_pe = b.inc("vector", VE)
                    b.wait("tensor", VE, ve_pe)
                b.wait("scalar", VE, VE.n)
                st_prev = b.dma("scalar", xiv[:, :, cs], ap_[:, :, :], ST)
            b.barrier()

    def mlp_up_phase():
        with ExitStack() as ps:
            PE = ps.enter_context
            h2f = PE(nc.sbuf_tensor("h2f5", [128, NK, TOWN], BF16))
            xip = PE(nc.sbuf_tensor("xip5", [128, NK, 512], F32))
            sq5 = PE(nc.sbuf_tensor("sq5", [128, 2, NK, 128], F32))
            wgb = PE(nc.sbuf_tensor("wgb5", [128, 2, NK, 128], BF16))
            wub = PE(nc.sbuf_tensor("wub5", [128, 2, NK, 128], BF16))
            sgl = PE(nc.sbuf_tensor("sgl5", [128, 2048], BF16))
            mb = PE(nc.sbuf_tensor("mb5", [128, 2, 2048], BF16))
            stat = PE(nc.sbuf_tensor("stat5", [128, 32], F32))
            colw = PE(nc.sbuf_tensor("colw5", [128, 16], F32))
            rowt = PE(nc.sbuf_tensor("rowt5", [16, 128], F32))
            ln2c = PE(nc.sbuf_tensor("ln2c5", [128, 16], F32))
            pG = PE(nc.psum_tensor("pG5", [128, 2048], F32))
            pU = PE(nc.psum_tensor("pU5", [128, 2048], F32))
            WG5 = b.slot_sems("WG5", 2)
            MS5 = b.slot_sems("MS5", 2)
            te_w5 = [0, 0]
            ve_m5 = [0, 0]
            b.dma("sync", ln2c[:, :], ap["ln2_wT"], LD)
            b.wait("vector", LD)
            for ptag in range(4):
                nt = 4
                b.wait("sync", TE, TE.n); b.wait("sync", VE, VE.n)
                b.dma("sync", xip[:, :, :], xiv[:, :, ptag * 512:(ptag + 1) * 512], LD)
                b.wait("scalar", LD); b.wait("tensor", LD); b.wait("vector", LD)
                b.wait("tensor", SE, SE.n); b.wait("tensor", VE, VE.n)
                te_sq5 = {}
                for r in range(4):
                    if r >= 2:
                        b.wait("scalar", TE, te_sq5[r - 2])
                    SC.activation(sq5[:, r % 2, :, :], xip[:, :, r * 128:(r + 1) * 128], AF.Square)
                    s = b.inc("scalar", SE)
                    b.wait("tensor", SE, s)
                    for k in range(NK):
                        TN.matmul(pG[:, 8 + r:9 + r], sq5[:, r % 2, k, :], onesColF[:],
                                  start=(r == 0 and k == 0),
                                  stop=(r == 3 and k == NK - 1))
                    te_sq5[r] = b.inc("tensor", TE)
                b.wait("vector", TE, TE.n)
                VR.memset(colw[:, :], 0.0)
                VR.tensor_copy(stat[:, 8:8 + nt], pG[:, 8:8 + nt])
                v = b.inc("vector", VE)
                b.wait("scalar", VE, v)
                SC.activation(colw[:, 8:8 + nt], stat[:, 8:8 + nt], AF.Sqrt,
                              bias=np.float32(1e-6), scale=np.float32(1.0 / D))
                s = b.inc("scalar", SE)
                b.wait("vector", SE, s)
                VR.reciprocal(colw[:, 8:8 + nt], colw[:, 8:8 + nt])
                v = b.inc("vector", VE)
                b.wait("tensor", VE, v)
                TN.transpose(pG[0:16, 0:128], colw[:, 0:16], identF[:, :])
                t = b.inc("tensor", TE)
                b.wait("vector", TE, t)
                VR.tensor_copy(rowt[:, :], pG[0:16, 0:128])
                v = b.inc("vector", VE)
                b.wait("tensor", VE, v)
                for r in range(4):
                    TN.matmul(pU[:, r * 128:(r + 1) * 128],
                              sel16[:, (8 + r) * 128:(9 + r) * 128], rowt[:, :],
                              start=(r == 0), stop=(r == 3))
                t = b.inc("tensor", TE)
                b.wait("vector", TE, t)
                for k in range(NK):
                    VR.scalar_tensor_tensor(h2f[:, k, ptag * 512:(ptag + 1) * 512],
                                            xip[:, k, :], ln2c[:, k:k + 1],
                                            pU[:, 0:512], ALU.mult, ALU.mult)
                v = b.inc("vector", VE)
                b.wait("tensor", VE, v)
            wgv = ap["w_gate"].rearrange("(k r) i -> r k i", r=128)
            wuv = ap["w_up"].rearrange("(k r) i -> r k i", r=128)
            for i in range(64):
                isl = i % 2
                if te_w5[isl]:
                    b.wait("gpsimd", TE, te_w5[isl])
                b.dma("gpsimd", wgb[:, isl], wgv[:, :, i * 128:(i + 1) * 128], WG5[isl])
                b.dma("gpsimd", wub[:, isl], wuv[:, :, i * 128:(i + 1) * 128], WG5[isl])
                b.wait("tensor", WG5[isl], WG5[isl].n)
                for tg in range(4):
                    for k in range(NK):
                        TN.matmul(pG[:, tg * 512:(tg + 1) * 512], wgb[:, isl, k, :],
                                  h2f[:, k, tg * 512:(tg + 1) * 512],
                                  start=(k == 0), stop=(k == NK - 1))
                for tg in range(4):
                    for k in range(NK):
                        TN.matmul(pU[:, tg * 512:(tg + 1) * 512], wub[:, isl, k, :],
                                  h2f[:, k, tg * 512:(tg + 1) * 512],
                                  start=(k == 0), stop=(k == NK - 1))
                te_w5[isl] = b.inc("tensor", TE)
                b.wait("scalar", TE, te_w5[isl])
                SC.activation(sgl[:, :], pG[:, 0:2048], AF.Silu)
                s = b.inc("scalar", SE)
                b.wait("vector", SE, s)
                if ve_m5[isl]:
                    b.wait("vector", MS5[isl], ve_m5[isl])
                VR.tensor_mul(mb[:, isl, :], sgl[:, :], pU[:, 0:2048])
                v = b.inc("vector", VE)
                b.wait("tensor", VE, v)
                b.wait("scalar", VE, v)
                ve_m5[isl] = b.dma("scalar", scr["mscr"].ap()[i], mb[:, isl, :], MS5[isl])
            b.barrier()

    def mlp_down_phase():
        with ExitStack() as ps:
            PE = ps.enter_context
            mb6 = PE(nc.sbuf_tensor("mb65", [128, 64, 1024], BF16))
            wdb = PE(nc.sbuf_tensor("wdb6", [128, 2, 32, 128], BF16))
            x1t = PE(nc.sbuf_tensor("x1t6", [128, 2, 1024], F32))
            x2b = PE(nc.sbuf_tensor("x2b6", [128, 2, 1024], F32))
            pD1 = PE(nc.psum_tensor("pD16", [128, 1024], F32))
            pD2 = PE(nc.psum_tensor("pD26", [128, 1024], F32))
            MT6 = b.slot_sems("MT6", 1)
            WD6 = b.slot_sems("WD6", 2)
            XT6 = b.slot_sems("XT6", 2)
            X26 = b.slot_sems("X26", 2)
            te_wd = [0, 0]
            ve_add = [0, 0]
            st_x2 = [0, 0]
            wdv = ap["w_down"].rearrange("(i r) d -> r i d", r=128)
            mscv = scr["mscr"].ap()
            for tg in range(2):
                b.wait("sync", TE, TE.n)
                b.dma("sync", mb6[:, :, :],
                      mscv[:, :, tg * 1024:(tg + 1) * 1024].rearrange("i p t -> p i t"),
                      MT6[0])
                b.wait("tensor", MT6[0], MT6[0].n)
                for d_ in range(16):
                    dsl = d_ % 2
                    pDx = pD1 if dsl == 0 else pD2
                    if ve_add[dsl]:
                        b.wait("gpsimd", VE, ve_add[dsl])
                    b.dma("gpsimd", x1t[:, dsl, :],
                          x1v[:, d_, THALO + tg * 1024:THALO + (tg + 1) * 1024], XT6[dsl])
                    b.wait("gpsimd", XT6[dsl], XT6[dsl].n)
                    GP.dma_start(out=x1t[:, dsl, :],
                                 in_=xiv[:, d_, tg * 1024:(tg + 1) * 1024],
                                 accum_op=ALU.add).then_inc(XT6[dsl].h, 16)
                    XT6[dsl].n += 16
                    for ih in range(2):
                        wsl = (2 * d_ + ih) % 2
                        if te_wd[wsl]:
                            b.wait("gpsimd", TE, te_wd[wsl])
                        b.dma("gpsimd", wdb[:, wsl],
                              wdv[:, ih * 32:(ih + 1) * 32, d_ * 128:(d_ + 1) * 128],
                              WD6[wsl])
                        b.wait("tensor", WD6[wsl], WD6[wsl].n)
                        if ih == 0 and ve_add[dsl]:
                            b.wait("tensor", VE, ve_add[dsl])
                        for i in range(32):
                            for gg_ in range(2):
                                TN.matmul(pDx[:, gg_ * 512:(gg_ + 1) * 512],
                                          wdb[:, wsl, i, :],
                                          mb6[:, ih * 32 + i, gg_ * 512:(gg_ + 1) * 512],
                                          start=(ih == 0 and i == 0),
                                          stop=(ih == 1 and i == 31))
                        te_wd[wsl] = b.inc("tensor", TE)
                    b.wait("vector", TE, te_wd[(2 * d_ + 1) % 2])
                    b.wait("vector", XT6[dsl], XT6[dsl].n)
                    if st_x2[dsl]:
                        b.wait("vector", X26[dsl], st_x2[dsl])
                    VR.tensor_add(x2b[:, dsl, :], x1t[:, dsl, :], pDx[:, 0:1024])
                    ve_add[dsl] = b.inc("vector", VE)
                    b.wait("scalar", VE, ve_add[dsl])
                    st_x2[dsl] = b.dma("scalar", x2v[:, d_, tg * 1024:(tg + 1) * 1024],
                                       x2b[:, dsl, :], X26[dsl])
            b.barrier()

    if "p1" in PHASES:
        ffn_phase("p1")
    if "p2" in PHASES or "p3" in PHASES or "p4" in PHASES:
        with ExitStack() as p24:
            g_res = p24.enter_context(nc.sbuf_tensor("g_res", [16, TOWN], F32))
            beta_res = p24.enter_context(nc.sbuf_tensor("beta_res", [16, TOWN], F32))
            hga_res = p24.enter_context(nc.sbuf_tensor("hga_res", [128, TOWN], BF16))
            if "p2" in PHASES:
                proj_phase(g_res, beta_res, hga_res)
            if "p3" in PHASES:
                scan_phase(g_res, beta_res)
            if "p4" in PHASES:
                owo_phase(hga_res)
                if DBG_OUT:
                    b.dma("scalar", scr["g_dbg"].ap(), g_res[:, :], ST)
                    b.dma("scalar", scr["beta_dbg"].ap(), beta_res[:, :], ST)
                    b.dma("scalar", scr["hga_dbg"].ap(), hga_res[:, :], ST)
                    b.wait("scalar", ST)
    if "p5" in PHASES:
        mlp_up_phase()
    if "p6" in PHASES:
        mlp_down_phase()
    if "p7" in PHASES:
        ffn_phase("p7")

    b.wait("gpsimd", ST)
    for name, out_ap in dbg_outs.items():
        b.dma("gpsimd", out_ap, scr[name].ap(), CA)
    b.wait("gpsimd", CA)
    b.wait("scalar", ST)

    b.finalize_hazards()
    with nc.Block() as block:
        block.sync(lambda e: b.r["sync"].replay(e))
        block.scalar(lambda e: b.r["scalar"].replay(e))
        block.tensor(lambda e: b.r["tensor"].replay(e))
        block.vector(lambda e: b.r["vector"].replay(e))
        block.gpsimd(lambda e: b.r["gpsimd"].replay(e))

    stack.close()
    return nc


def _consts():
    c = {}
    c["identF"] = np.eye(128, dtype=np.float32)
    c["identB"] = np.eye(128, dtype=np.float32).astype(ml_dtypes.bfloat16)
    c["onesColF"] = np.ones((128, 1), np.float32)
    c["onesRowF"] = np.ones((1, 128), np.float32)
    c["ones16"] = np.ones((16, CCH), np.float32)
    s = np.arange(128)[:, None]
    t = np.arange(128)[None, :]
    c["mUIT"] = np.where(t <= s, 0.0, -1e9).astype(ml_dtypes.bfloat16)
    c["mLST"] = np.where(s < t, 0.0, -1e9).astype(ml_dtypes.bfloat16)
    sm = np.zeros((16, 4, 512), np.float32)
    for g in range(4):
        for j in range(4):
            sm[4 * g + j, g, 128 * j:128 * (j + 1)] = 1.0
    c["selm4"] = sm
    c["Ib4"] = np.tile(np.eye(128, dtype=np.float32), (1, 4)).astype(ml_dtypes.bfloat16)
    c["twoIb4"] = np.tile(2.0 * np.eye(128, dtype=np.float32), (1, 4)).copy()
    e8 = np.zeros((8, 128), np.float32)
    for e in range(8):
        e8[e, 16 * e:16 * e + 16] = 1.0
    c["E8"] = e8.astype(ml_dtypes.bfloat16)
    c["sel16"] = np.kron(np.eye(16, dtype=np.float32), np.ones((1, 128), np.float32))
    return c


_NC_CACHE = {}


def kernel(**inputs):
    key = (tuple(PHASES), tuple(DBG_OUT))
    if key not in _NC_CACHE:
        _NC_CACHE[key] = build_program()
    nc = _NC_CACHE[key]

    x = np.ascontiguousarray(np.asarray(inputs["L0_attn_x"], np.float32))
    Bsz, T, _ = x.shape
    common = _consts()
    ren = {
        "wq": "L0_attn_wq", "wk": "L0_attn_wk", "wv": "L0_attn_wv", "wo": "L0_attn_wo",
        "wfa": "L0_attn_wfa", "wfb": "L0_attn_wfb", "wb": "L0_attn_wb",
        "wga": "L0_attn_wga", "wgb": "L0_attn_wgb",
        "w_gate": "L0_attn_w_gate", "w_up": "L0_attn_w_up", "w_down": "L0_attn_w_down",
    }
    for k, src in ren.items():
        common[k] = np.ascontiguousarray(np.asarray(inputs[src], np.float32))
    common["a_log"] = np.asarray(inputs["L0_attn_a_log"], np.float32).reshape(16, 1).copy()
    common["dt_bias"] = np.asarray(inputs["L0_attn_dt_bias"], np.float32).reshape(16, 1).copy()
    common["o_norm_w"] = np.asarray(inputs["L0_attn_o_norm_w"], np.float32).reshape(1, 128).copy()
    onw4 = np.zeros((4, 4, 128), np.float32)
    for r in range(4):
        onw4[r, r, :] = common["o_norm_w"][0]
    common["onw4"] = onw4
    common["ln1_wT"] = np.ascontiguousarray(
        np.asarray(inputs["L0_attn_ln1_w"], np.float32).reshape(16, 128).T)
    common["ln2_wT"] = np.ascontiguousarray(
        np.asarray(inputs["L0_attn_ln2_w"], np.float32).reshape(16, 128).T)
    for cn, src in (("conv_qT", "L0_attn_conv_q"), ("conv_kT", "L0_attn_conv_k"),
                    ("conv_vT", "L0_attn_conv_v")):
        cw = np.asarray(inputs[src], np.float32)
        common[cn] = np.ascontiguousarray(
            cw.reshape(4, 16, 128).transpose(2, 1, 0).reshape(128, 64))
    common["fwT"] = np.ascontiguousarray(np.asarray(inputs["L0_ffn_weight"], np.float32).T)
    common["gate_wT"] = np.ascontiguousarray(np.asarray(inputs["L0_ffn_gate_w"], np.float32).T)
    common["A_wT"] = np.ascontiguousarray(
        np.asarray(inputs["L0_ffn_A_w"], np.float32).reshape(128, D).T)
    common["B_wRT"] = np.ascontiguousarray(
        np.asarray(inputs["L0_ffn_B_w"], np.float32).transpose(0, 2, 1).reshape(128, D))

    in_maps = []
    for c in range(8):
        bidx, half = c >> 1, c & 1
        m = dict(common)
        m["x_own"] = np.ascontiguousarray(x[bidx, half * TOWN:(half + 1) * TOWN])
        m["x_halo"] = np.ascontiguousarray(
            x[bidx, TOWN - THALO:TOWN] if half == 1 else x[bidx, 0:THALO])
        m["halo_mask"] = np.full((128, 1), float(half), np.float32)
        m["mA"] = np.full((128, 1), float(half), np.float32)
        m["mB"] = np.zeros((128, 1), np.float32)
        in_maps.append(m)

    res = run_bass_kernel_spmd(nc, in_maps, core_ids=list(range(8)))
    out = np.empty((Bsz, T, D), np.float32)
    for c in range(8):
        bidx, half = c >> 1, c & 1
        out[bidx, half * TOWN:(half + 1) * TOWN] = res.results[c]["y"]
    if DBG_OUT:
        kernel.dbg = [{name: r["dbg_" + name] for name in DBG_OUT} for r in res.results]
    return out


# version 03
# speedup vs baseline: 1.0750x; 1.0342x over previous
import numpy as np
import ml_dtypes
from contextlib import ExitStack

import concourse.bass as bass
import concourse.mybir as mybir
from concourse.bass_utils import run_bass_kernel_spmd

F32 = mybir.dt.float32
BF16 = mybir.dt.bfloat16
AF = mybir.ActivationFunctionType
ALU = mybir.AluOpType
AXX = mybir.AxisListType.X

D = 2048
NK = 16
TOWN = 2048
THALO = 128
TT_ = TOWN + THALO
H, DK, DV = 16, 128, 128
CCH = 128
NCH = TOWN // CCH
LORA_ALPHA = 16.0

PHASES = ("p1", "p2", "p3", "p4", "p5", "p6", "p7")
DBG_OUT = ()
P3_SWEEP1_ONLY = False
P1_TRUNC = "full"
P1_SKIP_LORA = False

ENGINES = ("sync", "scalar", "tensor", "vector", "gpsimd")


class Rec:
    def __init__(self):
        self.items = []

    def __getattr__(self, name):
        if name.startswith("_"):
            raise AttributeError(name)
        def call(*a, **k):
            it = [name, a, k, None]
            self.items.append(it)
            class _P:
                def then_inc(_self, sem, n):
                    it[3] = (sem, n)
                    return _self
            return _P()
        return call

    def replay(self, eng):
        for name, a, k, inc in self.items:
            r = getattr(eng, name)(*a, **k)
            if inc is not None:
                r.then_inc(*inc)


class Sem:
    def __init__(self, handle, name):
        self.h = handle
        self.name = name
        self.n = 0


class B:
    def __init__(self, nc, stack):
        self.nc = nc
        self.stack = stack
        self.r = {e: Rec() for e in ENGINES}
        self.sems = {}

    def sem(self, name):
        s = Sem(self.stack.enter_context(self.nc.semaphore(name)), name)
        self.sems[name] = s
        return s

    def slot_sems(self, name, n):
        return [self.sem(f"{name}{i}") for i in range(n)]

    def dma(self, queue, out, in_, sem):
        self.r[queue].dma_start(out=out, in_=in_).then_inc(sem.h, 16)
        sem.n += 16
        return sem.n

    def wait(self, eng, sem, val=None):
        self.r[eng].wait_ge(sem.h, sem.n if val is None else val)

    def inc(self, eng, sem):
        it = self.r[eng].items[-1]
        assert it[3] is None
        sem.n += 1
        it[3] = (sem.h, 1)
        return sem.n

    def finalize_hazards(self):
        from concourse.bass import AP
        skip = {"wait_ge", "nop", "sem_inc", "dma_start", "dma_start_transpose"}
        for ename in ENGINES:
            rec = self.r[ename]
            if not rec.items:
                continue
            asem = self.sem("AUTO_" + ename)

            def outs_ins(name, a, k):
                outs, ins = set(), set()
                for key, v in list(enumerate(a)) + list(k.items()):
                    if isinstance(v, AP):
                        tn = v.tensor.name
                        if key == 0 or key in ("out", "accum_out", "out_max", "out_indices"):
                            outs.add(tn)
                        else:
                            ins.add(tn)
                return outs, ins

            new_items = []
            hist = []
            for it in rec.items:
                name, a, k, inc = it
                if name in skip:
                    new_items.append(it)
                    continue
                outs, ins = outs_ins(name, a, k)
                producer = None
                for pit, pouts in hist[-4:]:
                    if ins & pouts:
                        producer = pit
                if producer is not None:
                    if producer[3] is None:
                        producer[3] = (asem, "AUTOINC")
                    new_items.append(["wait_ge", ("EDGE", producer), {}, None])
                hist.append((it, outs))
                new_items.append(it)
            cnt = 0
            counts = {}
            prod_val = {}
            for it in new_items:
                if it[3] is not None and isinstance(it[3], tuple):
                    if it[3][1] == "AUTOINC":
                        cnt += 1
                        it[3] = (asem.h, 1)
                hnd, n = it[3] if it[3] is not None else (None, 0)
                if hnd is not None:
                    key = id(hnd)
                    c = counts.get(key, 0) + n
                    counts[key] = c
                    prod_val[id(it)] = (hnd, c)
            for it in new_items:
                if it[0] == "wait_ge" and len(it[1]) == 2 and it[1][0] == "EDGE":
                    hnd, val = prod_val[id(it[1][1])]
                    it[1] = (hnd, val)
            asem.n = cnt
            rec.items = new_items

    def barrier(self):
        bar = self.sems["BAR"]
        for name, sm in self.sems.items():
            if sm.n == 0 or name in ("TE", "VE", "SE", "GP", "BAR", "CC") \
                    or name.startswith("AUTO"):
                continue
            for eng in ("sync", "scalar", "gpsimd"):
                self.wait(eng, sm)
        target = bar.n + len(ENGINES)
        for e in ENGINES:
            self.r[e].nop().then_inc(bar.h, 1)
        bar.n = target
        for e in ENGINES:
            self.r[e].wait_ge(bar.h, target)


def build_program():
    nc = bass.Bass("TRN2", target_bir_lowering=False, debug=False,
                   detect_race_conditions=False)
    stack = ExitStack()
    b = B(nc, stack)
    ap = {}

    def inp(name, shape, dtype=F32):
        ap[name] = nc.dram_tensor(name, list(shape), dtype, kind="ExternalInput").ap()
        return ap[name]

    inp("x_own", (TOWN, D)); inp("x_halo", (THALO, D))
    for w in ("wq", "wk", "wv", "wo"):
        inp(w, (D, D))
    inp("conv_qT", (128, 64)); inp("conv_kT", (128, 64)); inp("conv_vT", (128, 64))
    inp("wfa", (D, 128)); inp("wfb", (128, 16)); inp("wb", (D, 16))
    inp("wga", (D, 128)); inp("wgb", (128, D))
    inp("a_log", (16, 1)); inp("dt_bias", (16, 1))
    inp("o_norm_w", (1, 128))
    inp("onw4", (4, 4, 128))
    inp("ln1_wT", (128, 16)); inp("ln2_wT", (128, 16))
    inp("w_gate", (D, 8192)); inp("w_up", (D, 8192)); inp("w_down", (8192, D))
    inp("fwT", (D, D)); inp("gate_wT", (D, 8))
    inp("A_wT", (D, 128)); inp("B_wRT", (128, D))
    inp("identF", (128, 128)); inp("identB", (128, 128), BF16)
    inp("onesColF", (128, 1)); inp("onesRowF", (1, 128)); inp("ones16", (16, CCH))
    inp("mUIT", (128, 128), BF16); inp("mLST", (128, 128), BF16)
    inp("selm4", (16, 4, 512))
    inp("Ib4", (128, 512), BF16); inp("twoIb4", (128, 512))
    inp("E8", (8, 128), BF16)
    inp("halo_mask", (128, 1)); inp("mA", (128, 1)); inp("mB", (128, 1))
    inp("sel4", (4, 4, 128))
    inp("sel16", (16, 2048))

    y = nc.dram_tensor("y", [TOWN, D], F32, kind="ExternalOutput").ap()

    scr = {}
    def scratch(name, shape, dtype=F32):
        scr[name] = nc.dram_tensor(name, list(shape), dtype, kind="Internal")
        return scr[name]

    scratch("x1scr", (D, TT_))
    scratch("xiscr", (D, TOWN))
    scratch("x2scr", (D, TOWN))
    scratch("qfm", (D, TOWN), BF16)
    scratch("kfm", (D, TOWN), BF16)
    scratch("ktok", (H, TOWN, DK), BF16)
    scratch("vtok", (H, TOWN, DV), BF16)
    scratch("ofm", (D, TOWN), BF16)
    scratch("uv_s", (NCH, 128, H * CCH), BF16)
    scratch("wiT_s", (NCH, 128, H * CCH), BF16)
    scratch("w2T_s", (NCH, 128, H * CCH), BF16)
    scratch("qh_s", (NCH, 128, H * CCH), BF16)
    scratch("kb_s", (NCH, 128, H * CCH), BF16)
    scratch("gcc_s", (NCH, 128, 16))
    scratch("mscr", (64, 128, TOWN), BF16)
    scratch("wdc", (64, 128, D), BF16)
    scratch("pcdbg", (128, NK, 520), BF16)
    scratch("scomb_dbg", (128, 2048))
    scratch("ma_dbg", (128, 1))
    scratch("g_dbg", (16, TOWN))
    scratch("beta_dbg", (16, TOWN))
    scratch("hga_dbg", (128, TOWN), BF16)
    scratch("cc_in", (128, H * DK))
    scratch("cc_out", (256, H * DK))

    dbg_outs = {}
    for name in DBG_OUT:
        t = scr[name]
        dbg_outs[name] = nc.dram_tensor("dbg_" + name, list(t.shape), t.dtype,
                                        kind="ExternalOutput").ap()

    E = stack.enter_context
    for s in ("LD", "CA", "ST", "TE", "VE", "SE", "GP", "BAR", "CC"):
        b.sem(s)
    LD, CA, ST = b.sems["LD"], b.sems["CA"], b.sems["ST"]
    TE, VE, SE, CC = b.sems["TE"], b.sems["VE"], b.sems["SE"], b.sems["CC"]
    SY, SC, TN, VR, GP = (b.r["sync"], b.r["scalar"], b.r["tensor"],
                          b.r["vector"], b.r["gpsimd"])

    identF = E(nc.sbuf_tensor("s_identF", [128, 128], F32))
    identB = E(nc.sbuf_tensor("s_identB", [128, 128], BF16))
    onesColF = E(nc.sbuf_tensor("s_onesColF", [128, 1], F32))
    onesRowF = E(nc.sbuf_tensor("s_onesRowF", [1, 128], F32))
    ones16 = E(nc.sbuf_tensor("s_ones16", [16, CCH], F32))
    mUIT = E(nc.sbuf_tensor("s_mUIT", [128, 128], BF16))
    mLST = E(nc.sbuf_tensor("s_mLST", [128, 128], BF16))
    selm4 = E(nc.sbuf_tensor("s_selm4", [16, 4, 512], F32))
    Ib4 = E(nc.sbuf_tensor("s_Ib4", [128, 512], BF16))
    twoIb4 = E(nc.sbuf_tensor("s_twoIb4", [128, 512], F32))
    E8 = E(nc.sbuf_tensor("s_E8", [8, 128], BF16))
    halo_m = E(nc.sbuf_tensor("s_halo_m", [128, 1], F32))
    mA = E(nc.sbuf_tensor("s_mA", [128, 1], F32))
    mB = E(nc.sbuf_tensor("s_mB", [128, 1], F32))
    gwt = E(nc.sbuf_tensor("s_gwt", [128, NK, 8], BF16))
    at = E(nc.sbuf_tensor("s_at", [128, NK, 128], BF16))
    sel16 = E(nc.sbuf_tensor("s_sel16", [16, 2048], F32))
    bt = E(nc.sbuf_tensor("s_bt", [128, NK, 128], BF16))

    for name, t in (("identF", identF), ("identB", identB), ("onesColF", onesColF),
                    ("onesRowF", onesRowF), ("ones16", ones16), ("mUIT", mUIT),
                    ("mLST", mLST), ("selm4", selm4), ("Ib4", Ib4), ("twoIb4", twoIb4), ("E8", E8),
                    ("halo_mask", halo_m), ("mA", mA), ("mB", mB), ("sel16", sel16)):
        b.dma("sync", t[:], ap[name], LD)
    b.wait("tensor", LD); b.wait("vector", LD); b.wait("scalar", LD); b.wait("gpsimd", LD)

    x1v = scr["x1scr"].ap().rearrange("(k r) t -> r k t", r=128)
    xiv = scr["xiscr"].ap().rearrange("(k r) t -> r k t", r=128)
    x2v = scr["x2scr"].ap().rearrange("(k r) t -> r k t", r=128)
    qfmv = scr["qfm"].ap().rearrange("(k r) t -> r k t", r=128)
    kfmv = scr["kfm"].ap().rearrange("(k r) t -> r k t", r=128)
    ofmv = scr["ofm"].ap().rearrange("(k r) t -> r k t", r=128)

    class TransposeHelper:
        def __init__(self, pbanks):
            self.p = pbanks
            self.last_ve = [0, 0]

        def run(self, items, ident):
            n = len(items)
            for base in range(0, n, 4):
                grp = items[base:base + 4]
                sel = (base // 4) % 2
                pp = self.p[sel]
                if self.last_ve[sel]:
                    b.wait("tensor", VE, self.last_ve[sel])
                for j, (sap, _) in enumerate(grp):
                    TN.transpose(pp[:, j * 128:(j + 1) * 128], sap, ident)
                t = b.inc("tensor", TE)
                b.wait("vector", TE, t)
                for j, (_, dap) in enumerate(grp):
                    VR.tensor_copy(dap, pp[:, j * 128:(j + 1) * 128])
                self.last_ve[sel] = b.inc("vector", VE)

    def ffn_setup():
        gwv = ap["gate_wT"].rearrange("(k r) e -> r k e", r=128)
        b.dma("gpsimd", gwt[:, :, :], gwv, CA)
        awv = ap["A_wT"].rearrange("(k r) e -> r k e", r=128)
        b.dma("gpsimd", at[:, :, :], awv, CA)
        bwv = ap["B_wRT"].rearrange("p (j c) -> p j c", j=16)
        b.dma("gpsimd", bt[:, :, :], bwv, CA)
        b.wait("tensor", CA)

    def ffn_phase(which):
        first = which == "p1"
        with ExitStack() as ps:
            PE = ps.enter_context
            xfm = PE(nc.sbuf_tensor("xfm" + which, [128, NK, 1024], F32))
            lnb = PE(nc.sbuf_tensor("lnb" + which, [128, NK, 1024], BF16))
            sq = PE(nc.sbuf_tensor("sq" + which, [128, 2, NK, 128], F32))
            xtm = PE(nc.sbuf_tensor("xtm" + which, [128, 2, D], F32))
            wt = PE(nc.sbuf_tensor("wt" + which, [128, 2, NK, 128], BF16))
            stat = PE(nc.sbuf_tensor("stat" + which, [128, 32], F32))
            colw = PE(nc.sbuf_tensor("colw" + which, [128, 16], F32))
            rowt = PE(nc.sbuf_tensor("rowt" + which, [16, 128], F32))
            cwT = PE(nc.sbuf_tensor("cwT" + which, [8, 1024], BF16))
            cwb = PE(nc.sbuf_tensor("cwb" + which, [128, 1024], BF16))
            tsc = PE(nc.sbuf_tensor("tsc" + which, [128, 1024], BF16))
            otm = PE(nc.sbuf_tensor("otm" + which, [128, 2, D], F32))
            pA = PE(nc.psum_tensor("pA" + which, [128, 1024], F32))
            pB = PE(nc.psum_tensor("pB" + which, [128, 1024], F32))
            pT = PE(nc.psum_tensor("pT" + which, [128, 512], F32))
            pT2 = PE(nc.psum_tensor("pT2" + which, [128, 512], F32))
            pTB = PE(nc.psum_tensor("pTB" + which, [128, 512], BF16))
            pSM = PE(nc.psum_tensor("pSM" + which, [128, 128], F32))
            pST = pSM

            if first:
                ffn_setup()
            fwv = ap["fwT"].rearrange("(k r) o -> r k o", r=128)
            XT = b.slot_sems("XT" + which, 2)
            WTS = b.slot_sems("WTS" + which, 2)
            OTS = b.slot_sems("OTS" + which, 2)
            xt_busy = [0, 0]
            wt_busy = [0, 0]
            ot_busy = [0, 0]

            thF = TransposeHelper([pT, pT2])
            panels = ([(THALO, "halo")] if first else []) + [(1024, 0), (1024, 1)]
            st_prev = 0
            for T, ptag in panels:
                halo = ptag == "halo"
                ntile = T // 128
                ngg = max(1, T // 512)
                wgg = min(512, T)
                b.wait("vector", ST, st_prev)
                if first:
                    src = ap["x_halo"] if halo else ap["x_own"]
                    off = 0 if halo else ptag * 1024
                    for r in range(ntile):
                        if xt_busy[r % 2]:
                            b.wait("sync", TE, xt_busy[r % 2])
                        b.dma("sync", xtm[:, r % 2, :],
                              src[off + r * 128: off + (r + 1) * 128, :], XT[r % 2])
                        b.wait("tensor", XT[r % 2], XT[r % 2].n)
                        items = [(xtm[:, r % 2, k * 128:(k + 1) * 128],
                                  xfm[:, k, r * 128:(r + 1) * 128]) for k in range(NK)]
                        thF.run(items, identF[:, :])
                        xt_busy[r % 2] = TE.n
                else:
                    b.wait("sync", TE, TE.n)
                    b.dma("sync", xfm[:, :, 0:1024], x2v[:, :, ptag * 1024:(ptag + 1) * 1024], LD)
                    b.wait("tensor", LD); b.wait("vector", LD); b.wait("scalar", LD)

                if first and P1_TRUNC == "xonly":
                    b.wait("scalar", VE)
                    c0 = 0 if halo else THALO + ptag * 1024
                    st_prev = b.dma("scalar", x1v[:, :, c0:c0 + T], xfm[:, :, 0:T], ST)
                    continue
                b.wait("scalar", VE)
                te_sq = {}
                for r in range(ntile):
                    if r >= 2:
                        b.wait("scalar", TE, te_sq[r - 2])
                    SC.activation(sq[:, r % 2, :, :], xfm[:, :, r * 128:(r + 1) * 128], AF.Square)
                    s = b.inc("scalar", SE)
                    b.wait("tensor", SE, s)
                    for k in range(NK):
                        TN.matmul(pST[:, r:r + 1], xfm[:, k, r * 128:(r + 1) * 128],
                                  onesColF[:], start=(r == 0 and k == 0), stop=False)
                        TN.matmul(pST[:, 8 + r:9 + r], sq[:, r % 2, k, :], onesColF[:],
                                  start=False, stop=(r == ntile - 1 and k == NK - 1))
                    te_sq[r] = b.inc("tensor", TE)
                t = TE.n
                b.wait("vector", TE, t)
                nt = ntile
                VR.memset(colw[:, :], 0.0)
                VR.tensor_copy(stat[:, 0:nt], pST[:, 0:nt])
                VR.tensor_copy(stat[:, 8:8 + nt], pST[:, 8:8 + nt])
                VR.tensor_scalar(colw[:, 0:nt], stat[:, 0:nt], 1.0 / D, None, ALU.mult)
                VR.tensor_mul(colw[:, 8:8 + nt], colw[:, 0:nt], colw[:, 0:nt])
                VR.scalar_tensor_tensor(colw[:, 8:8 + nt], stat[:, 8:8 + nt], 1.0 / D,
                                        colw[:, 8:8 + nt], ALU.mult, ALU.subtract)
                v = b.inc("vector", VE)
                b.wait("scalar", VE, v)
                SC.activation(colw[:, 8:8 + nt], colw[:, 8:8 + nt], AF.Sqrt, bias=np.float32(1e-5))
                s = b.inc("scalar", SE)
                b.wait("vector", SE, s)
                VR.reciprocal(colw[:, 8:8 + nt], colw[:, 8:8 + nt])
                VR.tensor_mul(colw[:, 0:nt], colw[:, 0:nt], colw[:, 8:8 + nt])
                VR.tensor_scalar(colw[:, 0:nt], colw[:, 0:nt], -1.0, None, ALU.mult)
                v = b.inc("vector", VE)
                b.wait("tensor", VE, v)
                TN.transpose(pSM[0:16, 0:128], colw[:, 0:16], identF[:, :])
                t = b.inc("tensor", TE)
                b.wait("vector", TE, t)
                VR.tensor_copy(rowt[:, :], pSM[0:16, 0:128])
                v = b.inc("vector", VE)
                b.wait("tensor", VE, v)
                for r in range(ntile):
                    TN.matmul(pA[:, r * 128:(r + 1) * 128],
                              sel16[:, (8 + r) * 128:(9 + r) * 128], rowt[:, :],
                              start=(r % 4 == 0), stop=(r % 4 == 3 or r == ntile - 1))
                    TN.matmul(pB[:, r * 128:(r + 1) * 128],
                              sel16[:, r * 128:(r + 1) * 128], rowt[:, :],
                              start=(r % 4 == 0), stop=(r % 4 == 3 or r == ntile - 1))
                t = b.inc("tensor", TE)
                b.wait("vector", TE, t)
                for k in range(NK):
                    VR.tensor_mul(lnb[:, k, 0:T], xfm[:, k, 0:T], pA[:, 0:T])
                    VR.tensor_add(lnb[:, k, 0:T], lnb[:, k, 0:T], pB[:, 0:T])
                v_lnb = b.inc("vector", VE)
                b.wait("tensor", VE, v_lnb)

                if first and P1_TRUNC == "ln":
                    qv_dbg = scr["qfm"].ap().rearrange("(k r) t -> r k t", r=128)
                    b.wait("scalar", VE)
                    if not halo:
                        b.dma("scalar", qv_dbg[:, :, ptag * 1024:ptag * 1024 + T],
                              lnb[:, :, 0:T], ST)
                    continue
                for r in range(ntile):
                    for k in range(NK):
                        TN.matmul(pSM[:, 0:8], lnb[:, k, r * 128:(r + 1) * 128],
                                  gwt[:, k, :], start=(k == 0), stop=(k == NK - 1))
                    t = b.inc("tensor", TE)
                    b.wait("vector", TE, t)
                    VR.reduce_max(stat[:, 0:1], pSM[:, 0:8], AXX)
                    VR.tensor_scalar(stat[:, 8:16], pSM[:, 0:8], stat[:, 0:1], 0.0,
                                     ALU.subtract, ALU.is_ge)
                    VR.scalar_tensor_tensor(stat[:, 16:24], stat[:, 8:16], -1e30, pSM[:, 0:8],
                                            ALU.mult, ALU.add)
                    VR.reduce_max(stat[:, 1:2], stat[:, 16:24], AXX)
                    VR.tensor_sub(stat[:, 2:3], stat[:, 1:2], stat[:, 0:1])
                    v = b.inc("vector", VE)
                    b.wait("scalar", VE, v)
                    SC.activation(stat[:, 3:4], stat[:, 2:3], AF.Sigmoid)
                    s = b.inc("scalar", SE)
                    b.wait("vector", SE, s)
                    VR.tensor_scalar(stat[:, 4:5], stat[:, 3:4], -1.0, 1.0, ALU.mult, ALU.add)
                    VR.tensor_scalar(stat[:, 24:32], stat[:, 16:24], stat[:, 1:2], 0.0,
                                     ALU.subtract, ALU.is_ge)
                    VR.tensor_scalar(stat[:, 8:16], stat[:, 8:16], stat[:, 4:5], None, ALU.mult)
                    VR.tensor_scalar(stat[:, 24:32], stat[:, 24:32], stat[:, 3:4], None, ALU.mult)
                    VR.tensor_add(stat[:, 24:32], stat[:, 24:32], stat[:, 8:16])
                    if first and P1_TRUNC == "lora" and not halo and ptag == 0:
                        VR.tensor_copy(tsc[:, r * 32:(r + 1) * 32], stat[:, 0:32])
                    v = b.inc("vector", VE)
                    b.wait("tensor", VE, v)
                    TN.transpose(pSM[0:8, 64:192] if False else pSM[0:8, 0:128],
                                 stat[:, 24:32], identF[:, :])
                    t = b.inc("tensor", TE)
                    b.wait("vector", TE, t)
                    VR.tensor_copy(cwT[:, r * 128:(r + 1) * 128], pSM[0:8, 0:128])
                    v = b.inc("vector", VE)
                    b.wait("tensor", VE, v)
                for r in range(ntile):
                    TN.matmul(pB[:, r * 128:(r + 1) * 128], E8[:], cwT[:, r * 128:(r + 1) * 128],
                              start=(r % 4 == 0), stop=(r % 4 == 3 or r == ntile - 1))
                for gg in range(ngg):
                    for k in range(NK):
                        TN.matmul(pA[:, gg * 512:gg * 512 + wgg], at[:, k, :],
                                  lnb[:, k, gg * 512:gg * 512 + wgg],
                                  start=(k == 0), stop=(k == NK - 1))
                t = b.inc("tensor", TE)
                b.wait("vector", TE, t)
                VR.tensor_copy(cwb[:, 0:T], pB[:, 0:T])
                if not (first and P1_TRUNC == "lora"):
                    for gg in range(ngg):
                        VR.scalar_tensor_tensor(tsc[:, gg * 512:gg * 512 + wgg],
                                                pA[:, gg * 512:gg * 512 + wgg], LORA_ALPHA,
                                                cwb[:, gg * 512:gg * 512 + wgg], ALU.mult, ALU.mult)
                v = b.inc("vector", VE)
                b.wait("tensor", VE, v)

                if first and P1_TRUNC == "lora" and not halo:
                    kv_dbg = scr["kfm"].ap().rearrange("(k r) t -> r k t", r=128)
                    b.wait("scalar", VE)
                    b.dma("scalar", kv_dbg[:, 0:1, ptag * 1024:ptag * 1024 + T]
                          .rearrange("r k t -> r (k t)"), tsc[:, 0:T], ST)
                    b.dma("scalar", kv_dbg[:, 1:2, ptag * 1024:ptag * 1024 + T]
                          .rearrange("r k t -> r (k t)"), cwb[:, 0:T], ST)
                if first and P1_TRUNC == "lora":
                    continue
                ve_done = {}
                te_mm = {}
                for m in range(NK):
                    bsel = m % 2
                    pO = pA if bsel == 0 else pB
                    if wt_busy[bsel]:
                        b.wait("gpsimd", TE, wt_busy[bsel])
                    b.dma("gpsimd", wt[:, bsel], fwv[:, :, m * 128:(m + 1) * 128], WTS[bsel])
                    b.wait("tensor", WTS[bsel], WTS[bsel].n)
                    if m >= 2:
                        b.wait("tensor", VE, ve_done[m - 2])
                    for gg in range(ngg):
                        for k in range(NK):
                            TN.matmul(pO[:, gg * 512:gg * 512 + wgg], wt[:, bsel, k, :],
                                      lnb[:, k, gg * 512:gg * 512 + wgg],
                                      start=(k == 0), stop=(P1_SKIP_LORA and k == NK - 1))
                        if P1_SKIP_LORA:
                            TN.matmul(pO[:, gg * 512:gg * 512 + wgg], bt[:, m, :],
                                      tsc[:, gg * 512:gg * 512 + wgg], start=False, stop=True,
                                      skip_group_check=True) if False else None
                        if not P1_SKIP_LORA:
                            TN.matmul(pO[:, gg * 512:gg * 512 + wgg], bt[:, m, :],
                                      tsc[:, gg * 512:gg * 512 + wgg], start=False, stop=True)
                    te_mm[m] = b.inc("tensor", TE)
                    wt_busy[bsel] = te_mm[m]
                    b.wait("vector", TE, te_mm[m])
                    for gg in range(ngg):
                        VR.scalar_tensor_tensor(xfm[:, m, gg * 512:gg * 512 + wgg],
                                                pO[:, gg * 512:gg * 512 + wgg], 0.5,
                                                xfm[:, m, gg * 512:gg * 512 + wgg],
                                                ALU.mult, ALU.add)
                    ve_done[m] = b.inc("vector", VE)
                b.wait("scalar", VE)
                if first:
                    c0 = 0 if halo else THALO + ptag * 1024
                    st_prev = b.dma("scalar", x1v[:, :, c0:c0 + T], xfm[:, :, 0:T], ST)
                else:
                    for r in range(ntile):
                        if ot_busy[r % 2]:
                            b.wait("vector", OTS[r % 2], ot_busy[r % 2])
                        items = [(xfm[:, k, r * 128:(r + 1) * 128],
                                  otm[:, r % 2, k * 128:(k + 1) * 128]) for k in range(NK)]
                        thF.run(items, identF[:, :])
                        b.wait("scalar", VE)
                        ot_busy[r % 2] = b.dma(
                            "scalar", y[ptag * 1024 + r * 128: ptag * 1024 + (r + 1) * 128, :],
                            otm[:, r % 2, :], OTS[r % 2])
            b.barrier()

    def proj_phase(g_res, beta_res, hga_res):
        with ExitStack() as ps:
            PE = ps.enter_context
            xp = PE(nc.sbuf_tensor("xp2", [128, NK, 512], F32))
            hb = PE(nc.sbuf_tensor("hb2", [128, NK, 512], BF16))
            sq2 = PE(nc.sbuf_tensor("sq2", [128, 2, NK, 128], BF16))
            pc = PE(nc.sbuf_tensor("pc2", [128, NK, 520], BF16))
            carryW = PE(nc.sbuf_tensor("carryW2", [128, 3, NK, 3], BF16))
            wbuf = PE(nc.sbuf_tensor("wbuf2", [128, 2, 512], BF16))
            stat = PE(nc.sbuf_tensor("stat2", [128, 32], F32))
            colw = PE(nc.sbuf_tensor("colw2", [128, 16], F32))
            rowt = PE(nc.sbuf_tensor("rowt2", [16, 128], F32))
            cvall = PE(nc.sbuf_tensor("cvall2", [128, NK, 512], BF16))
            sqq4 = PE(nc.sbuf_tensor("sqq42", [128, 4, 512], BF16))
            colw64 = PE(nc.sbuf_tensor("colw642", [128, 64], F32))
            lrow4 = PE(nc.sbuf_tensor("lrow42", [4, 2, 128], F32))
            sel4 = PE(nc.sbuf_tensor("sel42", [4, 4, 128], F32))
            onesColB = PE(nc.sbuf_tensor("onesColB2", [128, 1], BF16))
            tkb = PE(nc.sbuf_tensor("tkb2", [128, 2, 4, 128], BF16))
            hfa = PE(nc.sbuf_tensor("hfa2", [128, 512], BF16))
            wfa_s = PE(nc.sbuf_tensor("wfa_s", [128, NK, 128], BF16))
            wga_s = PE(nc.sbuf_tensor("wga_s", [128, NK, 128], BF16))
            wb_s = PE(nc.sbuf_tensor("wb_s", [128, NK, 16], BF16))
            wfb_s = PE(nc.sbuf_tensor("wfb_s", [128, 16], BF16))
            cqT = PE(nc.sbuf_tensor("cqT", [128, 64], F32))
            ckT = PE(nc.sbuf_tensor("ckT", [128, 64], F32))
            cvT = PE(nc.sbuf_tensor("cvT", [128, 64], F32))
            ln1c = PE(nc.sbuf_tensor("ln1c", [128, 16], F32))
            acol = PE(nc.sbuf_tensor("acol", [16, 4], F32))
            zw = PE(nc.sbuf_tensor("zw2", [16, 4, 512], F32))
            pP = PE(nc.psum_tensor("pP2", [128, 2048], F32))
            pA = PE(nc.psum_tensor("pA2", [128, 512], F32))
            pB = PE(nc.psum_tensor("pB2", [128, 512], F32))
            pC = PE(nc.psum_tensor("pC2", [128, 512], F32))
            pTB = PE(nc.psum_tensor("pTB2", [128, 1024], BF16))

            WSL = b.slot_sems("WSL2", 2)
            TKS = b.slot_sems("TKS2", 2)
            FMS = b.slot_sems("FMS2", 2)
            te_wslot = [0, 0]
            tk_busy = [0, 0]
            fm_busy = [0, 0]
            te_sqslot = [0, 0]
            ve_lcol = 0

            b.dma("gpsimd", wfa_s[:, :, :], ap["wfa"].rearrange("(k r) c -> r k c", r=128), CA)
            b.dma("gpsimd", wga_s[:, :, :], ap["wga"].rearrange("(k r) c -> r k c", r=128), CA)
            b.dma("gpsimd", wb_s[:, :, :], ap["wb"].rearrange("(k r) c -> r k c", r=128), CA)
            b.dma("gpsimd", wfb_s[:, :], ap["wfb"], CA)
            b.dma("sync", cqT[:, :], ap["conv_qT"], LD)
            b.dma("sync", ckT[:, :], ap["conv_kT"], LD)
            b.dma("sync", cvT[:, :], ap["conv_vT"], LD)
            b.dma("sync", ln1c[:, :], ap["ln1_wT"], LD)
            b.dma("sync", acol[0:16, 0:1], ap["a_log"], LD)
            b.dma("sync", sel4[:, :, :], ap["sel4"], LD)
            b.dma("gpsimd", onesColB[:, :], ap["onesColF"], CA)
            b.dma("sync", acol[0:16, 1:2], ap["dt_bias"], LD)
            b.wait("scalar", LD)
            SC.activation(acol[0:16, 2:3], acol[0:16, 0:1], AF.Exp)
            se0 = b.inc("scalar", SE)
            b.wait("vector", SE, se0)
            VR.tensor_scalar(acol[0:16, 2:3], acol[0:16, 2:3], -1.0, None, ALU.mult)
            b.inc("vector", VE)
            b.wait("tensor", LD); b.wait("tensor", CA); b.wait("vector", LD)

            te_xp = 0
            ve_xp = 0
            panels = [(THALO, "halo")] + [(512, p) for p in range(4)]
            for T, ptag in panels:
                halo = ptag == "halo"
                nt = ntile = T // 128
                if te_xp:
                    b.wait("sync", TE, te_xp)
                    b.wait("sync", VE, ve_xp)
                c0 = 0 if halo else THALO + ptag * 512
                b.dma("sync", xp[:, :, 0:T], x1v[:, :, c0:c0 + T], LD)
                b.wait("scalar", LD); b.wait("tensor", LD); b.wait("vector", LD)
                b.wait("tensor", SE, SE.n)
                b.wait("tensor", VE, VE.n)
                for r in range(ntile):
                    if te_sqslot[r % 2]:
                        b.wait("scalar", TE, te_sqslot[r % 2])
                    SC.activation(sq2[:, r % 2, :, :], xp[:, :, r * 128:(r + 1) * 128], AF.Square)
                    s = b.inc("scalar", SE)
                    b.wait("tensor", SE, s)
                    for k in range(NK):
                        TN.matmul(pC[:, r:r + 1], xp[:, k, r * 128:(r + 1) * 128],
                                  onesColF[:], start=(r == 0 and k == 0), stop=False)
                        TN.matmul(pC[:, 8 + r:9 + r], sq2[:, r % 2, k, :], onesColB[:],
                                  start=False, stop=(r == ntile - 1 and k == NK - 1))
                    te_sqslot[r % 2] = b.inc("tensor", TE)
                b.wait("vector", TE, TE.n)
                VR.memset(colw[:, :], 0.0)
                VR.tensor_copy(stat[:, 0:nt], pC[:, 0:nt])
                VR.tensor_copy(stat[:, 8:8 + nt], pC[:, 8:8 + nt])
                VR.tensor_scalar(colw[:, 0:nt], stat[:, 0:nt], 1.0 / D, None, ALU.mult)
                VR.tensor_mul(colw[:, 8:8 + nt], colw[:, 0:nt], colw[:, 0:nt])
                VR.scalar_tensor_tensor(colw[:, 8:8 + nt], stat[:, 8:8 + nt], 1.0 / D,
                                        colw[:, 8:8 + nt], ALU.mult, ALU.subtract)
                v = b.inc("vector", VE)
                b.wait("scalar", VE, v)
                SC.activation(colw[:, 8:8 + nt], colw[:, 8:8 + nt], AF.Sqrt,
                              bias=np.float32(1e-11), scale=np.float32(1.0 + 1e-6))
                s = b.inc("scalar", SE)
                b.wait("vector", SE, s)
                VR.reciprocal(colw[:, 8:8 + nt], colw[:, 8:8 + nt])
                VR.tensor_mul(colw[:, 0:nt], colw[:, 0:nt], colw[:, 8:8 + nt])
                VR.tensor_scalar(colw[:, 0:nt], colw[:, 0:nt], -1.0, None, ALU.mult)
                v = b.inc("vector", VE)
                b.wait("tensor", VE, v)
                TN.transpose(pC[0:16, 0:128], colw[:, 0:16], identF[:, :])
                t = b.inc("tensor", TE)
                b.wait("vector", TE, t)
                VR.tensor_copy(rowt[:, :], pC[0:16, 0:128])
                v = b.inc("vector", VE)
                b.wait("tensor", VE, v)
                for r in range(ntile):
                    TN.matmul(pA[:, r * 128:(r + 1) * 128],
                              sel16[:, (8 + r) * 128:(9 + r) * 128], rowt[:, :],
                              start=(r == 0), stop=(r == ntile - 1))
                    TN.matmul(pB[:, r * 128:(r + 1) * 128],
                              sel16[:, r * 128:(r + 1) * 128], rowt[:, :],
                              start=(r == 0), stop=(r == ntile - 1))
                t = b.inc("tensor", TE)
                b.wait("vector", TE, t)
                for k in range(NK):
                    VR.scalar_tensor_tensor(hb[:, k, 0:T], xp[:, k, 0:T], ln1c[:, k:k + 1],
                                            pA[:, 0:T], ALU.mult, ALU.mult)
                    VR.scalar_tensor_tensor(hb[:, k, 0:T], pB[:, 0:T], ln1c[:, k:k + 1],
                                            hb[:, k, 0:T], ALU.mult, ALU.add)
                v_hb = b.inc("vector", VE)
                b.wait("tensor", VE, v_hb)

                for wi, (wname, convT, is_q, is_k) in enumerate(
                        (("wq", cqT, True, False), ("wk", ckT, False, True),
                         ("wv", cvT, False, False))):
                    wap = ap[wname]
                    is_v = not (is_q or is_k)
                    b.wait("tensor", VE, VE.n)
                    if not halo:
                        VR.tensor_copy(pc[:, :, 0:3], carryW[:, wi])
                        b.inc("vector", VE)
                        if DBG_OUT and ptag == 1 and wi == 1:
                            b.wait("scalar", VE, VE.n)
                            b.dma("scalar", scr["pcdbg"].ap(), pc[:, :, :], ST)
                            b.wait("scalar", ST)
                    for grp in range(4):
                        for k in range(NK):
                            ksl = k % 2
                            if te_wslot[ksl]:
                                b.wait("gpsimd", TE, te_wslot[ksl])
                            b.dma("gpsimd", wbuf[:, ksl, :],
                                  wap[k * 128:(k + 1) * 128, grp * 512:(grp + 1) * 512],
                                  WSL[ksl])
                            b.wait("tensor", WSL[ksl], WSL[ksl].n)
                            for mm_ in range(4):
                                TN.matmul(pP[:, mm_ * 512:mm_ * 512 + T],
                                          wbuf[:, ksl, mm_ * 128:(mm_ + 1) * 128],
                                          hb[:, k, 0:T], start=(k == 0), stop=(k == NK - 1))
                            te_wslot[ksl] = b.inc("tensor", TE)
                        b.wait("vector", TE, TE.n)
                        for mm_ in range(4):
                            m = grp * 4 + mm_
                            VR.tensor_copy(pc[:, m, 3:3 + T], pP[:, mm_ * 512:mm_ * 512 + T])
                        ve_pc = b.inc("vector", VE)
                        b.wait("tensor", VE, ve_pc)
                    if halo:
                        VR.tensor_scalar(carryW[:, wi], pc[:, :, T:T + 3],
                                         halo_m[:, 0:1], None, ALU.mult)
                        b.inc("vector", VE)
                        continue
                    VR.tensor_copy(carryW[:, wi], pc[:, :, T:T + 3])
                    b.inc("vector", VE)
                    b.wait("vector", FMS[0], FMS[0].n)
                    for m in range(NK):
                        VR.tensor_scalar(cvall[:, m, :], pc[:, m, 3:3 + T],
                                         convT[:, 4 * m + 3:4 * m + 4], None, ALU.mult)
                        VR.scalar_tensor_tensor(cvall[:, m, :], pc[:, m, 2:2 + T],
                                                convT[:, 4 * m + 2:4 * m + 3], cvall[:, m, :],
                                                ALU.mult, ALU.add)
                        VR.scalar_tensor_tensor(cvall[:, m, :], pc[:, m, 1:1 + T],
                                                convT[:, 4 * m + 1:4 * m + 2], cvall[:, m, :],
                                                ALU.mult, ALU.add)
                        VR.scalar_tensor_tensor(cvall[:, m, :], pc[:, m, 0:T],
                                                convT[:, 4 * m:4 * m + 1], cvall[:, m, :],
                                                ALU.mult, ALU.add)
                    v_cv = b.inc("vector", VE)
                    b.wait("scalar", VE, v_cv)
                    for m in range(NK):
                        SC.activation(cvall[:, m, :], cvall[:, m, :], AF.Silu)
                    s_si = b.inc("scalar", SE)
                    if not is_v:
                        te_sqo2 = [0, 0, 0, 0]
                        b.wait("scalar", SE, s_si)
                        for m in range(NK):
                            if m >= 4 and te_sqo2[m % 4]:
                                b.wait("scalar", TE, te_sqo2[m % 4])
                            SC.activation(sqq4[:, m % 4, :], cvall[:, m, :], AF.Square)
                            sse = b.inc("scalar", SE)
                            b.wait("tensor", SE, sse)
                            for r in range(4):
                                TN.matmul(pC[:, 4 * m + r:4 * m + r + 1],
                                          sqq4[:, m % 4, r * 128:(r + 1) * 128], onesColB[:],
                                          start=(m == 0 and r == 0),
                                          stop=(m == NK - 1 and r == 3))
                            te_sqo2[m % 4] = b.inc("tensor", TE)
                        b.wait("vector", TE, TE.n)
                        VR.tensor_copy(colw64[:, :], pC[:, 0:64])
                        v = b.inc("vector", VE)
                        b.wait("scalar", VE, v)
                        if is_q:
                            SC.activation(colw64[:, :], colw64[:, :], AF.Sqrt,
                                          bias=np.float32(128e-6), scale=np.float32(128.0))
                        else:
                            SC.activation(colw64[:, :], colw64[:, :], AF.Sqrt,
                                          bias=np.float32(1e-6))
                        s_ = b.inc("scalar", SE)
                        b.wait("vector", SE, s_)
                        VR.reciprocal(colw64[:, :], colw64[:, :])
                        v = b.inc("vector", VE)
                        b.wait("tensor", VE, v)
                        ve_sc = {}
                        for m in range(NK):
                            TN.transpose(pC[0:4, 64 + (m % 2) * 128:64 + (m % 2) * 128 + 128],
                                         colw64[:, 4 * m:4 * m + 4], identF[:, :])
                            t = b.inc("tensor", TE)
                            b.wait("vector", TE, t)
                            VR.tensor_copy(lrow4[:, m % 2, :],
                                           pC[0:4, 64 + (m % 2) * 128:64 + (m % 2) * 128 + 128])
                            vlr = b.inc("vector", VE)
                            b.wait("tensor", VE, vlr)
                            if m >= 4 and (m - 4) in ve_sc:
                                b.wait("tensor", VE, ve_sc[m - 4])
                            for r in range(4):
                                TN.matmul(pP[:, (m % 4) * 512 + r * 128:
                                             (m % 4) * 512 + (r + 1) * 128],
                                          sel4[:, r, :], lrow4[:, m % 2, :],
                                          start=(r == 0), stop=(r == 3))
                            t = b.inc("tensor", TE)
                            b.wait("vector", TE, t)
                            VR.tensor_mul(cvall[:, m, :], cvall[:, m, :],
                                          pP[:, (m % 4) * 512:(m % 4) * 512 + 512])
                            ve_sc[m] = b.inc("vector", VE)
                        b.wait("scalar", VE, VE.n)
                    else:
                        b.wait("scalar", SE, s_si)
                    if not is_v:
                        dstv = qfmv if is_q else kfmv
                        b.wait("scalar", SE, SE.n)
                        b.wait("scalar", VE, VE.n)
                        for m in range(NK):
                            b.dma("scalar", dstv[:, m, ptag * 512:(ptag + 1) * 512],
                                  cvall[:, m, :], FMS[0])
                    if is_k or is_v:
                        dst = scr["ktok" if is_k else "vtok"].ap()
                        b.wait("tensor", SE, SE.n)
                        b.wait("tensor", VE, VE.n)
                        for mp in range(0, NK, 2):
                            for mo in range(2):
                                m = mp + mo
                                for r in range(4):
                                    TN.transpose(pTB[:, mo * 512 + r * 128:
                                                     mo * 512 + (r + 1) * 128],
                                                 cvall[:, m, r * 128:(r + 1) * 128],
                                                 identB[:, :])
                            t = b.inc("tensor", TE)
                            b.wait("vector", TE, t)
                            for mo in range(2):
                                tsl = mo
                                if tk_busy[tsl]:
                                    b.wait("vector", TKS[tsl], tk_busy[tsl])
                                VR.tensor_copy(tkb[:, tsl].rearrange("p a c -> p (a c)"),
                                               pTB[:, mo * 512:mo * 512 + 512])
                            v_tk = b.inc("vector", VE)
                            b.wait("tensor", VE, v_tk)
                            b.wait("scalar", VE, v_tk)
                            for mo in range(2):
                                tk_busy[mo] = b.dma(
                                    "scalar",
                                    dst[mp + mo][ptag * 512:(ptag + 1) * 512, :]
                                    .rearrange("(rr t) d -> t rr d", t=128),
                                    tkb[:, mo], TKS[mo])
                if halo:
                    te_xp = TE.n
                    ve_xp = VE.n
                    continue

                b.wait("tensor", VE, VE.n)
                for k in range(NK):
                    TN.matmul(pA[:, 0:T], wfa_s[:, k, :], hb[:, k, 0:T],
                              start=(k == 0), stop=(k == NK - 1))
                t = b.inc("tensor", TE)
                b.wait("vector", TE, t)
                VR.tensor_copy(hfa[:, 0:T], pA[:, 0:T])
                v = b.inc("vector", VE)
                b.wait("tensor", VE, v)
                TN.matmul(pB[0:16, 0:T], wfb_s[:, :], hfa[:, 0:T], start=True, stop=True)
                for k in range(NK):
                    TN.matmul(pC[0:16, 0:T], wb_s[:, k, :], hb[:, k, 0:T],
                              start=(k == 0), stop=(k == NK - 1))
                t = b.inc("tensor", TE)
                b.wait("scalar", TE, t)
                b.wait("vector", TE, t)
                VR.tensor_scalar(zw[:, 0, 0:T], pB[0:16, 0:T], acol[0:16, 1:2], None, ALU.add)
                v = b.inc("vector", VE)
                b.wait("scalar", VE, v)
                SC.activation(zw[:, 1, 0:T], zw[:, 0, 0:T], AF.Abs)
                SC.activation(zw[:, 2, 0:T], zw[:, 1, 0:T], AF.Exp, scale=np.float32(-1.0))
                SC.activation(zw[:, 2, 0:T], zw[:, 2, 0:T], AF.Ln, bias=np.float32(1.0))
                SC.activation(beta_res[:, ptag * 512:ptag * 512 + T], pC[0:16, 0:T], AF.Sigmoid)
                s = b.inc("scalar", SE)
                b.wait("vector", SE, s)
                VR.tensor_scalar_max(zw[:, 3, 0:T], zw[:, 0, 0:T], 0.0)
                VR.tensor_add(zw[:, 3, 0:T], zw[:, 3, 0:T], zw[:, 2, 0:T])
                VR.tensor_scalar(g_res[:, ptag * 512:ptag * 512 + T],
                                 zw[:, 3, 0:T], acol[0:16, 2:3], None, ALU.mult)
                v = b.inc("vector", VE)
                b.wait("tensor", VE, v)
                for k in range(NK):
                    TN.matmul(pA[:, 0:T], wga_s[:, k, :], hb[:, k, 0:T],
                              start=(k == 0), stop=(k == NK - 1))
                t = b.inc("tensor", TE)
                b.wait("vector", TE, t)
                VR.tensor_copy(hga_res[:, ptag * 512:ptag * 512 + T], pA[:, 0:T])
                b.inc("vector", VE)
                te_xp = TE.n
                ve_xp = VE.n
            b.barrier()

    def scan_phase(g_res, beta_res):
        with ExitStack() as ps:
            PE = ps.enter_context
            S = PE(nc.sbuf_tensor("S3", [128, H, DV], F32))
            S0b = PE(nc.sbuf_tensor("S0b3", [128, H, DV], BF16))
            qf = PE(nc.sbuf_tensor("qf3", [128, 2, H, CCH], BF16))
            kf = PE(nc.sbuf_tensor("kf3", [128, 2, H, CCH], BF16))
            kt = PE(nc.sbuf_tensor("kt3", [128, 2, H, DK], BF16))
            vt = PE(nc.sbuf_tensor("vt3", [128, 2, H, DV], BF16))
            Uv = PE(nc.sbuf_tensor("Uv3", [128, 2, H, DV], BF16))
            WiT = PE(nc.sbuf_tensor("WiT3", [128, 2, H, CCH], BF16))
            W2T = PE(nc.sbuf_tensor("W2T3", [128, 2, H, CCH], BF16))
            Qh = PE(nc.sbuf_tensor("Qh3", [128, 2, H, CCH], BF16))
            Kb = PE(nc.sbuf_tensor("Kb3", [128, 2, H, DK], BF16))
            Ua = PE(nc.sbuf_tensor("Ua3", [128, H, DV], BF16))
            ofmb = PE(nc.sbuf_tensor("ofmb3", [128, 2, H, CCH], BF16))
            gcc = PE(nc.sbuf_tensor("gcc3", [128, 2, 16], F32))
            DpT4 = PE(nc.sbuf_tensor("DpT4", [128, 512], BF16))
            DsL4 = PE(nc.sbuf_tensor("DsL4", [128, 512], BF16))
            Grow4 = PE(nc.sbuf_tensor("Grow4", [128, 512], BF16))
            MPOS4 = PE(nc.sbuf_tensor("MPOS4", [128, 512], BF16))
            MT4 = PE(nc.sbuf_tensor("MT4", [128, 512], BF16))
            A4 = PE(nc.sbuf_tensor("A4", [128, 512], BF16))
            W4 = PE(nc.sbuf_tensor("W4", [128, 512], BF16))
            V4 = PE(nc.sbuf_tensor("V4", [128, 512], BF16))
            G4 = PE(nc.sbuf_tensor("G4", [128, 512], BF16))
            Zv4 = PE(nc.sbuf_tensor("Zv4", [128, 512], BF16))
            Y4 = PE(nc.sbuf_tensor("Y4", [128, 512], BF16))
            ot4 = PE(nc.sbuf_tensor("ot4", [128, 512], BF16))
            cT = PE(nc.sbuf_tensor("cT3", [16, 128], F32))
            lnbT = PE(nc.sbuf_tensor("lnbT3", [16, 128], F32))
            cplusT = PE(nc.sbuf_tensor("cplusT3", [16, 128], F32))
            cT4 = PE(nc.sbuf_tensor("cT43", [16, 512], F32))
            msk4 = PE(nc.sbuf_tensor("msk43", [16, 512], F32))
            nmsk4 = PE(nc.sbuf_tensor("nmsk43", [16, 512], F32))
            cols = PE(nc.sbuf_tensor("cols3", [128, 64], F32))
            negcC = PE(nc.sbuf_tensor("negcC3", [128, 16], F32))
            ratC = PE(nc.sbuf_tensor("ratC3", [128, 16], F32))
            bgC = PE(nc.sbuf_tensor("bgC3", [128, 16], F32))
            gCcur = PE(nc.sbuf_tensor("gCcur3", [128, 16], F32))
            clast = PE(nc.sbuf_tensor("clast3", [1, 16], F32))
            s0buf = PE(nc.sbuf_tensor("s0buf3", [128, H, DV], F32))
            s1buf = PE(nc.sbuf_tensor("s1buf3", [128, H, DV], F32))
            pCR = PE(nc.psum_tensor("pCR3", [128, 512], F32))
            pMM = PE(nc.psum_tensor("pMM3", [128, 512], F32))
            pN1 = PE(nc.psum_tensor("pN13", [128, 512], F32))
            pN2 = PE(nc.psum_tensor("pN23", [128, 512], F32))
            pU = PE(nc.psum_tensor("pU3", [128, 512], F32))
            pO = PE(nc.psum_tensor("pO3", [128, 512], F32))
            pS = PE(nc.psum_tensor("pS3", [128, 512], F32))
            pOT = PE(nc.psum_tensor("pOT3", [128, 512], BF16))

            QKS = b.slot_sems("QKS3", 2)
            UVS = b.slot_sems("UVS3", 2)
            OFS = b.slot_sems("OFS3", 2)
            te_qk = [0, 0]
            wr_uv = [0, 0]
            se_uv = [0, 0]
            ve_of = [0, 0]

            ktokv = scr["ktok"].ap()
            vtokv = scr["vtok"].ap()

            def load_qkv(ch, sl):
                if te_qk[sl]:
                    b.wait("sync", TE, te_qk[sl])
                cs = slice(ch * CCH, (ch + 1) * CCH)
                b.dma("sync", qf[:, sl], qfmv[:, :, cs], QKS[sl])
                b.dma("sync", kf[:, sl], kfmv[:, :, cs], QKS[sl])
                b.dma("sync", kt[:, sl],
                      ktokv[:, cs, :].rearrange("h t d -> t h d"), QKS[sl])
                b.dma("sync", vt[:, sl],
                      vtokv[:, cs, :].rearrange("h t d -> t h d"), QKS[sl])

            def load_pass2(ch, sl):
                if wr_uv[sl]:
                    b.wait("sync", VE, wr_uv[sl])
                if te_qk[sl]:
                    b.wait("sync", TE, te_qk[sl])
                if uv_store[sl]:
                    b.wait("sync", UVS[sl], uv_store[sl])
                b.wait("sync", SE, se_uv[sl])
                b.dma("sync", Uv[:, sl], scr["uv_s"].ap()[ch].rearrange("p (h c) -> p h c", h=H), UVS[sl])
                b.dma("sync", WiT[:, sl], scr["wiT_s"].ap()[ch].rearrange("p (h c) -> p h c", h=H), UVS[sl])
                b.dma("sync", W2T[:, sl], scr["w2T_s"].ap()[ch].rearrange("p (h c) -> p h c", h=H), UVS[sl])
                b.dma("sync", Qh[:, sl], scr["qh_s"].ap()[ch].rearrange("p (h c) -> p h c", h=H), UVS[sl])
                b.dma("sync", Kb[:, sl], scr["kb_s"].ap()[ch].rearrange("p (h c) -> p h c", h=H), UVS[sl])
                b.dma("sync", gcc[:, sl], scr["gcc_s"].ap()[ch], UVS[sl])

            def sweep(pass2):
                for ch in range(NCH):
                    sl = ch % 2
                    if not pass2:
                        if ch == 0:
                            load_qkv(0, 0)
                        if ch + 1 < NCH:
                            load_qkv(ch + 1, (ch + 1) % 2)
                        b.wait("vector", QKS[sl], QKS[sl].n)
                        b.wait("tensor", QKS[sl], QKS[sl].n)
                        if uv_store[sl]:
                            b.wait("vector", UVS[sl], uv_store[sl])
                            b.wait("scalar", UVS[sl], uv_store[sl])
                    else:
                        if ch == 0:
                            load_pass2(0, 0)
                        if ch + 1 < NCH:
                            load_pass2(ch + 1, (ch + 1) % 2)
                        b.wait("tensor", UVS[sl], UVS[sl].n)
                        b.wait("vector", UVS[sl], UVS[sl].n)
                    cs = slice(ch * CCH, (ch + 1) * CCH)
                    if not pass2:
                        VR.tensor_tensor_scan(cT[:, :], ones16[:, 0:CCH],
                                              g_res[:, cs], 0.0, ALU.mult, ALU.add)
                        v = b.inc("vector", VE)
                        b.wait("scalar", VE, v)
                        SC.activation(lnbT[:, :], beta_res[:, cs], AF.Ln)
                        sln = b.inc("scalar", SE)
                        b.wait("vector", SE, sln)
                        VR.tensor_add(cplusT[:, :], cT[:, :], lnbT[:, :])
                        for j in range(4):
                            VR.tensor_copy(cT4[:, j * 128:(j + 1) * 128], cT[:, :])
                        v = b.inc("vector", VE)
                        b.wait("tensor", VE, v)
                        TN.transpose(pCR[0:128, 0:16], cT[:, :], identF[0:16, 0:16])
                        TN.transpose(pCR[0:128, 16:32], cplusT[:, :], identF[0:16, 0:16])
                        t = b.inc("tensor", TE)
                        b.wait("vector", TE, t)
                        VR.tensor_copy(cols[:, 0:32], pCR[:, 0:32])
                        v = b.inc("vector", VE)
                        b.wait("tensor", VE, v)
                        TN.transpose(pCR[0:128, 32:48], beta_res[:, cs], identF[0:16, 0:16])
                        TN.transpose(pCR[0:1, 48:64], cT[:, 127:128], identF[0:16, 0:16])
                        t = b.inc("tensor", TE)
                        b.wait("vector", TE, t)
                        VR.tensor_copy(cols[:, 32:48], pCR[:, 32:48])
                        VR.tensor_copy(clast[:, :], pCR[0:1, 48:64])
                        VR.tensor_scalar(negcC[:, :], cols[:, 0:16], -1.0, None, ALU.mult)
                        v = b.inc("vector", VE)
                        b.wait("tensor", VE, v)
                        TN.matmul(pCR[:, 0:16], onesRowF[:], clast[:, :], start=True, stop=True)
                        t = b.inc("tensor", TE)
                        b.wait("vector", TE, t)
                        VR.tensor_sub(cols[:, 48:64], pCR[:, 0:16], cols[:, 0:16])
                        v = b.inc("vector", VE)
                        b.wait("scalar", VE, v)
                        SC.activation(ratC[:, :], cols[:, 48:64], AF.Exp)
                        SC.activation(gCcur[:, :], pCR[:, 0:16], AF.Exp)
                        SC.activation(bgC[:, :], cols[:, 16:32], AF.Exp)
                        sexp = b.inc("scalar", SE)
                        b.wait("vector", SE, sexp)
                        b.wait("tensor", SE, sexp)
                        VR.tensor_copy(gcc[:, sl], gCcur[:, :])
                        v = b.inc("vector", VE)
                        b.wait("scalar", VE, v)
                        b.dma("scalar", scr["gcc_s"].ap()[ch], gcc[:, sl], UVS[sl])
                    else:
                        pass

                    if DBG_OUT and pass2 and ch == 0 and "scomb_dbg" in DBG_OUT:
                        b.wait("scalar", VE, VE.n)
                        b.wait("scalar", UVS[sl], UVS[sl].n)
                        pd = scr["pcdbg"].ap().rearrange("p a c -> p (a c)")
                        b.dma("scalar", pd[:, 0:2048],
                              S0b[:, :, :].rearrange("p a c -> p (a c)"), ST)
                        b.dma("scalar", pd[:, 2048:4096],
                              Uv[:, sl].rearrange("p a c -> p (a c)"), ST)
                        b.dma("scalar", pd[:, 4096:6144],
                              WiT[:, sl].rearrange("p a c -> p (a c)"), ST)
                        b.dma("scalar", pd[:, 6144:8192],
                              W2T[:, sl].rearrange("p a c -> p (a c)"), ST)
                        for e_ in ENGINES:
                            b.wait(e_, ST)
                    for g in range(4):
                        hs = slice(4 * g, 4 * g + 4)
                        if not pass2:
                            VR.tensor_mul(msk4[:, :], cT4[:, :], selm4[:, g, :])
                            VR.tensor_scalar(nmsk4[:, :], msk4[:, :], -1.0, None, ALU.mult)
                            v = b.inc("vector", VE)
                            b.wait("tensor", VE, v)
                            TN.matmul(pCR[:, 0:512], ones16[:, 0:128], msk4[:, :],
                                      start=True, stop=False)
                            t = b.inc("tensor", TE)
                            b.wait("scalar", TE, t)
                            SC.activation(Grow4[:, :], pCR[:, 0:512], AF.Exp)
                            sg_ = b.inc("scalar", SE)
                            b.wait("tensor", SE, sg_)
                            TN.matmul(pCR[:, 0:512], mUIT[:, :], Ib4[:, :],
                                      start=False, stop=True)
                            t = b.inc("tensor", TE)
                            b.wait("scalar", TE, t)
                            for j in range(4):
                                SC.activation(DpT4[:, j * 128:(j + 1) * 128],
                                              pCR[:, j * 128:(j + 1) * 128], AF.Exp,
                                              bias=negcC[:, 4 * g + j:4 * g + j + 1])
                            s_ = b.inc("scalar", SE)
                            b.wait("tensor", SE, s_)
                            TN.matmul(pN1[:, 0:512], ones16[:, 0:128], nmsk4[:, :],
                                      start=True, stop=False)
                            TN.matmul(pN1[:, 0:512], mLST[:, :], Ib4[:, :],
                                      start=False, stop=True)
                            t = b.inc("tensor", TE)
                            b.wait("scalar", TE, t)
                            for j in range(4):
                                SC.activation(DsL4[:, j * 128:(j + 1) * 128],
                                              pN1[:, j * 128:(j + 1) * 128], AF.Exp,
                                              bias=cols[:, 16 + 4 * g + j:17 + 4 * g + j])
                            s_ = b.inc("scalar", SE)
                            b.wait("tensor", SE, s_)
                            for j in range(4):
                                TN.matmul(pMM[:, j * 128:(j + 1) * 128],
                                          kf[:, sl, 4 * g + j, :], kf[:, sl, 4 * g + j, :],
                                          start=(j == 0), stop=(j == 3))
                            t = b.inc("tensor", TE)
                            b.wait("vector", TE, t)
                            VR.tensor_mul(MPOS4[:, :], DsL4[:, :], pMM[:, 0:512])
                            v = b.inc("vector", VE)
                            b.wait("tensor", VE, v)
                            for j in range(4):
                                TN.transpose(pOT[:, j * 128:(j + 1) * 128],
                                             MPOS4[:, j * 128:(j + 1) * 128], identB[:, :])
                            t = b.inc("tensor", TE)
                            b.wait("scalar", TE, t)
                            SC.copy(MT4[:, :], pOT[:, 0:512])
                            s_ = b.inc("scalar", SE)
                            b.wait("vector", SE, s_)
                            VR.tensor_add(A4[:, :], MPOS4[:, :], Ib4[:, :])
                            VR.scalar_tensor_tensor(W4[:, :], MT4[:, :], -1.0, Ib4[:, :],
                                                    ALU.mult, ALU.add)
                            VR.scalar_tensor_tensor(V4[:, :], MPOS4[:, :], -1.0, Ib4[:, :],
                                                    ALU.mult, ALU.add)
                            v = b.inc("vector", VE)
                            b.wait("tensor", VE, v)
                            for j in range(4):
                                TN.matmul(pMM[:, j * 128:(j + 1) * 128],
                                          kf[:, sl, 4 * g + j, :], qf[:, sl, 4 * g + j, :],
                                          start=(j == 0), stop=(j == 3))
                            t = b.inc("tensor", TE)
                            b.wait("vector", TE, t)
                            VR.tensor_mul(W2T[:, sl, hs, :].rearrange("p a c -> p (a c)"),
                                          DpT4[:, :], pMM[:, 0:512])
                            VR.tensor_mul(Qh[:, sl, hs, :].rearrange("p a c -> p (a c)"),
                                          qf[:, sl, hs, :].rearrange("p a c -> p (a c)"),
                                          Grow4[:, :])
                            for j in range(4):
                                VR.tensor_scalar(Zv4[:, j * 128:(j + 1) * 128],
                                                 vt[:, sl, 4 * g + j, :],
                                                 cols[:, 32 + 4 * g + j:33 + 4 * g + j],
                                                 None, ALU.mult)
                                VR.tensor_scalar(Y4[:, j * 128:(j + 1) * 128],
                                                 kt[:, sl, 4 * g + j, :],
                                                 bgC[:, 4 * g + j:4 * g + j + 1],
                                                 None, ALU.mult)
                                VR.tensor_scalar(Kb[:, sl, 4 * g + j, :],
                                                 kt[:, sl, 4 * g + j, :],
                                                 ratC[:, 4 * g + j:4 * g + j + 1],
                                                 None, ALU.mult)
                            v = b.inc("vector", VE)
                            b.wait("tensor", VE, v)
                            for it_ in range(6):
                                for j in range(4):
                                    TN.matmul(pN1[:, j * 128:(j + 1) * 128],
                                              A4[:, j * 128:(j + 1) * 128],
                                              W4[:, j * 128:(j + 1) * 128],
                                              start=(j == 0), stop=(j == 3))
                                t = b.inc("tensor", TE)
                                b.wait("vector", TE, t)
                                VR.scalar_tensor_tensor(G4[:, :], pN1[:, 0:512], -1.0,
                                                        twoIb4[:, :], ALU.mult, ALU.add)
                                v = b.inc("vector", VE)
                                b.wait("tensor", VE, v)
                                for j in range(4):
                                    TN.matmul(pN2[:, j * 128:(j + 1) * 128],
                                              V4[:, j * 128:(j + 1) * 128],
                                              G4[:, j * 128:(j + 1) * 128],
                                              start=(j == 0), stop=(j == 3))
                                for j in range(4):
                                    TN.matmul(pN1[:, j * 128:(j + 1) * 128],
                                              G4[:, j * 128:(j + 1) * 128],
                                              V4[:, j * 128:(j + 1) * 128],
                                              start=(j == 0), stop=(j == 3))
                                t = b.inc("tensor", TE)
                                b.wait("vector", TE, t)
                                b.wait("scalar", TE, t)
                                SC.copy(W4[:, :], pN2[:, 0:512])
                                s_ = b.inc("scalar", SE)
                                if it_ < 5:
                                    VR.tensor_copy(V4[:, :], pN1[:, 0:512])
                                    v = b.inc("vector", VE)
                                    b.wait("tensor", VE, v)
                                b.wait("tensor", SE, s_)
                                b.wait("vector", SE, s_)
                            for j in range(4):
                                TN.matmul(pN2[:, j * 128:(j + 1) * 128],
                                          W4[:, j * 128:(j + 1) * 128],
                                          Zv4[:, j * 128:(j + 1) * 128],
                                          start=(j == 0), stop=(j == 3))
                            for j in range(4):
                                TN.matmul(pN1[:, j * 128:(j + 1) * 128],
                                          Y4[:, j * 128:(j + 1) * 128],
                                          W4[:, j * 128:(j + 1) * 128],
                                          start=(j == 0), stop=(j == 3))
                            t = b.inc("tensor", TE)
                            b.wait("vector", TE, t)
                            b.wait("scalar", TE, t)
                            VR.tensor_copy(Uv[:, sl, hs, :].rearrange("p a c -> p (a c)"),
                                           pN2[:, 0:512])
                            SC.copy(WiT[:, sl, hs, :].rearrange("p a c -> p (a c)"),
                                    pN1[:, 0:512])
                            v = b.inc("vector", VE)
                            s_ = b.inc("scalar", SE)
                            b.wait("tensor", VE, v)
                            b.wait("tensor", SE, s_)
                        for j in range(4):
                            TN.matmul(pU[:, j * 128:(j + 1) * 128],
                                      WiT[:, sl, 4 * g + j, :], S0b[:, 4 * g + j, :],
                                      start=(j == 0), stop=(j == 3))
                        t = b.inc("tensor", TE)
                        b.wait("vector", TE, t)
                        VR.scalar_tensor_tensor(Ua[:, hs, :].rearrange("p a c -> p (a c)"),
                                                pU[:, 0:512], -1.0,
                                                Uv[:, sl, hs, :].rearrange("p a c -> p (a c)"),
                                                ALU.mult, ALU.add)
                        v = b.inc("vector", VE)
                        b.wait("tensor", VE, v)
                        for j in range(4):
                            TN.matmul(pO[:, j * 128:(j + 1) * 128],
                                      Qh[:, sl, 4 * g + j, :], S0b[:, 4 * g + j, :],
                                      start=(j == 0), stop=False)
                            TN.matmul(pO[:, j * 128:(j + 1) * 128],
                                      W2T[:, sl, 4 * g + j, :], Ua[:, 4 * g + j, :],
                                      start=False, stop=(j == 3))
                        t = b.inc("tensor", TE)
                        b.wait("vector", TE, t)
                        VR.tensor_copy(ot4[:, :], pO[:, 0:512])
                        v = b.inc("vector", VE)
                        b.wait("tensor", VE, v)
                        for j in range(4):
                            TN.transpose(pOT[:, j * 128:(j + 1) * 128],
                                         ot4[:, j * 128:(j + 1) * 128], identB[:, :])
                        for j in range(4):
                            TN.matmul(pS[:, j * 128:(j + 1) * 128],
                                      Kb[:, sl, 4 * g + j, :], Ua[:, 4 * g + j, :],
                                      start=(j == 0), stop=(j == 3))
                        t = b.inc("tensor", TE)
                        b.wait("scalar", TE, t)
                        b.wait("vector", TE, t)
                        if ve_of[sl] and g == 0:
                            b.wait("scalar", OFS[sl], ve_of[sl])
                        SC.copy(ofmb[:, sl, hs, :].rearrange("p a c -> p (a c)"),
                                pOT[:, 0:512])
                        s_ = b.inc("scalar", SE)
                        for j in range(4):
                            VR.tensor_scalar(S[:, 4 * g + j, :], S[:, 4 * g + j, :],
                                             gCcur[:, 4 * g + j:4 * g + j + 1] if not pass2
                                             else gcc[:, sl, 4 * g + j:4 * g + j + 1],
                                             None, ALU.mult)
                        VR.tensor_add(S[:, hs, :].rearrange("p a c -> p (a c)"),
                                      S[:, hs, :].rearrange("p a c -> p (a c)"),
                                      pS[:, 0:512])
                        v = b.inc("vector", VE)
                        b.wait("tensor", VE, v)
                        b.wait("scalar", VE, v)
                    if DBG_OUT and pass2 and ch == 0 and g == 3:
                        b.wait("scalar", VE, VE.n)
                        b.dma("scalar", scr["pcdbg"].ap().rearrange("p a c -> p (a c)")[:, 0:2048]
                              .rearrange("p (a c) -> p a c", a=16),
                              Ua[:, :, :], ST)
                        b.dma("scalar", scr["pcdbg"].ap().rearrange("p a c -> p (a c)")[:, 2048:4096]
                              .rearrange("p (a c) -> p a c", a=16),
                              S0b[:, :, :], ST)
                        b.wait("scalar", ST)
                    VR.tensor_copy(S0b[:, :, :], S[:, :, :])
                    v_s0 = b.inc("vector", VE)
                    b.wait("tensor", VE, v_s0)
                    b.wait("scalar", SE, SE.n)
                    ve_of[sl] = b.dma("scalar", ofmv[:, :, cs], ofmb[:, sl], OFS[sl])
                    if not pass2:
                        b.wait("scalar", VE, VE.n)
                        b.wait("scalar", SE, SE.n)
                        for nm, tns in (("uv_s", Uv), ("wiT_s", WiT), ("w2T_s", W2T),
                                        ("qh_s", Qh), ("kb_s", Kb)):
                            b.dma("scalar", scr[nm].ap()[ch],
                                  tns[:, sl].rearrange("p a c -> p (a c)"), UVS[sl])
                        uv_store[sl] = UVS[sl].n
                        wr_uv[sl] = VE.n
                        te_qk[sl] = TE.n
                        se_uv[sl] = SE.n
                    else:
                        wr_uv[sl] = VE.n
                        te_qk[sl] = TE.n
                        se_uv[sl] = SE.n

            uv_store = [0, 0]
            VR.memset(S[:, :, :], 0.0)
            v0 = b.inc("vector", VE)
            b.wait("vector", VE, v0)
            VR.tensor_copy(S0b[:, :, :], S[:, :, :])
            b.inc("vector", VE)
            b.wait("tensor", VE, VE.n)
            b.wait("scalar", VE, VE.n)
            sweep(False)
            b.wait("scalar", VE, VE.n)
            b.dma("scalar", scr["cc_in"].ap(), S[:, :, :].rearrange("p a c -> p (a c)"), ST)
            b.wait("gpsimd", ST)
            GP.collective_compute(
                "AllGather", ALU.bypass,
                replica_groups=[[0, 1], [2, 3], [4, 5], [6, 7]],
                ins=[scr["cc_in"].ap()], outs=[scr["cc_out"].ap()],
            )
            b.inc("gpsimd", b.sems["CC"])
            b.wait("gpsimd", b.sems["CC"], b.sems["CC"].n)
            b.dma("gpsimd", s0buf[:, :, :],
                  scr["cc_out"].ap()[0:128].rearrange("p (a c) -> p a c", a=H), CA)
            b.dma("gpsimd", s1buf[:, :, :],
                  scr["cc_out"].ap()[128:256].rearrange("p (a c) -> p a c", a=H), CA)
            b.wait("vector", CA)
            b.wait("vector", OFS[0]); b.wait("vector", OFS[1])
            b.wait("vector", UVS[0]); b.wait("vector", UVS[1])
            VR.tensor_scalar(S[:, :, :], s0buf[:, :, :], mA[:, 0:1], None, ALU.mult)
            VR.tensor_scalar(s1buf[:, :, :], s1buf[:, :, :], mB[:, 0:1], None, ALU.mult)
            VR.tensor_add(S[:, :, :], S[:, :, :], s1buf[:, :, :])
            VR.tensor_copy(S0b[:, :, :], S[:, :, :])
            v = b.inc("vector", VE)
            b.wait("tensor", VE, v)
            b.wait("scalar", VE, v)
            if DBG_OUT:
                b.dma("scalar", scr["scomb_dbg"].ap(),
                      S[:, :, :].rearrange("p a c -> p (a c)"), ST)
                b.wait("scalar", ST)
            if not P3_SWEEP1_ONLY:
                sweep(True)
            b.barrier()

    def owo_phase(hga_res):
        with ExitStack() as ps:
            PE = ps.enter_context
            op_ = PE(nc.sbuf_tensor("op4", [128, NK, 512], BF16))
            x1p = PE(nc.sbuf_tensor("x1p4", [128, NK, 512], F32))
            ap_ = PE(nc.sbuf_tensor("ap4", [128, NK, 512], F32))
            sq4 = PE(nc.sbuf_tensor("sq4", [128, 2, NK, 128], F32))
            ogb = PE(nc.sbuf_tensor("ogb4", [128, NK, 512], BF16))
            sgall = PE(nc.sbuf_tensor("sgall4", [128, NK, 512], BF16))
            sqo = PE(nc.sbuf_tensor("sqo4", [128, 4, 512], F32))
            wgb_s = PE(nc.sbuf_tensor("wgb_s4", [128, D], BF16))
            colw64 = PE(nc.sbuf_tensor("colw644", [128, 64], F32))
            lrow4 = PE(nc.sbuf_tensor("lrow44", [4, 2, 128], F32))
            onw4 = PE(nc.sbuf_tensor("onw44", [4, 4, 128], F32))
            wob = PE(nc.sbuf_tensor("wob4", [128, 2, 512], BF16))
            stat = PE(nc.sbuf_tensor("stat4", [128, 32], F32))
            colw = PE(nc.sbuf_tensor("colw4", [128, 16], F32))
            rowt = PE(nc.sbuf_tensor("rowt4", [16, 128], F32))
            pA = PE(nc.psum_tensor("pA4", [128, 512], F32))
            pB = PE(nc.psum_tensor("pB4", [128, 512], F32))
            pC = PE(nc.psum_tensor("pC4", [128, 512], F32))
            pD = PE(nc.psum_tensor("pD4", [128, 512], F32))
            pE = PE(nc.psum_tensor("pE4", [128, 2048], F32))

            WOS = b.slot_sems("WOS4", 2)
            te_wo = [0, 0]
            te_sq4 = [0, 0]
            b.dma("gpsimd", wgb_s[:, :], ap["wgb"], CA)
            b.dma("sync", onw4[:, :, :], ap["onw4"], LD)
            b.wait("tensor", CA); b.wait("tensor", LD)
            se_g = {}
            ve_og = {}
            te_sqo = [0, 0, 0, 0]

            st_prev = 0
            for ptag in range(4):
                T = 512
                ntile = nt = 4
                b.wait("vector", ST, st_prev)
                b.wait("scalar", ST, st_prev)
                b.wait("sync", TE, TE.n); b.wait("sync", VE, VE.n)
                cs = slice(ptag * 512, (ptag + 1) * 512)
                b.dma("sync", op_[:, :, :], ofmv[:, :, cs], LD)
                b.dma("sync", x1p[:, :, :], x1v[:, :, THALO + ptag * 512:THALO + (ptag + 1) * 512], LD)
                b.wait("scalar", LD); b.wait("tensor", LD); b.wait("vector", LD)
                b.wait("tensor", SE, SE.n)
                b.wait("tensor", VE, VE.n)
                for r in range(ntile):
                    if te_sq4[r % 2]:
                        b.wait("scalar", TE, te_sq4[r % 2])
                    SC.activation(sq4[:, r % 2, :, :], x1p[:, :, r * 128:(r + 1) * 128], AF.Square)
                    s = b.inc("scalar", SE)
                    b.wait("tensor", SE, s)
                    for k in range(NK):
                        TN.matmul(pC[:, r:r + 1], x1p[:, k, r * 128:(r + 1) * 128],
                                  onesColF[:], start=(r == 0 and k == 0), stop=False)
                        TN.matmul(pC[:, 8 + r:9 + r], sq4[:, r % 2, k, :], onesColF[:],
                                  start=False, stop=(r == ntile - 1 and k == NK - 1))
                    te_sq4[r % 2] = b.inc("tensor", TE)
                b.wait("vector", TE, TE.n)
                VR.memset(colw[:, :], 0.0)
                VR.tensor_copy(stat[:, 0:nt], pC[:, 0:nt])
                VR.tensor_copy(stat[:, 8:8 + nt], pC[:, 8:8 + nt])
                VR.tensor_scalar(colw[:, 0:nt], stat[:, 0:nt], 1.0 / D, None, ALU.mult)
                VR.tensor_mul(colw[:, 8:8 + nt], colw[:, 0:nt], colw[:, 0:nt])
                VR.scalar_tensor_tensor(colw[:, 8:8 + nt], stat[:, 8:8 + nt], 1.0 / D,
                                        colw[:, 8:8 + nt], ALU.mult, ALU.subtract)
                v = b.inc("vector", VE)
                b.wait("scalar", VE, v)
                SC.activation(colw[:, 8:8 + nt], colw[:, 8:8 + nt], AF.Sqrt, bias=np.float32(1e-5))
                s = b.inc("scalar", SE)
                b.wait("vector", SE, s)
                VR.reciprocal(colw[:, 8:8 + nt], colw[:, 8:8 + nt])
                VR.tensor_mul(colw[:, 0:nt], colw[:, 0:nt], colw[:, 8:8 + nt])
                VR.tensor_scalar(colw[:, 0:nt], colw[:, 0:nt], -1.0, None, ALU.mult)
                v = b.inc("vector", VE)
                b.wait("tensor", VE, v)
                TN.transpose(pC[0:16, 0:128], colw[:, 0:16], identF[:, :])
                t = b.inc("tensor", TE)
                b.wait("vector", TE, t)
                VR.tensor_copy(rowt[:, :], pC[0:16, 0:128])
                v = b.inc("vector", VE)
                b.wait("tensor", VE, v)
                for r in range(ntile):
                    TN.matmul(pA[:, r * 128:(r + 1) * 128],
                              sel16[:, (8 + r) * 128:(9 + r) * 128], rowt[:, :],
                              start=(r == 0), stop=(r == ntile - 1))
                    TN.matmul(pB[:, r * 128:(r + 1) * 128],
                              sel16[:, r * 128:(r + 1) * 128], rowt[:, :],
                              start=(r == 0), stop=(r == ntile - 1))
                t = b.inc("tensor", TE)
                b.wait("vector", TE, t)
                for k in range(NK):
                    VR.tensor_mul(ap_[:, k, :], x1p[:, k, :], pA[:, 0:512])
                    VR.tensor_add(ap_[:, k, :], ap_[:, k, :], pB[:, 0:512])
                v_a = b.inc("vector", VE)
                b.wait("tensor", VE, v_a)
                for m in range(NK):
                    pX = pA if m % 2 == 0 else pB
                    if m >= 2:
                        b.wait("tensor", SE, se_g[m - 2])
                    TN.matmul(pX[:, 0:512], wgb_s[:, m * 128:(m + 1) * 128],
                              hga_res[:, cs], start=True, stop=True)
                    t = b.inc("tensor", TE)
                    b.wait("scalar", TE, t)
                    SC.activation(sgall[:, m, :], pX[:, 0:512], AF.Silu)
                    se_g[m] = b.inc("scalar", SE)
                for m in range(NK):
                    if m >= 4 and te_sqo[m % 4]:
                        b.wait("scalar", TE, te_sqo[m % 4])
                    SC.activation(sqo[:, m % 4, :], op_[:, m, :], AF.Square)
                    sse = b.inc("scalar", SE)
                    b.wait("tensor", SE, sse)
                    for r in range(4):
                        TN.matmul(pC[:, 4 * m + r:4 * m + r + 1],
                                  sqo[:, m % 4, r * 128:(r + 1) * 128], onesColF[:],
                                  start=(m == 0 and r == 0),
                                  stop=(m == NK - 1 and r == 3))
                    te_rms = te_sqo[m % 4] = b.inc("tensor", TE)
                b.wait("vector", TE, te_rms)
                VR.tensor_copy(colw64[:, :], pC[:, 0:64])
                v = b.inc("vector", VE)
                b.wait("scalar", VE, v)
                SC.activation(colw64[:, :], colw64[:, :], AF.Sqrt,
                              bias=np.float32(1e-6), scale=np.float32(1.0 / 128.0))
                srms = b.inc("scalar", SE)
                b.wait("vector", SE, srms)
                VR.reciprocal(colw64[:, :], colw64[:, :])
                v = b.inc("vector", VE)
                b.wait("tensor", VE, v)
                b.wait("tensor", SE, SE.n)
                for m in range(NK):
                    pX = pA if m % 2 == 0 else pB
                    TN.transpose(pD[0:4, (m % 2) * 128:(m % 2) * 128 + 128],
                                 colw64[:, 4 * m:4 * m + 4], identF[:, :])
                    t = b.inc("tensor", TE)
                    b.wait("vector", TE, t)
                    VR.tensor_copy(lrow4[:, m % 2, :], pD[0:4, (m % 2) * 128:(m % 2) * 128 + 128])
                    vlr = b.inc("vector", VE)
                    b.wait("tensor", VE, vlr)
                    if m >= 2:
                        b.wait("tensor", VE, ve_og[m - 2])
                    for r in range(4):
                        TN.matmul(pX[:, r * 128:(r + 1) * 128],
                                  onw4[:, r, :], lrow4[:, m % 2, :],
                                  start=(r == 0), stop=(r == 3))
                    t = b.inc("tensor", TE)
                    b.wait("vector", TE, t)
                    VR.tensor_mul(ogb[:, m, :], op_[:, m, :], pX[:, 0:512])
                    VR.tensor_mul(ogb[:, m, :], ogb[:, m, :], sgall[:, m, :])
                    ve_og[m] = b.inc("vector", VE)
                b.wait("tensor", VE, VE.n)
                for dq in range(4):
                    for k in range(NK):
                        ksl = k % 2
                        if te_wo[ksl]:
                            b.wait("gpsimd", TE, te_wo[ksl])
                        b.dma("gpsimd", wob[:, ksl, :],
                              ap["wo"][k * 128:(k + 1) * 128, dq * 512:(dq + 1) * 512],
                              WOS[ksl])
                        b.wait("tensor", WOS[ksl], WOS[ksl].n)
                        for dd in range(4):
                            TN.matmul(pE[:, dd * 512:dd * 512 + 512],
                                      wob[:, ksl, dd * 128:(dd + 1) * 128],
                                      ogb[:, k, :], start=(k == 0), stop=(k == NK - 1))
                        te_wo[ksl] = b.inc("tensor", TE)
                    b.wait("vector", TE, TE.n)
                    for dd in range(4):
                        d_ = dq * 4 + dd
                        VR.scalar_tensor_tensor(ap_[:, d_, :], pE[:, dd * 512:dd * 512 + 512],
                                                1.0, ap_[:, d_, :], ALU.mult, ALU.add)
                    ve_pe = b.inc("vector", VE)
                    b.wait("tensor", VE, ve_pe)
                b.wait("scalar", VE, VE.n)
                st_prev = b.dma("scalar", xiv[:, :, cs], ap_[:, :, :], ST)
            b.barrier()

    def mlp_up_phase():
        with ExitStack() as ps:
            PE = ps.enter_context
            h2f = PE(nc.sbuf_tensor("h2f5", [128, NK, TOWN], BF16))
            xip = PE(nc.sbuf_tensor("xip5", [128, NK, 512], F32))
            sq5 = PE(nc.sbuf_tensor("sq5", [128, 2, NK, 128], F32))
            wgb = PE(nc.sbuf_tensor("wgb5", [128, 2, NK, 128], BF16))
            wub = PE(nc.sbuf_tensor("wub5", [128, 2, NK, 128], BF16))
            sgl = PE(nc.sbuf_tensor("sgl5", [128, 2048], BF16))
            mb = PE(nc.sbuf_tensor("mb5", [128, 2, 2048], BF16))
            stat = PE(nc.sbuf_tensor("stat5", [128, 32], F32))
            colw = PE(nc.sbuf_tensor("colw5", [128, 16], F32))
            rowt = PE(nc.sbuf_tensor("rowt5", [16, 128], F32))
            ln2c = PE(nc.sbuf_tensor("ln2c5", [128, 16], F32))
            pG = PE(nc.psum_tensor("pG5", [128, 2048], F32))
            pU = PE(nc.psum_tensor("pU5", [128, 2048], F32))
            WG5 = b.slot_sems("WG5", 2)
            MS5 = b.slot_sems("MS5", 2)
            te_w5 = [0, 0]
            ve_m5 = [0, 0]
            b.dma("sync", ln2c[:, :], ap["ln2_wT"], LD)
            b.wait("vector", LD)
            for ptag in range(4):
                nt = 4
                b.wait("sync", TE, TE.n); b.wait("sync", VE, VE.n)
                b.dma("sync", xip[:, :, :], xiv[:, :, ptag * 512:(ptag + 1) * 512], LD)
                b.wait("scalar", LD); b.wait("tensor", LD); b.wait("vector", LD)
                b.wait("tensor", SE, SE.n); b.wait("tensor", VE, VE.n)
                te_sq5 = {}
                for r in range(4):
                    if r >= 2:
                        b.wait("scalar", TE, te_sq5[r - 2])
                    SC.activation(sq5[:, r % 2, :, :], xip[:, :, r * 128:(r + 1) * 128], AF.Square)
                    s = b.inc("scalar", SE)
                    b.wait("tensor", SE, s)
                    for k in range(NK):
                        TN.matmul(pG[:, 8 + r:9 + r], sq5[:, r % 2, k, :], onesColF[:],
                                  start=(r == 0 and k == 0),
                                  stop=(r == 3 and k == NK - 1))
                    te_sq5[r] = b.inc("tensor", TE)
                b.wait("vector", TE, TE.n)
                VR.memset(colw[:, :], 0.0)
                VR.tensor_copy(stat[:, 8:8 + nt], pG[:, 8:8 + nt])
                v = b.inc("vector", VE)
                b.wait("scalar", VE, v)
                SC.activation(colw[:, 8:8 + nt], stat[:, 8:8 + nt], AF.Sqrt,
                              bias=np.float32(1e-6), scale=np.float32(1.0 / D))
                s = b.inc("scalar", SE)
                b.wait("vector", SE, s)
                VR.reciprocal(colw[:, 8:8 + nt], colw[:, 8:8 + nt])
                v = b.inc("vector", VE)
                b.wait("tensor", VE, v)
                TN.transpose(pG[0:16, 0:128], colw[:, 0:16], identF[:, :])
                t = b.inc("tensor", TE)
                b.wait("vector", TE, t)
                VR.tensor_copy(rowt[:, :], pG[0:16, 0:128])
                v = b.inc("vector", VE)
                b.wait("tensor", VE, v)
                for r in range(4):
                    TN.matmul(pU[:, r * 128:(r + 1) * 128],
                              sel16[:, (8 + r) * 128:(9 + r) * 128], rowt[:, :],
                              start=(r == 0), stop=(r == 3))
                t = b.inc("tensor", TE)
                b.wait("vector", TE, t)
                for k in range(NK):
                    VR.scalar_tensor_tensor(h2f[:, k, ptag * 512:(ptag + 1) * 512],
                                            xip[:, k, :], ln2c[:, k:k + 1],
                                            pU[:, 0:512], ALU.mult, ALU.mult)
                v = b.inc("vector", VE)
                b.wait("tensor", VE, v)
            wgv = ap["w_gate"].rearrange("(k r) i -> r k i", r=128)
            wuv = ap["w_up"].rearrange("(k r) i -> r k i", r=128)
            for i in range(64):
                isl = i % 2
                if te_w5[isl]:
                    b.wait("gpsimd", TE, te_w5[isl])
                b.dma("gpsimd", wgb[:, isl], wgv[:, :, i * 128:(i + 1) * 128], WG5[isl])
                b.dma("gpsimd", wub[:, isl], wuv[:, :, i * 128:(i + 1) * 128], WG5[isl])
                b.wait("tensor", WG5[isl], WG5[isl].n)
                for tg in range(4):
                    for k in range(NK):
                        TN.matmul(pG[:, tg * 512:(tg + 1) * 512], wgb[:, isl, k, :],
                                  h2f[:, k, tg * 512:(tg + 1) * 512],
                                  start=(k == 0), stop=(k == NK - 1))
                for tg in range(4):
                    for k in range(NK):
                        TN.matmul(pU[:, tg * 512:(tg + 1) * 512], wub[:, isl, k, :],
                                  h2f[:, k, tg * 512:(tg + 1) * 512],
                                  start=(k == 0), stop=(k == NK - 1))
                te_w5[isl] = b.inc("tensor", TE)
                b.wait("scalar", TE, te_w5[isl])
                SC.activation(sgl[:, :], pG[:, 0:2048], AF.Silu)
                s = b.inc("scalar", SE)
                b.wait("vector", SE, s)
                if ve_m5[isl]:
                    b.wait("vector", MS5[isl], ve_m5[isl])
                VR.tensor_mul(mb[:, isl, :], sgl[:, :], pU[:, 0:2048])
                v = b.inc("vector", VE)
                b.wait("tensor", VE, v)
                b.wait("scalar", VE, v)
                ve_m5[isl] = b.dma("scalar", scr["mscr"].ap()[i], mb[:, isl, :], MS5[isl])
            b.barrier()

    def mlp_down_phase():
        with ExitStack() as ps:
            PE = ps.enter_context
            mb6 = PE(nc.sbuf_tensor("mb65", [128, 64, 1024], BF16))
            wdb = PE(nc.sbuf_tensor("wdb6", [128, 2, 32, 128], BF16))
            x1t = PE(nc.sbuf_tensor("x1t6", [128, 2, 1024], F32))
            x2b = PE(nc.sbuf_tensor("x2b6", [128, 2, 1024], F32))
            pD1 = PE(nc.psum_tensor("pD16", [128, 1024], F32))
            pD2 = PE(nc.psum_tensor("pD26", [128, 1024], F32))
            MT6 = b.slot_sems("MT6", 1)
            WD6 = b.slot_sems("WD6", 2)
            XT6 = b.slot_sems("XT6", 2)
            X26 = b.slot_sems("X26", 2)
            te_wd = [0, 0]
            ve_add = [0, 0]
            st_x2 = [0, 0]
            wdv = ap["w_down"].rearrange("(i r) d -> r i d", r=128)
            mscv = scr["mscr"].ap()
            for tg in range(2):
                b.wait("sync", TE, TE.n)
                b.dma("sync", mb6[:, :, :],
                      mscv[:, :, tg * 1024:(tg + 1) * 1024].rearrange("i p t -> p i t"),
                      MT6[0])
                b.wait("tensor", MT6[0], MT6[0].n)
                for d_ in range(16):
                    dsl = d_ % 2
                    pDx = pD1 if dsl == 0 else pD2
                    if ve_add[dsl]:
                        b.wait("gpsimd", VE, ve_add[dsl])
                    b.dma("gpsimd", x1t[:, dsl, :],
                          x1v[:, d_, THALO + tg * 1024:THALO + (tg + 1) * 1024], XT6[dsl])
                    b.wait("gpsimd", XT6[dsl], XT6[dsl].n)
                    GP.dma_start(out=x1t[:, dsl, :],
                                 in_=xiv[:, d_, tg * 1024:(tg + 1) * 1024],
                                 accum_op=ALU.add).then_inc(XT6[dsl].h, 16)
                    XT6[dsl].n += 16
                    for ih in range(2):
                        wsl = (2 * d_ + ih) % 2
                        if te_wd[wsl]:
                            b.wait("gpsimd", TE, te_wd[wsl])
                        b.dma("gpsimd", wdb[:, wsl],
                              wdv[:, ih * 32:(ih + 1) * 32, d_ * 128:(d_ + 1) * 128],
                              WD6[wsl])
                        b.wait("tensor", WD6[wsl], WD6[wsl].n)
                        if ih == 0 and ve_add[dsl]:
                            b.wait("tensor", VE, ve_add[dsl])
                        for i in range(32):
                            for gg_ in range(2):
                                TN.matmul(pDx[:, gg_ * 512:(gg_ + 1) * 512],
                                          wdb[:, wsl, i, :],
                                          mb6[:, ih * 32 + i, gg_ * 512:(gg_ + 1) * 512],
                                          start=(ih == 0 and i == 0),
                                          stop=(ih == 1 and i == 31))
                        te_wd[wsl] = b.inc("tensor", TE)
                    b.wait("vector", TE, te_wd[(2 * d_ + 1) % 2])
                    b.wait("vector", XT6[dsl], XT6[dsl].n)
                    if st_x2[dsl]:
                        b.wait("vector", X26[dsl], st_x2[dsl])
                    VR.tensor_add(x2b[:, dsl, :], x1t[:, dsl, :], pDx[:, 0:1024])
                    ve_add[dsl] = b.inc("vector", VE)
                    b.wait("scalar", VE, ve_add[dsl])
                    st_x2[dsl] = b.dma("scalar", x2v[:, d_, tg * 1024:(tg + 1) * 1024],
                                       x2b[:, dsl, :], X26[dsl])
            b.barrier()

    if "p1" in PHASES:
        ffn_phase("p1")
    if "p2" in PHASES or "p3" in PHASES or "p4" in PHASES:
        with ExitStack() as p24:
            g_res = p24.enter_context(nc.sbuf_tensor("g_res", [16, TOWN], F32))
            beta_res = p24.enter_context(nc.sbuf_tensor("beta_res", [16, TOWN], F32))
            hga_res = p24.enter_context(nc.sbuf_tensor("hga_res", [128, TOWN], BF16))
            if "p2" in PHASES:
                proj_phase(g_res, beta_res, hga_res)
            if "p3" in PHASES:
                scan_phase(g_res, beta_res)
            if "p4" in PHASES:
                owo_phase(hga_res)
                if DBG_OUT:
                    b.dma("scalar", scr["g_dbg"].ap(), g_res[:, :], ST)
                    b.dma("scalar", scr["beta_dbg"].ap(), beta_res[:, :], ST)
                    b.dma("scalar", scr["hga_dbg"].ap(), hga_res[:, :], ST)
                    b.wait("scalar", ST)
    if "p5" in PHASES:
        mlp_up_phase()
    if "p6" in PHASES:
        mlp_down_phase()
    if "p7" in PHASES:
        ffn_phase("p7")

    b.wait("gpsimd", ST)
    for name, out_ap in dbg_outs.items():
        b.dma("gpsimd", out_ap, scr[name].ap(), CA)
    b.wait("gpsimd", CA)
    b.wait("scalar", ST)

    b.finalize_hazards()
    with nc.Block() as block:
        block.sync(lambda e: b.r["sync"].replay(e))
        block.scalar(lambda e: b.r["scalar"].replay(e))
        block.tensor(lambda e: b.r["tensor"].replay(e))
        block.vector(lambda e: b.r["vector"].replay(e))
        block.gpsimd(lambda e: b.r["gpsimd"].replay(e))

    stack.close()
    return nc


def _consts():
    c = {}
    c["identF"] = np.eye(128, dtype=np.float32)
    c["identB"] = np.eye(128, dtype=np.float32).astype(ml_dtypes.bfloat16)
    c["onesColF"] = np.ones((128, 1), np.float32)
    c["onesRowF"] = np.ones((1, 128), np.float32)
    c["ones16"] = np.ones((16, CCH), np.float32)
    s = np.arange(128)[:, None]
    t = np.arange(128)[None, :]
    c["mUIT"] = np.where(t <= s, 0.0, -1e9).astype(ml_dtypes.bfloat16)
    c["mLST"] = np.where(s < t, 0.0, -1e9).astype(ml_dtypes.bfloat16)
    sm = np.zeros((16, 4, 512), np.float32)
    for g in range(4):
        for j in range(4):
            sm[4 * g + j, g, 128 * j:128 * (j + 1)] = 1.0
    c["selm4"] = sm
    c["Ib4"] = np.tile(np.eye(128, dtype=np.float32), (1, 4)).astype(ml_dtypes.bfloat16)
    c["twoIb4"] = np.tile(2.0 * np.eye(128, dtype=np.float32), (1, 4)).copy()
    e8 = np.zeros((8, 128), np.float32)
    for e in range(8):
        e8[e, 16 * e:16 * e + 16] = 1.0
    c["E8"] = e8.astype(ml_dtypes.bfloat16)
    c["sel16"] = np.kron(np.eye(16, dtype=np.float32), np.ones((1, 128), np.float32))
    sel4 = np.zeros((4, 4, 128), np.float32)
    for r in range(4):
        sel4[r, r, :] = 1.0
    c["sel4"] = sel4
    return c


_NC_CACHE = {}


def kernel(**inputs):
    key = (tuple(PHASES), tuple(DBG_OUT))
    if key not in _NC_CACHE:
        _NC_CACHE[key] = build_program()
    nc = _NC_CACHE[key]

    x = np.ascontiguousarray(np.asarray(inputs["L0_attn_x"], np.float32))
    Bsz, T, _ = x.shape
    common = _consts()
    ren = {
        "wq": "L0_attn_wq", "wk": "L0_attn_wk", "wv": "L0_attn_wv", "wo": "L0_attn_wo",
        "wfa": "L0_attn_wfa", "wfb": "L0_attn_wfb", "wb": "L0_attn_wb",
        "wga": "L0_attn_wga", "wgb": "L0_attn_wgb",
        "w_gate": "L0_attn_w_gate", "w_up": "L0_attn_w_up", "w_down": "L0_attn_w_down",
    }
    for k, src in ren.items():
        common[k] = np.ascontiguousarray(np.asarray(inputs[src], np.float32))
    common["a_log"] = np.asarray(inputs["L0_attn_a_log"], np.float32).reshape(16, 1).copy()
    common["dt_bias"] = np.asarray(inputs["L0_attn_dt_bias"], np.float32).reshape(16, 1).copy()
    common["o_norm_w"] = np.asarray(inputs["L0_attn_o_norm_w"], np.float32).reshape(1, 128).copy()
    onw4 = np.zeros((4, 4, 128), np.float32)
    for r in range(4):
        onw4[r, r, :] = common["o_norm_w"][0]
    common["onw4"] = onw4
    common["ln1_wT"] = np.ascontiguousarray(
        np.asarray(inputs["L0_attn_ln1_w"], np.float32).reshape(16, 128).T)
    common["ln2_wT"] = np.ascontiguousarray(
        np.asarray(inputs["L0_attn_ln2_w"], np.float32).reshape(16, 128).T)
    for cn, src in (("conv_qT", "L0_attn_conv_q"), ("conv_kT", "L0_attn_conv_k"),
                    ("conv_vT", "L0_attn_conv_v")):
        cw = np.asarray(inputs[src], np.float32)
        common[cn] = np.ascontiguousarray(
            cw.reshape(4, 16, 128).transpose(2, 1, 0).reshape(128, 64))
    common["fwT"] = np.ascontiguousarray(np.asarray(inputs["L0_ffn_weight"], np.float32).T)
    common["gate_wT"] = np.ascontiguousarray(np.asarray(inputs["L0_ffn_gate_w"], np.float32).T)
    common["A_wT"] = np.ascontiguousarray(
        np.asarray(inputs["L0_ffn_A_w"], np.float32).reshape(128, D).T)
    common["B_wRT"] = np.ascontiguousarray(
        np.asarray(inputs["L0_ffn_B_w"], np.float32).transpose(0, 2, 1).reshape(128, D))

    in_maps = []
    for c in range(8):
        bidx, half = c >> 1, c & 1
        m = dict(common)
        m["x_own"] = np.ascontiguousarray(x[bidx, half * TOWN:(half + 1) * TOWN])
        m["x_halo"] = np.ascontiguousarray(
            x[bidx, TOWN - THALO:TOWN] if half == 1 else x[bidx, 0:THALO])
        m["halo_mask"] = np.full((128, 1), float(half), np.float32)
        m["mA"] = np.full((128, 1), float(half), np.float32)
        m["mB"] = np.zeros((128, 1), np.float32)
        in_maps.append(m)

    res = run_bass_kernel_spmd(nc, in_maps, core_ids=list(range(8)))
    out = np.empty((Bsz, T, D), np.float32)
    for c in range(8):
        bidx, half = c >> 1, c & 1
        out[bidx, half * TOWN:(half + 1) * TOWN] = res.results[c]["y"]
    if DBG_OUT:
        kernel.dbg = [{name: r["dbg_" + name] for name in DBG_OUT} for r in res.results]
    return out


# version 04 (final)
# speedup vs baseline: 1.2074x; 1.1232x over previous
import numpy as np
import ml_dtypes
from contextlib import ExitStack

import concourse.bass as bass
import concourse.mybir as mybir
from concourse.bass_utils import run_bass_kernel_spmd

F32 = mybir.dt.float32
BF16 = mybir.dt.bfloat16
AF = mybir.ActivationFunctionType
ALU = mybir.AluOpType
AXX = mybir.AxisListType.X

D = 2048
NK = 16
TOWN = 2048
THALO = 128
TT_ = TOWN + THALO
H, DK, DV = 16, 128, 128
CCH = 128
NCH = TOWN // CCH
LORA_ALPHA = 16.0

PHASES = ("p1", "p2", "p3", "p4", "p5", "p6", "p7")
DBG_OUT = ()
P3_SWEEP1_ONLY = False
P1_TRUNC = "full"
P1_SKIP_LORA = False

ENGINES = ("sync", "scalar", "tensor", "vector", "gpsimd")


class Rec:
    def __init__(self):
        self.items = []

    def __getattr__(self, name):
        if name.startswith("_"):
            raise AttributeError(name)
        def call(*a, **k):
            it = [name, a, k, None]
            self.items.append(it)
            class _P:
                def then_inc(_self, sem, n):
                    it[3] = (sem, n)
                    return _self
            return _P()
        return call

    def replay(self, eng):
        for name, a, k, inc in self.items:
            r = getattr(eng, name)(*a, **k)
            if inc is not None:
                r.then_inc(*inc)


class Sem:
    def __init__(self, handle, name):
        self.h = handle
        self.name = name
        self.n = 0


class B:
    def __init__(self, nc, stack):
        self.nc = nc
        self.stack = stack
        self.r = {e: Rec() for e in ENGINES}
        self.sems = {}

    def sem(self, name):
        s = Sem(self.stack.enter_context(self.nc.semaphore(name)), name)
        self.sems[name] = s
        return s

    def slot_sems(self, name, n):
        return [self.sem(f"{name}{i}") for i in range(n)]

    def dma(self, queue, out, in_, sem):
        self.r[queue].dma_start(out=out, in_=in_).then_inc(sem.h, 16)
        sem.n += 16
        return sem.n

    def wait(self, eng, sem, val=None):
        self.r[eng].wait_ge(sem.h, sem.n if val is None else val)

    def inc(self, eng, sem):
        it = self.r[eng].items[-1]
        assert it[3] is None
        sem.n += 1
        it[3] = (sem.h, 1)
        return sem.n

    def finalize_hazards(self):
        from concourse.bass import AP
        skip = {"wait_ge", "nop", "sem_inc", "dma_start", "dma_start_transpose"}
        for ename in ENGINES:
            rec = self.r[ename]
            if not rec.items:
                continue
            asem = self.sem("AUTO_" + ename)

            def outs_ins(name, a, k):
                outs, ins = set(), set()
                for key, v in list(enumerate(a)) + list(k.items()):
                    if isinstance(v, AP):
                        tn = v.tensor.name
                        if key == 0 or key in ("out", "accum_out", "out_max", "out_indices"):
                            outs.add(tn)
                        else:
                            ins.add(tn)
                return outs, ins

            new_items = []
            hist = []
            for it in rec.items:
                name, a, k, inc = it
                if name in skip:
                    new_items.append(it)
                    continue
                outs, ins = outs_ins(name, a, k)
                producer = None
                for pit, pouts in hist[-4:]:
                    if ins & pouts:
                        producer = pit
                if producer is not None:
                    if producer[3] is None:
                        producer[3] = (asem, "AUTOINC")
                    new_items.append(["wait_ge", ("EDGE", producer), {}, None])
                hist.append((it, outs))
                new_items.append(it)
            cnt = 0
            counts = {}
            prod_val = {}
            for it in new_items:
                if it[3] is not None and isinstance(it[3], tuple):
                    if it[3][1] == "AUTOINC":
                        cnt += 1
                        it[3] = (asem.h, 1)
                hnd, n = it[3] if it[3] is not None else (None, 0)
                if hnd is not None:
                    key = id(hnd)
                    c = counts.get(key, 0) + n
                    counts[key] = c
                    prod_val[id(it)] = (hnd, c)
            for it in new_items:
                if it[0] == "wait_ge" and len(it[1]) == 2 and it[1][0] == "EDGE":
                    hnd, val = prod_val[id(it[1][1])]
                    it[1] = (hnd, val)
            asem.n = cnt
            rec.items = new_items

    def barrier(self):
        bar = self.sems["BAR"]
        for name, sm in self.sems.items():
            if sm.n == 0 or name in ("TE", "VE", "SE", "GP", "BAR", "CC") \
                    or name.startswith("AUTO"):
                continue
            for eng in ("sync", "scalar", "gpsimd"):
                self.wait(eng, sm)
        target = bar.n + len(ENGINES)
        for e in ENGINES:
            self.r[e].nop().then_inc(bar.h, 1)
        bar.n = target
        for e in ENGINES:
            self.r[e].wait_ge(bar.h, target)


def build_program():
    nc = bass.Bass("TRN2", target_bir_lowering=False, debug=False,
                   detect_race_conditions=False)
    stack = ExitStack()
    b = B(nc, stack)
    ap = {}

    def inp(name, shape, dtype=F32):
        ap[name] = nc.dram_tensor(name, list(shape), dtype, kind="ExternalInput").ap()
        return ap[name]

    inp("x_own", (TOWN, D)); inp("x_halo", (THALO, D))
    for w in ("wq", "wk", "wv", "wo"):
        inp(w, (D, D))
    inp("conv_qT", (128, 64)); inp("conv_kT", (128, 64)); inp("conv_vT", (128, 64))
    inp("wfa", (D, 128)); inp("wfb", (128, 16)); inp("wb", (D, 16))
    inp("wga", (D, 128)); inp("wgb", (128, D))
    inp("a_log", (16, 1)); inp("dt_bias", (16, 1))
    inp("o_norm_w", (1, 128))
    inp("onw4", (4, 4, 128))
    inp("ln1_wT", (128, 16)); inp("ln2_wT", (128, 16))
    inp("w_gate", (D, 8192)); inp("w_up", (D, 8192)); inp("w_down", (8192, D))
    inp("fwT", (D, D)); inp("gate_wT", (D, 8))
    inp("A_wT", (D, 128)); inp("B_wRT", (128, D))
    inp("identF", (128, 128)); inp("identB", (128, 128), BF16)
    inp("onesColF", (128, 1)); inp("onesRowF", (1, 128)); inp("ones16", (16, CCH))
    inp("mUIT", (128, 128), BF16); inp("mLST", (128, 128), BF16)
    inp("selm4", (16, 4, 512))
    inp("Ib4", (128, 512), BF16); inp("twoIb4", (128, 512))
    inp("E8", (8, 128), BF16)
    inp("halo_mask", (128, 1)); inp("mA", (128, 1)); inp("mB", (128, 1))
    inp("sel4", (4, 4, 128))
    inp("sel16", (16, 2048))

    y = nc.dram_tensor("y", [TOWN, D], F32, kind="ExternalOutput").ap()

    scr = {}
    def scratch(name, shape, dtype=F32):
        scr[name] = nc.dram_tensor(name, list(shape), dtype, kind="Internal")
        return scr[name]

    scratch("x1scr", (D, TT_))
    scratch("xiscr", (D, TOWN))
    scratch("x2scr", (D, TOWN))
    scratch("qfm", (D, TOWN), BF16)
    scratch("kfm", (D, TOWN), BF16)
    scratch("ktok", (H, TOWN, DK), BF16)
    scratch("vtok", (H, TOWN, DV), BF16)
    scratch("ofm", (D, TOWN), BF16)
    scratch("uv_s", (NCH, 128, H * CCH), BF16)
    scratch("wiT_s", (NCH, 128, H * CCH), BF16)
    scratch("w2T_s", (NCH, 128, H * CCH), BF16)
    scratch("qh_s", (NCH, 128, H * CCH), BF16)
    scratch("kb_s", (NCH, 128, H * CCH), BF16)
    scratch("gcc_s", (NCH, 128, 16))
    scratch("mscr", (64, 128, TOWN), BF16)
    scratch("wdc", (64, 128, D), BF16)
    scratch("pcdbg", (128, NK, 520), BF16)
    scratch("scomb_dbg", (128, 2048))
    scratch("ma_dbg", (128, 1))
    scratch("g_dbg", (16, TOWN))
    scratch("beta_dbg", (16, TOWN))
    scratch("hga_dbg", (128, TOWN), BF16)
    scratch("cc_in", (128, H * DK))
    scratch("cc_out", (256, H * DK))

    dbg_outs = {}
    for name in DBG_OUT:
        t = scr[name]
        dbg_outs[name] = nc.dram_tensor("dbg_" + name, list(t.shape), t.dtype,
                                        kind="ExternalOutput").ap()

    E = stack.enter_context
    for s in ("LD", "CA", "ST", "TE", "VE", "SE", "GP", "BAR", "CC"):
        b.sem(s)
    LD, CA, ST = b.sems["LD"], b.sems["CA"], b.sems["ST"]
    TE, VE, SE, CC = b.sems["TE"], b.sems["VE"], b.sems["SE"], b.sems["CC"]
    SY, SC, TN, VR, GP = (b.r["sync"], b.r["scalar"], b.r["tensor"],
                          b.r["vector"], b.r["gpsimd"])

    identF = E(nc.sbuf_tensor("s_identF", [128, 128], F32))
    identB = E(nc.sbuf_tensor("s_identB", [128, 128], BF16))
    onesColF = E(nc.sbuf_tensor("s_onesColF", [128, 1], F32))
    onesRowF = E(nc.sbuf_tensor("s_onesRowF", [1, 128], F32))
    ones16 = E(nc.sbuf_tensor("s_ones16", [16, CCH], F32))
    mUIT = E(nc.sbuf_tensor("s_mUIT", [128, 128], BF16))
    mLST = E(nc.sbuf_tensor("s_mLST", [128, 128], BF16))
    selm4 = E(nc.sbuf_tensor("s_selm4", [16, 4, 512], F32))
    Ib4 = E(nc.sbuf_tensor("s_Ib4", [128, 512], BF16))
    twoIb4 = E(nc.sbuf_tensor("s_twoIb4", [128, 512], F32))
    E8 = E(nc.sbuf_tensor("s_E8", [8, 128], BF16))
    halo_m = E(nc.sbuf_tensor("s_halo_m", [128, 1], F32))
    mA = E(nc.sbuf_tensor("s_mA", [128, 1], F32))
    mB = E(nc.sbuf_tensor("s_mB", [128, 1], F32))
    gwt = E(nc.sbuf_tensor("s_gwt", [128, NK, 8], BF16))
    at = E(nc.sbuf_tensor("s_at", [128, NK, 128], BF16))
    sel16 = E(nc.sbuf_tensor("s_sel16", [16, 2048], F32))
    bt = E(nc.sbuf_tensor("s_bt", [128, NK, 128], BF16))

    for name, t in (("identF", identF), ("identB", identB), ("onesColF", onesColF),
                    ("onesRowF", onesRowF), ("ones16", ones16), ("mUIT", mUIT),
                    ("mLST", mLST), ("selm4", selm4), ("Ib4", Ib4), ("twoIb4", twoIb4), ("E8", E8),
                    ("halo_mask", halo_m), ("mA", mA), ("mB", mB), ("sel16", sel16)):
        b.dma("sync", t[:], ap[name], LD)
    b.wait("tensor", LD); b.wait("vector", LD); b.wait("scalar", LD); b.wait("gpsimd", LD)

    x1v = scr["x1scr"].ap().rearrange("(k r) t -> r k t", r=128)
    xiv = scr["xiscr"].ap().rearrange("(k r) t -> r k t", r=128)
    x2v = scr["x2scr"].ap().rearrange("(k r) t -> r k t", r=128)
    qfmv = scr["qfm"].ap().rearrange("(k r) t -> r k t", r=128)
    kfmv = scr["kfm"].ap().rearrange("(k r) t -> r k t", r=128)
    ofmv = scr["ofm"].ap().rearrange("(k r) t -> r k t", r=128)

    class TransposeHelper:
        def __init__(self, pbanks):
            self.p = pbanks
            self.last_ve = [0, 0]

        def run(self, items, ident):
            n = len(items)
            for base in range(0, n, 4):
                grp = items[base:base + 4]
                sel = (base // 4) % 2
                pp = self.p[sel]
                if self.last_ve[sel]:
                    b.wait("tensor", VE, self.last_ve[sel])
                for j, (sap, _) in enumerate(grp):
                    TN.transpose(pp[:, j * 128:(j + 1) * 128], sap, ident)
                t = b.inc("tensor", TE)
                b.wait("vector", TE, t)
                for j, (_, dap) in enumerate(grp):
                    VR.tensor_copy(dap, pp[:, j * 128:(j + 1) * 128])
                self.last_ve[sel] = b.inc("vector", VE)

    def ffn_setup():
        gwv = ap["gate_wT"].rearrange("(k r) e -> r k e", r=128)
        b.dma("gpsimd", gwt[:, :, :], gwv, CA)
        awv = ap["A_wT"].rearrange("(k r) e -> r k e", r=128)
        b.dma("gpsimd", at[:, :, :], awv, CA)
        bwv = ap["B_wRT"].rearrange("p (j c) -> p j c", j=16)
        b.dma("gpsimd", bt[:, :, :], bwv, CA)
        b.wait("tensor", CA)

    def ffn_phase(which):
        first = which == "p1"
        with ExitStack() as ps:
            PE = ps.enter_context
            xfm = PE(nc.sbuf_tensor("xfm" + which, [128, NK, 1024], F32))
            lnb = PE(nc.sbuf_tensor("lnb" + which, [128, NK, 1024], BF16))
            sq = PE(nc.sbuf_tensor("sq" + which, [128, 2, NK, 128], F32))
            xtm = PE(nc.sbuf_tensor("xtm" + which, [128, 2, D], F32))
            wt = PE(nc.sbuf_tensor("wt" + which, [128, 2, NK, 128], BF16))
            stat = PE(nc.sbuf_tensor("stat" + which, [128, 32], F32))
            colw = PE(nc.sbuf_tensor("colw" + which, [128, 16], F32))
            rowt = PE(nc.sbuf_tensor("rowt" + which, [16, 128], F32))
            cwT = PE(nc.sbuf_tensor("cwT" + which, [8, 1024], BF16))
            cwb = PE(nc.sbuf_tensor("cwb" + which, [128, 1024], BF16))
            tsc = PE(nc.sbuf_tensor("tsc" + which, [128, 1024], BF16))
            otm = PE(nc.sbuf_tensor("otm" + which, [128, 2, D], F32))
            pA = PE(nc.psum_tensor("pA" + which, [128, 1024], F32))
            pB = PE(nc.psum_tensor("pB" + which, [128, 1024], F32))
            pT = PE(nc.psum_tensor("pT" + which, [128, 512], F32))
            pT2 = PE(nc.psum_tensor("pT2" + which, [128, 512], F32))
            pTB = PE(nc.psum_tensor("pTB" + which, [128, 512], BF16))
            pSM = PE(nc.psum_tensor("pSM" + which, [128, 128], F32))
            pST = pSM

            if first:
                ffn_setup()
            fwv = ap["fwT"].rearrange("(k r) o -> r k o", r=128)
            XT = b.slot_sems("XT" + which, 2)
            WTS = b.slot_sems("WTS" + which, 2)
            OTS = b.slot_sems("OTS" + which, 2)
            xt_busy = [0, 0]
            wt_busy = [0, 0]
            ot_busy = [0, 0]

            thF = TransposeHelper([pT, pT2])
            panels = ([(THALO, "halo")] if first else []) + [(1024, 0), (1024, 1)]
            st_prev = 0
            for T, ptag in panels:
                halo = ptag == "halo"
                ntile = T // 128
                ngg = max(1, T // 512)
                wgg = min(512, T)
                b.wait("vector", ST, st_prev)
                if first:
                    src = ap["x_halo"] if halo else ap["x_own"]
                    off = 0 if halo else ptag * 1024
                    for r in range(ntile):
                        if xt_busy[r % 2]:
                            b.wait("sync", TE, xt_busy[r % 2])
                        b.dma("sync", xtm[:, r % 2, :],
                              src[off + r * 128: off + (r + 1) * 128, :], XT[r % 2])
                        b.wait("tensor", XT[r % 2], XT[r % 2].n)
                        items = [(xtm[:, r % 2, k * 128:(k + 1) * 128],
                                  xfm[:, k, r * 128:(r + 1) * 128]) for k in range(NK)]
                        thF.run(items, identF[:, :])
                        xt_busy[r % 2] = TE.n
                else:
                    b.wait("sync", TE, TE.n)
                    b.dma("sync", xfm[:, :, 0:1024], x2v[:, :, ptag * 1024:(ptag + 1) * 1024], LD)
                    b.wait("tensor", LD); b.wait("vector", LD); b.wait("scalar", LD)

                if first and P1_TRUNC == "xonly":
                    b.wait("scalar", VE)
                    c0 = 0 if halo else THALO + ptag * 1024
                    st_prev = b.dma("scalar", x1v[:, :, c0:c0 + T], xfm[:, :, 0:T], ST)
                    continue
                b.wait("scalar", VE)
                te_sq = {}
                for r in range(ntile):
                    if r >= 2:
                        b.wait("scalar", TE, te_sq[r - 2])
                    SC.activation(sq[:, r % 2, :, :], xfm[:, :, r * 128:(r + 1) * 128], AF.Square)
                    s = b.inc("scalar", SE)
                    b.wait("tensor", SE, s)
                    for k in range(NK):
                        TN.matmul(pST[:, r:r + 1], xfm[:, k, r * 128:(r + 1) * 128],
                                  onesColF[:], start=(r == 0 and k == 0), stop=False)
                        TN.matmul(pST[:, 8 + r:9 + r], sq[:, r % 2, k, :], onesColF[:],
                                  start=False, stop=(r == ntile - 1 and k == NK - 1))
                    te_sq[r] = b.inc("tensor", TE)
                t = TE.n
                b.wait("vector", TE, t)
                nt = ntile
                VR.memset(colw[:, :], 0.0)
                VR.tensor_copy(stat[:, 0:nt], pST[:, 0:nt])
                VR.tensor_copy(stat[:, 8:8 + nt], pST[:, 8:8 + nt])
                VR.tensor_scalar(colw[:, 0:nt], stat[:, 0:nt], 1.0 / D, None, ALU.mult)
                VR.tensor_mul(colw[:, 8:8 + nt], colw[:, 0:nt], colw[:, 0:nt])
                VR.scalar_tensor_tensor(colw[:, 8:8 + nt], stat[:, 8:8 + nt], 1.0 / D,
                                        colw[:, 8:8 + nt], ALU.mult, ALU.subtract)
                v = b.inc("vector", VE)
                b.wait("scalar", VE, v)
                SC.activation(colw[:, 8:8 + nt], colw[:, 8:8 + nt], AF.Sqrt, bias=np.float32(1e-5))
                s = b.inc("scalar", SE)
                b.wait("vector", SE, s)
                VR.reciprocal(colw[:, 8:8 + nt], colw[:, 8:8 + nt])
                VR.tensor_mul(colw[:, 0:nt], colw[:, 0:nt], colw[:, 8:8 + nt])
                VR.tensor_scalar(colw[:, 0:nt], colw[:, 0:nt], -1.0, None, ALU.mult)
                v = b.inc("vector", VE)
                b.wait("tensor", VE, v)
                TN.transpose(pSM[0:16, 0:128], colw[:, 0:16], identF[:, :])
                t = b.inc("tensor", TE)
                b.wait("vector", TE, t)
                VR.tensor_copy(rowt[:, :], pSM[0:16, 0:128])
                v = b.inc("vector", VE)
                b.wait("tensor", VE, v)
                for r in range(ntile):
                    TN.matmul(pA[:, r * 128:(r + 1) * 128],
                              sel16[:, (8 + r) * 128:(9 + r) * 128], rowt[:, :],
                              start=(r % 4 == 0), stop=(r % 4 == 3 or r == ntile - 1))
                    TN.matmul(pB[:, r * 128:(r + 1) * 128],
                              sel16[:, r * 128:(r + 1) * 128], rowt[:, :],
                              start=(r % 4 == 0), stop=(r % 4 == 3 or r == ntile - 1))
                t = b.inc("tensor", TE)
                b.wait("vector", TE, t)
                for k in range(NK):
                    VR.tensor_mul(lnb[:, k, 0:T], xfm[:, k, 0:T], pA[:, 0:T])
                    VR.tensor_add(lnb[:, k, 0:T], lnb[:, k, 0:T], pB[:, 0:T])
                v_lnb = b.inc("vector", VE)
                b.wait("tensor", VE, v_lnb)

                if first and P1_TRUNC == "ln":
                    qv_dbg = scr["qfm"].ap().rearrange("(k r) t -> r k t", r=128)
                    b.wait("scalar", VE)
                    if not halo:
                        b.dma("scalar", qv_dbg[:, :, ptag * 1024:ptag * 1024 + T],
                              lnb[:, :, 0:T], ST)
                    continue
                for r in range(ntile):
                    for k in range(NK):
                        TN.matmul(pSM[:, 0:8], lnb[:, k, r * 128:(r + 1) * 128],
                                  gwt[:, k, :], start=(k == 0), stop=(k == NK - 1))
                    t = b.inc("tensor", TE)
                    b.wait("vector", TE, t)
                    VR.reduce_max(stat[:, 0:1], pSM[:, 0:8], AXX)
                    VR.tensor_scalar(stat[:, 8:16], pSM[:, 0:8], stat[:, 0:1], 0.0,
                                     ALU.subtract, ALU.is_ge)
                    VR.scalar_tensor_tensor(stat[:, 16:24], stat[:, 8:16], -1e30, pSM[:, 0:8],
                                            ALU.mult, ALU.add)
                    VR.reduce_max(stat[:, 1:2], stat[:, 16:24], AXX)
                    VR.tensor_sub(stat[:, 2:3], stat[:, 1:2], stat[:, 0:1])
                    v = b.inc("vector", VE)
                    b.wait("scalar", VE, v)
                    SC.activation(stat[:, 3:4], stat[:, 2:3], AF.Sigmoid)
                    s = b.inc("scalar", SE)
                    b.wait("vector", SE, s)
                    VR.tensor_scalar(stat[:, 4:5], stat[:, 3:4], -1.0, 1.0, ALU.mult, ALU.add)
                    VR.tensor_scalar(stat[:, 24:32], stat[:, 16:24], stat[:, 1:2], 0.0,
                                     ALU.subtract, ALU.is_ge)
                    VR.tensor_scalar(stat[:, 8:16], stat[:, 8:16], stat[:, 4:5], None, ALU.mult)
                    VR.tensor_scalar(stat[:, 24:32], stat[:, 24:32], stat[:, 3:4], None, ALU.mult)
                    VR.tensor_add(stat[:, 24:32], stat[:, 24:32], stat[:, 8:16])
                    if first and P1_TRUNC == "lora" and not halo and ptag == 0:
                        VR.tensor_copy(tsc[:, r * 32:(r + 1) * 32], stat[:, 0:32])
                    v = b.inc("vector", VE)
                    b.wait("tensor", VE, v)
                    TN.transpose(pSM[0:8, 64:192] if False else pSM[0:8, 0:128],
                                 stat[:, 24:32], identF[:, :])
                    t = b.inc("tensor", TE)
                    b.wait("vector", TE, t)
                    VR.tensor_copy(cwT[:, r * 128:(r + 1) * 128], pSM[0:8, 0:128])
                    v = b.inc("vector", VE)
                    b.wait("tensor", VE, v)
                for r in range(ntile):
                    TN.matmul(pB[:, r * 128:(r + 1) * 128], E8[:], cwT[:, r * 128:(r + 1) * 128],
                              start=(r % 4 == 0), stop=(r % 4 == 3 or r == ntile - 1))
                for gg in range(ngg):
                    for k in range(NK):
                        TN.matmul(pA[:, gg * 512:gg * 512 + wgg], at[:, k, :],
                                  lnb[:, k, gg * 512:gg * 512 + wgg],
                                  start=(k == 0), stop=(k == NK - 1))
                t = b.inc("tensor", TE)
                b.wait("vector", TE, t)
                VR.tensor_copy(cwb[:, 0:T], pB[:, 0:T])
                if not (first and P1_TRUNC == "lora"):
                    for gg in range(ngg):
                        VR.scalar_tensor_tensor(tsc[:, gg * 512:gg * 512 + wgg],
                                                pA[:, gg * 512:gg * 512 + wgg], LORA_ALPHA,
                                                cwb[:, gg * 512:gg * 512 + wgg], ALU.mult, ALU.mult)
                v = b.inc("vector", VE)
                b.wait("tensor", VE, v)

                if first and P1_TRUNC == "lora" and not halo:
                    kv_dbg = scr["kfm"].ap().rearrange("(k r) t -> r k t", r=128)
                    b.wait("scalar", VE)
                    b.dma("scalar", kv_dbg[:, 0:1, ptag * 1024:ptag * 1024 + T]
                          .rearrange("r k t -> r (k t)"), tsc[:, 0:T], ST)
                    b.dma("scalar", kv_dbg[:, 1:2, ptag * 1024:ptag * 1024 + T]
                          .rearrange("r k t -> r (k t)"), cwb[:, 0:T], ST)
                if first and P1_TRUNC == "lora":
                    continue
                ve_done = {}
                te_mm = {}
                for m in range(NK):
                    bsel = m % 2
                    pO = pA if bsel == 0 else pB
                    if wt_busy[bsel]:
                        b.wait("gpsimd", TE, wt_busy[bsel])
                    b.dma("gpsimd", wt[:, bsel], fwv[:, :, m * 128:(m + 1) * 128], WTS[bsel])
                    b.wait("tensor", WTS[bsel], WTS[bsel].n)
                    if m >= 2:
                        b.wait("tensor", VE, ve_done[m - 2])
                    for gg in range(ngg):
                        for k in range(NK):
                            TN.matmul(pO[:, gg * 512:gg * 512 + wgg], wt[:, bsel, k, :],
                                      lnb[:, k, gg * 512:gg * 512 + wgg],
                                      start=(k == 0), stop=(P1_SKIP_LORA and k == NK - 1))
                        if P1_SKIP_LORA:
                            TN.matmul(pO[:, gg * 512:gg * 512 + wgg], bt[:, m, :],
                                      tsc[:, gg * 512:gg * 512 + wgg], start=False, stop=True,
                                      skip_group_check=True) if False else None
                        if not P1_SKIP_LORA:
                            TN.matmul(pO[:, gg * 512:gg * 512 + wgg], bt[:, m, :],
                                      tsc[:, gg * 512:gg * 512 + wgg], start=False, stop=True)
                    te_mm[m] = b.inc("tensor", TE)
                    wt_busy[bsel] = te_mm[m]
                    b.wait("vector", TE, te_mm[m])
                    for gg in range(ngg):
                        VR.scalar_tensor_tensor(xfm[:, m, gg * 512:gg * 512 + wgg],
                                                pO[:, gg * 512:gg * 512 + wgg], 0.5,
                                                xfm[:, m, gg * 512:gg * 512 + wgg],
                                                ALU.mult, ALU.add)
                    ve_done[m] = b.inc("vector", VE)
                b.wait("scalar", VE)
                if first:
                    c0 = 0 if halo else THALO + ptag * 1024
                    st_prev = b.dma("scalar", x1v[:, :, c0:c0 + T], xfm[:, :, 0:T], ST)
                else:
                    for r in range(ntile):
                        if ot_busy[r % 2]:
                            b.wait("vector", OTS[r % 2], ot_busy[r % 2])
                        items = [(xfm[:, k, r * 128:(r + 1) * 128],
                                  otm[:, r % 2, k * 128:(k + 1) * 128]) for k in range(NK)]
                        thF.run(items, identF[:, :])
                        b.wait("scalar", VE)
                        ot_busy[r % 2] = b.dma(
                            "scalar", y[ptag * 1024 + r * 128: ptag * 1024 + (r + 1) * 128, :],
                            otm[:, r % 2, :], OTS[r % 2])
            b.barrier()

    def proj_phase(g_res, beta_res, hga_res):
        with ExitStack() as ps:
            PE = ps.enter_context
            xp = PE(nc.sbuf_tensor("xp2", [128, NK, 512], F32))
            hb = PE(nc.sbuf_tensor("hb2", [128, NK, 512], BF16))
            sq2 = PE(nc.sbuf_tensor("sq2", [128, 2, NK, 128], BF16))
            pc = PE(nc.sbuf_tensor("pc2", [128, NK, 520], BF16))
            carryW = PE(nc.sbuf_tensor("carryW2", [128, 3, NK, 3], BF16))
            wbuf = PE(nc.sbuf_tensor("wbuf2", [128, 2, NK, 512], BF16))
            stat = PE(nc.sbuf_tensor("stat2", [128, 32], F32))
            colw = PE(nc.sbuf_tensor("colw2", [128, 16], F32))
            rowt = PE(nc.sbuf_tensor("rowt2", [16, 128], F32))
            cvall = PE(nc.sbuf_tensor("cvall2", [128, NK, 512], BF16))
            sqq4 = PE(nc.sbuf_tensor("sqq42", [128, 4, 512], BF16))
            colw64 = PE(nc.sbuf_tensor("colw642", [128, 64], F32))
            lrow4 = PE(nc.sbuf_tensor("lrow42", [4, 2, 128], F32))
            sel4 = PE(nc.sbuf_tensor("sel42", [4, 4, 128], F32))
            onesColB = PE(nc.sbuf_tensor("onesColB2", [128, 1], BF16))
            tkb = PE(nc.sbuf_tensor("tkb2", [128, 2, 4, 128], BF16))
            hfa = PE(nc.sbuf_tensor("hfa2", [128, 512], BF16))
            wfa_s = PE(nc.sbuf_tensor("wfa_s", [128, NK, 128], BF16))
            wga_s = PE(nc.sbuf_tensor("wga_s", [128, NK, 128], BF16))
            wb_s = PE(nc.sbuf_tensor("wb_s", [128, NK, 16], BF16))
            wfb_s = PE(nc.sbuf_tensor("wfb_s", [128, 16], BF16))
            cqT = PE(nc.sbuf_tensor("cqT", [128, 64], F32))
            ckT = PE(nc.sbuf_tensor("ckT", [128, 64], F32))
            cvT = PE(nc.sbuf_tensor("cvT", [128, 64], F32))
            ln1c = PE(nc.sbuf_tensor("ln1c", [128, 16], F32))
            acol = PE(nc.sbuf_tensor("acol", [16, 4], F32))
            zw = PE(nc.sbuf_tensor("zw2", [16, 4, 512], F32))
            pP = PE(nc.psum_tensor("pP2", [128, 2048], F32))
            pA = PE(nc.psum_tensor("pA2", [128, 512], F32))
            pB = PE(nc.psum_tensor("pB2", [128, 512], F32))
            pC = PE(nc.psum_tensor("pC2", [128, 512], F32))
            pTB = PE(nc.psum_tensor("pTB2", [128, 1024], BF16))

            WSL = b.slot_sems("WSL2", 2)
            TKS = b.slot_sems("TKS2", 2)
            FMS = b.slot_sems("FMS2", 2)
            te_wslot = [0, 0]
            tk_busy = [0, 0]
            fm_busy = [0, 0]
            te_sqslot = [0, 0]
            ve_lcol = 0

            b.dma("gpsimd", wfa_s[:, :, :], ap["wfa"].rearrange("(k r) c -> r k c", r=128), CA)
            b.dma("gpsimd", wga_s[:, :, :], ap["wga"].rearrange("(k r) c -> r k c", r=128), CA)
            b.dma("gpsimd", wb_s[:, :, :], ap["wb"].rearrange("(k r) c -> r k c", r=128), CA)
            b.dma("gpsimd", wfb_s[:, :], ap["wfb"], CA)
            b.dma("sync", cqT[:, :], ap["conv_qT"], LD)
            b.dma("sync", ckT[:, :], ap["conv_kT"], LD)
            b.dma("sync", cvT[:, :], ap["conv_vT"], LD)
            b.dma("sync", ln1c[:, :], ap["ln1_wT"], LD)
            b.dma("sync", acol[0:16, 0:1], ap["a_log"], LD)
            b.dma("sync", sel4[:, :, :], ap["sel4"], LD)
            b.dma("gpsimd", onesColB[:, :], ap["onesColF"], CA)
            b.dma("sync", acol[0:16, 1:2], ap["dt_bias"], LD)
            b.wait("scalar", LD)
            SC.activation(acol[0:16, 2:3], acol[0:16, 0:1], AF.Exp)
            se0 = b.inc("scalar", SE)
            b.wait("vector", SE, se0)
            VR.tensor_scalar(acol[0:16, 2:3], acol[0:16, 2:3], -1.0, None, ALU.mult)
            b.inc("vector", VE)
            b.wait("tensor", LD); b.wait("tensor", CA); b.wait("vector", LD)

            te_xp = 0
            ve_xp = 0
            panels = [(THALO, "halo")] + [(512, p) for p in range(4)]
            for T, ptag in panels:
                halo = ptag == "halo"
                nt = ntile = T // 128
                if te_xp:
                    b.wait("sync", TE, te_xp)
                    b.wait("sync", VE, ve_xp)
                c0 = 0 if halo else THALO + ptag * 512
                b.dma("sync", xp[:, :, 0:T], x1v[:, :, c0:c0 + T], LD)
                b.wait("scalar", LD); b.wait("tensor", LD); b.wait("vector", LD)
                b.wait("tensor", SE, SE.n)
                b.wait("tensor", VE, VE.n)
                for r in range(ntile):
                    if te_sqslot[r % 2]:
                        b.wait("scalar", TE, te_sqslot[r % 2])
                    SC.activation(sq2[:, r % 2, :, :], xp[:, :, r * 128:(r + 1) * 128], AF.Square)
                    s = b.inc("scalar", SE)
                    b.wait("tensor", SE, s)
                    for k in range(NK):
                        TN.matmul(pC[:, r:r + 1], xp[:, k, r * 128:(r + 1) * 128],
                                  onesColF[:], start=(r == 0 and k == 0), stop=False)
                        TN.matmul(pC[:, 8 + r:9 + r], sq2[:, r % 2, k, :], onesColB[:],
                                  start=False, stop=(r == ntile - 1 and k == NK - 1))
                    te_sqslot[r % 2] = b.inc("tensor", TE)
                b.wait("vector", TE, TE.n)
                VR.memset(colw[:, :], 0.0)
                VR.tensor_copy(stat[:, 0:nt], pC[:, 0:nt])
                VR.tensor_copy(stat[:, 8:8 + nt], pC[:, 8:8 + nt])
                VR.tensor_scalar(colw[:, 0:nt], stat[:, 0:nt], 1.0 / D, None, ALU.mult)
                VR.tensor_mul(colw[:, 8:8 + nt], colw[:, 0:nt], colw[:, 0:nt])
                VR.scalar_tensor_tensor(colw[:, 8:8 + nt], stat[:, 8:8 + nt], 1.0 / D,
                                        colw[:, 8:8 + nt], ALU.mult, ALU.subtract)
                v = b.inc("vector", VE)
                b.wait("scalar", VE, v)
                SC.activation(colw[:, 8:8 + nt], colw[:, 8:8 + nt], AF.Sqrt,
                              bias=np.float32(1e-11), scale=np.float32(1.0 + 1e-6))
                s = b.inc("scalar", SE)
                b.wait("vector", SE, s)
                VR.reciprocal(colw[:, 8:8 + nt], colw[:, 8:8 + nt])
                VR.tensor_mul(colw[:, 0:nt], colw[:, 0:nt], colw[:, 8:8 + nt])
                VR.tensor_scalar(colw[:, 0:nt], colw[:, 0:nt], -1.0, None, ALU.mult)
                v = b.inc("vector", VE)
                b.wait("tensor", VE, v)
                TN.transpose(pC[0:16, 0:128], colw[:, 0:16], identF[:, :])
                t = b.inc("tensor", TE)
                b.wait("vector", TE, t)
                VR.tensor_copy(rowt[:, :], pC[0:16, 0:128])
                v = b.inc("vector", VE)
                b.wait("tensor", VE, v)
                for r in range(ntile):
                    TN.matmul(pA[:, r * 128:(r + 1) * 128],
                              sel16[:, (8 + r) * 128:(9 + r) * 128], rowt[:, :],
                              start=(r == 0), stop=(r == ntile - 1))
                    TN.matmul(pB[:, r * 128:(r + 1) * 128],
                              sel16[:, r * 128:(r + 1) * 128], rowt[:, :],
                              start=(r == 0), stop=(r == ntile - 1))
                t = b.inc("tensor", TE)
                b.wait("vector", TE, t)
                for k in range(NK):
                    VR.scalar_tensor_tensor(hb[:, k, 0:T], xp[:, k, 0:T], ln1c[:, k:k + 1],
                                            pA[:, 0:T], ALU.mult, ALU.mult)
                    VR.scalar_tensor_tensor(hb[:, k, 0:T], pB[:, 0:T], ln1c[:, k:k + 1],
                                            hb[:, k, 0:T], ALU.mult, ALU.add)
                v_hb = b.inc("vector", VE)
                b.wait("tensor", VE, v_hb)

                for wi, (wname, convT, is_q, is_k) in enumerate(
                        (("wq", cqT, True, False), ("wk", ckT, False, True),
                         ("wv", cvT, False, False))):
                    wap = ap[wname]
                    is_v = not (is_q or is_k)
                    b.wait("tensor", VE, VE.n)
                    if not halo:
                        VR.tensor_copy(pc[:, :, 0:3], carryW[:, wi])
                        b.inc("vector", VE)
                        if DBG_OUT and ptag == 1 and wi == 1:
                            b.wait("scalar", VE, VE.n)
                            b.dma("scalar", scr["pcdbg"].ap(), pc[:, :, :], ST)
                            b.wait("scalar", ST)
                    wapR = wap.rearrange("(k r) c -> r k c", r=128)
                    wdma = [0] * 4
                    for grp in range(2):
                        if te_wslot[grp]:
                            b.wait("gpsimd", TE, te_wslot[grp])
                        wdma[grp] = b.dma("gpsimd", wbuf[:, grp],
                                          wapR[:, :, grp * 512:(grp + 1) * 512], WSL[grp])
                    for grp in range(4):
                        gsl = grp % 2
                        b.wait("tensor", WSL[gsl], wdma[grp])
                        for k in range(NK):
                            for mm_ in range(4):
                                TN.matmul(pP[:, mm_ * 512:mm_ * 512 + T],
                                          wbuf[:, gsl, k, mm_ * 128:(mm_ + 1) * 128],
                                          hb[:, k, 0:T], start=(k == 0), stop=(k == NK - 1))
                        te_wslot[gsl] = b.inc("tensor", TE)
                        if grp + 2 < 4:
                            b.wait("gpsimd", TE, te_wslot[gsl])
                            wdma[grp + 2] = b.dma("gpsimd", wbuf[:, gsl],
                                                  wapR[:, :, (grp + 2) * 512:(grp + 3) * 512],
                                                  WSL[gsl])
                        b.wait("vector", TE, TE.n)
                        for mm_ in range(4):
                            m = grp * 4 + mm_
                            VR.tensor_copy(pc[:, m, 3:3 + T], pP[:, mm_ * 512:mm_ * 512 + T])
                        ve_pc = b.inc("vector", VE)
                        b.wait("tensor", VE, ve_pc)
                    if halo:
                        VR.tensor_scalar(carryW[:, wi], pc[:, :, T:T + 3],
                                         halo_m[:, 0:1], None, ALU.mult)
                        b.inc("vector", VE)
                        continue
                    VR.tensor_copy(carryW[:, wi], pc[:, :, T:T + 3])
                    b.inc("vector", VE)
                    b.wait("vector", FMS[0], FMS[0].n)
                    for m in range(NK):
                        VR.tensor_scalar(cvall[:, m, :], pc[:, m, 3:3 + T],
                                         convT[:, 4 * m + 3:4 * m + 4], None, ALU.mult)
                        VR.scalar_tensor_tensor(cvall[:, m, :], pc[:, m, 2:2 + T],
                                                convT[:, 4 * m + 2:4 * m + 3], cvall[:, m, :],
                                                ALU.mult, ALU.add)
                        VR.scalar_tensor_tensor(cvall[:, m, :], pc[:, m, 1:1 + T],
                                                convT[:, 4 * m + 1:4 * m + 2], cvall[:, m, :],
                                                ALU.mult, ALU.add)
                        VR.scalar_tensor_tensor(cvall[:, m, :], pc[:, m, 0:T],
                                                convT[:, 4 * m:4 * m + 1], cvall[:, m, :],
                                                ALU.mult, ALU.add)
                    v_cv = b.inc("vector", VE)
                    b.wait("scalar", VE, v_cv)
                    for m in range(NK):
                        SC.activation(cvall[:, m, :], cvall[:, m, :], AF.Silu)
                    s_si = b.inc("scalar", SE)
                    if not is_v:
                        te_sqo2 = [0, 0, 0, 0]
                        b.wait("scalar", SE, s_si)
                        for m in range(NK):
                            if m >= 4 and te_sqo2[m % 4]:
                                b.wait("scalar", TE, te_sqo2[m % 4])
                            SC.activation(sqq4[:, m % 4, :], cvall[:, m, :], AF.Square)
                            sse = b.inc("scalar", SE)
                            b.wait("tensor", SE, sse)
                            for r in range(4):
                                TN.matmul(pC[:, 4 * m + r:4 * m + r + 1],
                                          sqq4[:, m % 4, r * 128:(r + 1) * 128], onesColB[:],
                                          start=(m == 0 and r == 0),
                                          stop=(m == NK - 1 and r == 3))
                            te_sqo2[m % 4] = b.inc("tensor", TE)
                        b.wait("vector", TE, TE.n)
                        VR.tensor_copy(colw64[:, :], pC[:, 0:64])
                        v = b.inc("vector", VE)
                        b.wait("scalar", VE, v)
                        if is_q:
                            SC.activation(colw64[:, :], colw64[:, :], AF.Sqrt,
                                          bias=np.float32(128e-6), scale=np.float32(128.0))
                        else:
                            SC.activation(colw64[:, :], colw64[:, :], AF.Sqrt,
                                          bias=np.float32(1e-6))
                        s_ = b.inc("scalar", SE)
                        b.wait("vector", SE, s_)
                        VR.reciprocal(colw64[:, :], colw64[:, :])
                        v = b.inc("vector", VE)
                        b.wait("tensor", VE, v)
                        ve_sc = {}
                        for m in range(NK):
                            TN.transpose(pC[0:4, 64 + (m % 2) * 128:64 + (m % 2) * 128 + 128],
                                         colw64[:, 4 * m:4 * m + 4], identF[:, :])
                            t = b.inc("tensor", TE)
                            b.wait("vector", TE, t)
                            VR.tensor_copy(lrow4[:, m % 2, :],
                                           pC[0:4, 64 + (m % 2) * 128:64 + (m % 2) * 128 + 128])
                            vlr = b.inc("vector", VE)
                            b.wait("tensor", VE, vlr)
                            if m >= 4 and (m - 4) in ve_sc:
                                b.wait("tensor", VE, ve_sc[m - 4])
                            for r in range(4):
                                TN.matmul(pP[:, (m % 4) * 512 + r * 128:
                                             (m % 4) * 512 + (r + 1) * 128],
                                          sel4[:, r, :], lrow4[:, m % 2, :],
                                          start=(r == 0), stop=(r == 3))
                            t = b.inc("tensor", TE)
                            b.wait("vector", TE, t)
                            VR.tensor_mul(cvall[:, m, :], cvall[:, m, :],
                                          pP[:, (m % 4) * 512:(m % 4) * 512 + 512])
                            ve_sc[m] = b.inc("vector", VE)
                        b.wait("scalar", VE, VE.n)
                    else:
                        b.wait("scalar", SE, s_si)
                    if not is_v:
                        dstv = qfmv if is_q else kfmv
                        b.wait("scalar", SE, SE.n)
                        b.wait("scalar", VE, VE.n)
                        for m in range(NK):
                            b.dma("scalar", dstv[:, m, ptag * 512:(ptag + 1) * 512],
                                  cvall[:, m, :], FMS[0])
                    if is_k or is_v:
                        dst = scr["ktok" if is_k else "vtok"].ap()
                        b.wait("tensor", SE, SE.n)
                        b.wait("tensor", VE, VE.n)
                        for mp in range(0, NK, 2):
                            for mo in range(2):
                                m = mp + mo
                                for r in range(4):
                                    TN.transpose(pTB[:, mo * 512 + r * 128:
                                                     mo * 512 + (r + 1) * 128],
                                                 cvall[:, m, r * 128:(r + 1) * 128],
                                                 identB[:, :])
                            t = b.inc("tensor", TE)
                            b.wait("vector", TE, t)
                            for mo in range(2):
                                tsl = mo
                                if tk_busy[tsl]:
                                    b.wait("vector", TKS[tsl], tk_busy[tsl])
                                VR.tensor_copy(tkb[:, tsl].rearrange("p a c -> p (a c)"),
                                               pTB[:, mo * 512:mo * 512 + 512])
                            v_tk = b.inc("vector", VE)
                            b.wait("tensor", VE, v_tk)
                            b.wait("scalar", VE, v_tk)
                            for mo in range(2):
                                tk_busy[mo] = b.dma(
                                    "scalar",
                                    dst[mp + mo][ptag * 512:(ptag + 1) * 512, :]
                                    .rearrange("(rr t) d -> t rr d", t=128),
                                    tkb[:, mo], TKS[mo])
                if halo:
                    te_xp = TE.n
                    ve_xp = VE.n
                    continue

                b.wait("tensor", VE, VE.n)
                for k in range(NK):
                    TN.matmul(pA[:, 0:T], wfa_s[:, k, :], hb[:, k, 0:T],
                              start=(k == 0), stop=(k == NK - 1))
                t = b.inc("tensor", TE)
                b.wait("vector", TE, t)
                VR.tensor_copy(hfa[:, 0:T], pA[:, 0:T])
                v = b.inc("vector", VE)
                b.wait("tensor", VE, v)
                TN.matmul(pB[0:16, 0:T], wfb_s[:, :], hfa[:, 0:T], start=True, stop=True)
                for k in range(NK):
                    TN.matmul(pC[0:16, 0:T], wb_s[:, k, :], hb[:, k, 0:T],
                              start=(k == 0), stop=(k == NK - 1))
                t = b.inc("tensor", TE)
                b.wait("scalar", TE, t)
                b.wait("vector", TE, t)
                VR.tensor_scalar(zw[:, 0, 0:T], pB[0:16, 0:T], acol[0:16, 1:2], None, ALU.add)
                v = b.inc("vector", VE)
                b.wait("scalar", VE, v)
                SC.activation(zw[:, 1, 0:T], zw[:, 0, 0:T], AF.Abs)
                SC.activation(zw[:, 2, 0:T], zw[:, 1, 0:T], AF.Exp, scale=np.float32(-1.0))
                SC.activation(zw[:, 2, 0:T], zw[:, 2, 0:T], AF.Ln, bias=np.float32(1.0))
                SC.activation(beta_res[:, ptag * 512:ptag * 512 + T], pC[0:16, 0:T], AF.Sigmoid)
                s = b.inc("scalar", SE)
                b.wait("vector", SE, s)
                VR.tensor_scalar_max(zw[:, 3, 0:T], zw[:, 0, 0:T], 0.0)
                VR.tensor_add(zw[:, 3, 0:T], zw[:, 3, 0:T], zw[:, 2, 0:T])
                VR.tensor_scalar(g_res[:, ptag * 512:ptag * 512 + T],
                                 zw[:, 3, 0:T], acol[0:16, 2:3], None, ALU.mult)
                v = b.inc("vector", VE)
                b.wait("tensor", VE, v)
                for k in range(NK):
                    TN.matmul(pA[:, 0:T], wga_s[:, k, :], hb[:, k, 0:T],
                              start=(k == 0), stop=(k == NK - 1))
                t = b.inc("tensor", TE)
                b.wait("vector", TE, t)
                VR.tensor_copy(hga_res[:, ptag * 512:ptag * 512 + T], pA[:, 0:T])
                b.inc("vector", VE)
                te_xp = TE.n
                ve_xp = VE.n
            b.barrier()

    def scan_phase(g_res, beta_res):
        with ExitStack() as ps:
            PE = ps.enter_context
            S = PE(nc.sbuf_tensor("S3", [128, H, DV], F32))
            S0b = PE(nc.sbuf_tensor("S0b3", [128, H, DV], BF16))
            qf = PE(nc.sbuf_tensor("qf3", [128, 2, H, CCH], BF16))
            kf = PE(nc.sbuf_tensor("kf3", [128, 2, H, CCH], BF16))
            kt = PE(nc.sbuf_tensor("kt3", [128, 2, H, DK], BF16))
            vt = PE(nc.sbuf_tensor("vt3", [128, 2, H, DV], BF16))
            Uv = PE(nc.sbuf_tensor("Uv3", [128, 2, H, DV], BF16))
            WiT = PE(nc.sbuf_tensor("WiT3", [128, 2, H, CCH], BF16))
            W2T = PE(nc.sbuf_tensor("W2T3", [128, 2, H, CCH], BF16))
            Qh = PE(nc.sbuf_tensor("Qh3", [128, 2, H, CCH], BF16))
            Kb = PE(nc.sbuf_tensor("Kb3", [128, 2, H, DK], BF16))
            Ua = PE(nc.sbuf_tensor("Ua3", [128, H, DV], BF16))
            ofmb = PE(nc.sbuf_tensor("ofmb3", [128, 2, H, CCH], BF16))
            gcc = PE(nc.sbuf_tensor("gcc3", [128, 2, 16], F32))
            DpT4 = PE(nc.sbuf_tensor("DpT4", [128, 512], BF16))
            DsL4 = PE(nc.sbuf_tensor("DsL4", [128, 512], BF16))
            Grow4 = PE(nc.sbuf_tensor("Grow4", [128, 512], BF16))
            MPOS4 = PE(nc.sbuf_tensor("MPOS4", [128, 512], BF16))
            MT4 = PE(nc.sbuf_tensor("MT4", [128, 512], BF16))
            A4 = PE(nc.sbuf_tensor("A4", [128, 512], BF16))
            W4 = PE(nc.sbuf_tensor("W4", [128, 512], BF16))
            V4 = PE(nc.sbuf_tensor("V4", [128, 512], BF16))
            G4 = PE(nc.sbuf_tensor("G4", [128, 512], BF16))
            Zv4 = PE(nc.sbuf_tensor("Zv4", [128, 512], BF16))
            Y4 = PE(nc.sbuf_tensor("Y4", [128, 512], BF16))
            ot4 = PE(nc.sbuf_tensor("ot4", [128, 512], BF16))
            cT = PE(nc.sbuf_tensor("cT3", [16, 128], F32))
            lnbT = PE(nc.sbuf_tensor("lnbT3", [16, 128], F32))
            cplusT = PE(nc.sbuf_tensor("cplusT3", [16, 128], F32))
            cT4 = PE(nc.sbuf_tensor("cT43", [16, 512], F32))
            msk4 = PE(nc.sbuf_tensor("msk43", [16, 512], F32))
            nmsk4 = PE(nc.sbuf_tensor("nmsk43", [16, 512], F32))
            cols = PE(nc.sbuf_tensor("cols3", [128, 64], F32))
            negcC = PE(nc.sbuf_tensor("negcC3", [128, 16], F32))
            ratC = PE(nc.sbuf_tensor("ratC3", [128, 16], F32))
            bgC = PE(nc.sbuf_tensor("bgC3", [128, 16], F32))
            gCcur = PE(nc.sbuf_tensor("gCcur3", [128, 16], F32))
            clast = PE(nc.sbuf_tensor("clast3", [1, 16], F32))
            s0buf = PE(nc.sbuf_tensor("s0buf3", [128, H, DV], F32))
            s1buf = PE(nc.sbuf_tensor("s1buf3", [128, H, DV], F32))
            pCR = PE(nc.psum_tensor("pCR3", [128, 512], F32))
            pMM = PE(nc.psum_tensor("pMM3", [128, 512], F32))
            pN1 = PE(nc.psum_tensor("pN13", [128, 512], F32))
            pN2 = PE(nc.psum_tensor("pN23", [128, 512], F32))
            pU = PE(nc.psum_tensor("pU3", [128, 512], F32))
            pO = PE(nc.psum_tensor("pO3", [128, 512], F32))
            pS = PE(nc.psum_tensor("pS3", [128, 512], F32))
            pOT = PE(nc.psum_tensor("pOT3", [128, 512], BF16))

            QKS = b.slot_sems("QKS3", 2)
            UVS = b.slot_sems("UVS3", 2)
            OFS = b.slot_sems("OFS3", 2)
            te_qk = [0, 0]
            wr_uv = [0, 0]
            se_uv = [0, 0]
            ve_of = [0, 0]

            ktokv = scr["ktok"].ap()
            vtokv = scr["vtok"].ap()

            def load_qkv(ch, sl):
                if te_qk[sl]:
                    b.wait("sync", TE, te_qk[sl])
                cs = slice(ch * CCH, (ch + 1) * CCH)
                b.dma("sync", qf[:, sl], qfmv[:, :, cs], QKS[sl])
                b.dma("sync", kf[:, sl], kfmv[:, :, cs], QKS[sl])
                b.dma("sync", kt[:, sl],
                      ktokv[:, cs, :].rearrange("h t d -> t h d"), QKS[sl])
                b.dma("sync", vt[:, sl],
                      vtokv[:, cs, :].rearrange("h t d -> t h d"), QKS[sl])

            def load_pass2(ch, sl):
                if wr_uv[sl]:
                    b.wait("sync", VE, wr_uv[sl])
                if te_qk[sl]:
                    b.wait("sync", TE, te_qk[sl])
                if uv_store[sl]:
                    b.wait("sync", UVS[sl], uv_store[sl])
                b.wait("sync", SE, se_uv[sl])
                b.dma("sync", Uv[:, sl], scr["uv_s"].ap()[ch].rearrange("p (h c) -> p h c", h=H), UVS[sl])
                b.dma("sync", WiT[:, sl], scr["wiT_s"].ap()[ch].rearrange("p (h c) -> p h c", h=H), UVS[sl])
                b.dma("sync", W2T[:, sl], scr["w2T_s"].ap()[ch].rearrange("p (h c) -> p h c", h=H), UVS[sl])
                b.dma("sync", Qh[:, sl], scr["qh_s"].ap()[ch].rearrange("p (h c) -> p h c", h=H), UVS[sl])
                b.dma("sync", Kb[:, sl], scr["kb_s"].ap()[ch].rearrange("p (h c) -> p h c", h=H), UVS[sl])
                b.dma("sync", gcc[:, sl], scr["gcc_s"].ap()[ch], UVS[sl])

            def sweep(pass2):
                for ch in range(NCH):
                    sl = ch % 2
                    if not pass2:
                        if ch == 0:
                            load_qkv(0, 0)
                        if ch + 1 < NCH:
                            load_qkv(ch + 1, (ch + 1) % 2)
                        b.wait("vector", QKS[sl], QKS[sl].n)
                        b.wait("tensor", QKS[sl], QKS[sl].n)
                        if uv_store[sl]:
                            b.wait("vector", UVS[sl], uv_store[sl])
                            b.wait("scalar", UVS[sl], uv_store[sl])
                    else:
                        if ch == 0:
                            load_pass2(0, 0)
                        if ch + 1 < NCH:
                            load_pass2(ch + 1, (ch + 1) % 2)
                        b.wait("tensor", UVS[sl], UVS[sl].n)
                        b.wait("vector", UVS[sl], UVS[sl].n)
                    cs = slice(ch * CCH, (ch + 1) * CCH)
                    if not pass2:
                        VR.tensor_tensor_scan(cT[:, :], ones16[:, 0:CCH],
                                              g_res[:, cs], 0.0, ALU.mult, ALU.add)
                        v = b.inc("vector", VE)
                        b.wait("scalar", VE, v)
                        SC.activation(lnbT[:, :], beta_res[:, cs], AF.Ln)
                        sln = b.inc("scalar", SE)
                        b.wait("vector", SE, sln)
                        VR.tensor_add(cplusT[:, :], cT[:, :], lnbT[:, :])
                        for j in range(4):
                            VR.tensor_copy(cT4[:, j * 128:(j + 1) * 128], cT[:, :])
                        v = b.inc("vector", VE)
                        b.wait("tensor", VE, v)
                        TN.transpose(pCR[0:128, 0:16], cT[:, :], identF[0:16, 0:16])
                        TN.transpose(pCR[0:128, 16:32], cplusT[:, :], identF[0:16, 0:16])
                        t = b.inc("tensor", TE)
                        b.wait("vector", TE, t)
                        VR.tensor_copy(cols[:, 0:32], pCR[:, 0:32])
                        v = b.inc("vector", VE)
                        b.wait("tensor", VE, v)
                        TN.transpose(pCR[0:128, 32:48], beta_res[:, cs], identF[0:16, 0:16])
                        TN.transpose(pCR[0:1, 48:64], cT[:, 127:128], identF[0:16, 0:16])
                        t = b.inc("tensor", TE)
                        b.wait("vector", TE, t)
                        VR.tensor_copy(cols[:, 32:48], pCR[:, 32:48])
                        VR.tensor_copy(clast[:, :], pCR[0:1, 48:64])
                        VR.tensor_scalar(negcC[:, :], cols[:, 0:16], -1.0, None, ALU.mult)
                        v = b.inc("vector", VE)
                        b.wait("tensor", VE, v)
                        TN.matmul(pCR[:, 0:16], onesRowF[:], clast[:, :], start=True, stop=True)
                        t = b.inc("tensor", TE)
                        b.wait("vector", TE, t)
                        VR.tensor_sub(cols[:, 48:64], pCR[:, 0:16], cols[:, 0:16])
                        v = b.inc("vector", VE)
                        b.wait("scalar", VE, v)
                        SC.activation(ratC[:, :], cols[:, 48:64], AF.Exp)
                        SC.activation(gCcur[:, :], pCR[:, 0:16], AF.Exp)
                        SC.activation(bgC[:, :], cols[:, 16:32], AF.Exp)
                        sexp = b.inc("scalar", SE)
                        b.wait("vector", SE, sexp)
                        b.wait("tensor", SE, sexp)
                        VR.tensor_copy(gcc[:, sl], gCcur[:, :])
                        v = b.inc("vector", VE)
                        b.wait("scalar", VE, v)
                        b.dma("scalar", scr["gcc_s"].ap()[ch], gcc[:, sl], UVS[sl])
                    else:
                        pass

                    if DBG_OUT and pass2 and ch == 0 and "scomb_dbg" in DBG_OUT:
                        b.wait("scalar", VE, VE.n)
                        b.wait("scalar", UVS[sl], UVS[sl].n)
                        pd = scr["pcdbg"].ap().rearrange("p a c -> p (a c)")
                        b.dma("scalar", pd[:, 0:2048],
                              S0b[:, :, :].rearrange("p a c -> p (a c)"), ST)
                        b.dma("scalar", pd[:, 2048:4096],
                              Uv[:, sl].rearrange("p a c -> p (a c)"), ST)
                        b.dma("scalar", pd[:, 4096:6144],
                              WiT[:, sl].rearrange("p a c -> p (a c)"), ST)
                        b.dma("scalar", pd[:, 6144:8192],
                              W2T[:, sl].rearrange("p a c -> p (a c)"), ST)
                        for e_ in ENGINES:
                            b.wait(e_, ST)
                    for g in range(4):
                        hs = slice(4 * g, 4 * g + 4)
                        if not pass2:
                            VR.tensor_mul(msk4[:, :], cT4[:, :], selm4[:, g, :])
                            VR.tensor_scalar(nmsk4[:, :], msk4[:, :], -1.0, None, ALU.mult)
                            v = b.inc("vector", VE)
                            b.wait("tensor", VE, v)
                            TN.matmul(pCR[:, 0:512], ones16[:, 0:128], msk4[:, :],
                                      start=True, stop=False)
                            t = b.inc("tensor", TE)
                            b.wait("scalar", TE, t)
                            SC.activation(Grow4[:, :], pCR[:, 0:512], AF.Exp)
                            sg_ = b.inc("scalar", SE)
                            b.wait("tensor", SE, sg_)
                            TN.matmul(pCR[:, 0:512], mUIT[:, :], Ib4[:, :],
                                      start=False, stop=True)
                            t = b.inc("tensor", TE)
                            b.wait("scalar", TE, t)
                            for j in range(4):
                                SC.activation(DpT4[:, j * 128:(j + 1) * 128],
                                              pCR[:, j * 128:(j + 1) * 128], AF.Exp,
                                              bias=negcC[:, 4 * g + j:4 * g + j + 1])
                            s_ = b.inc("scalar", SE)
                            b.wait("tensor", SE, s_)
                            TN.matmul(pN1[:, 0:512], ones16[:, 0:128], nmsk4[:, :],
                                      start=True, stop=False)
                            TN.matmul(pN1[:, 0:512], mLST[:, :], Ib4[:, :],
                                      start=False, stop=True)
                            t = b.inc("tensor", TE)
                            b.wait("scalar", TE, t)
                            for j in range(4):
                                SC.activation(DsL4[:, j * 128:(j + 1) * 128],
                                              pN1[:, j * 128:(j + 1) * 128], AF.Exp,
                                              bias=cols[:, 16 + 4 * g + j:17 + 4 * g + j])
                            s_ = b.inc("scalar", SE)
                            b.wait("tensor", SE, s_)
                            for j in range(4):
                                TN.matmul(pMM[:, j * 128:(j + 1) * 128],
                                          kf[:, sl, 4 * g + j, :], kf[:, sl, 4 * g + j, :],
                                          start=(j == 0), stop=(j == 3))
                            t = b.inc("tensor", TE)
                            b.wait("vector", TE, t)
                            VR.tensor_mul(MPOS4[:, :], DsL4[:, :], pMM[:, 0:512])
                            v = b.inc("vector", VE)
                            b.wait("tensor", VE, v)
                            for j in range(4):
                                TN.transpose(pOT[:, j * 128:(j + 1) * 128],
                                             MPOS4[:, j * 128:(j + 1) * 128], identB[:, :])
                            t = b.inc("tensor", TE)
                            b.wait("scalar", TE, t)
                            SC.copy(MT4[:, :], pOT[:, 0:512])
                            s_ = b.inc("scalar", SE)
                            b.wait("vector", SE, s_)
                            VR.tensor_add(A4[:, :], MPOS4[:, :], Ib4[:, :])
                            VR.scalar_tensor_tensor(W4[:, :], MT4[:, :], -1.0, Ib4[:, :],
                                                    ALU.mult, ALU.add)
                            VR.scalar_tensor_tensor(V4[:, :], MPOS4[:, :], -1.0, Ib4[:, :],
                                                    ALU.mult, ALU.add)
                            v = b.inc("vector", VE)
                            b.wait("tensor", VE, v)
                            for j in range(4):
                                TN.matmul(pMM[:, j * 128:(j + 1) * 128],
                                          kf[:, sl, 4 * g + j, :], qf[:, sl, 4 * g + j, :],
                                          start=(j == 0), stop=(j == 3))
                            t = b.inc("tensor", TE)
                            b.wait("vector", TE, t)
                            VR.tensor_mul(W2T[:, sl, hs, :].rearrange("p a c -> p (a c)"),
                                          DpT4[:, :], pMM[:, 0:512])
                            VR.tensor_mul(Qh[:, sl, hs, :].rearrange("p a c -> p (a c)"),
                                          qf[:, sl, hs, :].rearrange("p a c -> p (a c)"),
                                          Grow4[:, :])
                            for j in range(4):
                                VR.tensor_scalar(Zv4[:, j * 128:(j + 1) * 128],
                                                 vt[:, sl, 4 * g + j, :],
                                                 cols[:, 32 + 4 * g + j:33 + 4 * g + j],
                                                 None, ALU.mult)
                                VR.tensor_scalar(Y4[:, j * 128:(j + 1) * 128],
                                                 kt[:, sl, 4 * g + j, :],
                                                 bgC[:, 4 * g + j:4 * g + j + 1],
                                                 None, ALU.mult)
                                VR.tensor_scalar(Kb[:, sl, 4 * g + j, :],
                                                 kt[:, sl, 4 * g + j, :],
                                                 ratC[:, 4 * g + j:4 * g + j + 1],
                                                 None, ALU.mult)
                            v = b.inc("vector", VE)
                            b.wait("tensor", VE, v)
                            for it_ in range(6):
                                for j in range(4):
                                    TN.matmul(pN1[:, j * 128:(j + 1) * 128],
                                              A4[:, j * 128:(j + 1) * 128],
                                              W4[:, j * 128:(j + 1) * 128],
                                              start=(j == 0), stop=(j == 3))
                                t = b.inc("tensor", TE)
                                b.wait("vector", TE, t)
                                VR.scalar_tensor_tensor(G4[:, :], pN1[:, 0:512], -1.0,
                                                        twoIb4[:, :], ALU.mult, ALU.add)
                                v = b.inc("vector", VE)
                                b.wait("tensor", VE, v)
                                for j in range(4):
                                    TN.matmul(pN2[:, j * 128:(j + 1) * 128],
                                              V4[:, j * 128:(j + 1) * 128],
                                              G4[:, j * 128:(j + 1) * 128],
                                              start=(j == 0), stop=(j == 3))
                                for j in range(4):
                                    TN.matmul(pN1[:, j * 128:(j + 1) * 128],
                                              G4[:, j * 128:(j + 1) * 128],
                                              V4[:, j * 128:(j + 1) * 128],
                                              start=(j == 0), stop=(j == 3))
                                t = b.inc("tensor", TE)
                                b.wait("vector", TE, t)
                                b.wait("scalar", TE, t)
                                SC.copy(W4[:, :], pN2[:, 0:512])
                                s_ = b.inc("scalar", SE)
                                if it_ < 5:
                                    VR.tensor_copy(V4[:, :], pN1[:, 0:512])
                                    v = b.inc("vector", VE)
                                    b.wait("tensor", VE, v)
                                b.wait("tensor", SE, s_)
                                b.wait("vector", SE, s_)
                            for j in range(4):
                                TN.matmul(pN2[:, j * 128:(j + 1) * 128],
                                          W4[:, j * 128:(j + 1) * 128],
                                          Zv4[:, j * 128:(j + 1) * 128],
                                          start=(j == 0), stop=(j == 3))
                            for j in range(4):
                                TN.matmul(pN1[:, j * 128:(j + 1) * 128],
                                          Y4[:, j * 128:(j + 1) * 128],
                                          W4[:, j * 128:(j + 1) * 128],
                                          start=(j == 0), stop=(j == 3))
                            t = b.inc("tensor", TE)
                            b.wait("vector", TE, t)
                            b.wait("scalar", TE, t)
                            VR.tensor_copy(Uv[:, sl, hs, :].rearrange("p a c -> p (a c)"),
                                           pN2[:, 0:512])
                            SC.copy(WiT[:, sl, hs, :].rearrange("p a c -> p (a c)"),
                                    pN1[:, 0:512])
                            v = b.inc("vector", VE)
                            s_ = b.inc("scalar", SE)
                            b.wait("tensor", VE, v)
                            b.wait("tensor", SE, s_)
                        for j in range(4):
                            TN.matmul(pU[:, j * 128:(j + 1) * 128],
                                      WiT[:, sl, 4 * g + j, :], S0b[:, 4 * g + j, :],
                                      start=(j == 0), stop=(j == 3))
                        t = b.inc("tensor", TE)
                        b.wait("vector", TE, t)
                        VR.scalar_tensor_tensor(Ua[:, hs, :].rearrange("p a c -> p (a c)"),
                                                pU[:, 0:512], -1.0,
                                                Uv[:, sl, hs, :].rearrange("p a c -> p (a c)"),
                                                ALU.mult, ALU.add)
                        v = b.inc("vector", VE)
                        b.wait("tensor", VE, v)
                        for j in range(4):
                            TN.matmul(pO[:, j * 128:(j + 1) * 128],
                                      Qh[:, sl, 4 * g + j, :], S0b[:, 4 * g + j, :],
                                      start=(j == 0), stop=False)
                            TN.matmul(pO[:, j * 128:(j + 1) * 128],
                                      W2T[:, sl, 4 * g + j, :], Ua[:, 4 * g + j, :],
                                      start=False, stop=(j == 3))
                        t = b.inc("tensor", TE)
                        b.wait("vector", TE, t)
                        VR.tensor_copy(ot4[:, :], pO[:, 0:512])
                        v = b.inc("vector", VE)
                        b.wait("tensor", VE, v)
                        for j in range(4):
                            TN.transpose(pOT[:, j * 128:(j + 1) * 128],
                                         ot4[:, j * 128:(j + 1) * 128], identB[:, :])
                        for j in range(4):
                            TN.matmul(pS[:, j * 128:(j + 1) * 128],
                                      Kb[:, sl, 4 * g + j, :], Ua[:, 4 * g + j, :],
                                      start=(j == 0), stop=(j == 3))
                        t = b.inc("tensor", TE)
                        b.wait("scalar", TE, t)
                        b.wait("vector", TE, t)
                        if ve_of[sl] and g == 0:
                            b.wait("scalar", OFS[sl], ve_of[sl])
                        SC.copy(ofmb[:, sl, hs, :].rearrange("p a c -> p (a c)"),
                                pOT[:, 0:512])
                        s_ = b.inc("scalar", SE)
                        for j in range(4):
                            VR.tensor_scalar(S[:, 4 * g + j, :], S[:, 4 * g + j, :],
                                             gCcur[:, 4 * g + j:4 * g + j + 1] if not pass2
                                             else gcc[:, sl, 4 * g + j:4 * g + j + 1],
                                             None, ALU.mult)
                        VR.tensor_add(S[:, hs, :].rearrange("p a c -> p (a c)"),
                                      S[:, hs, :].rearrange("p a c -> p (a c)"),
                                      pS[:, 0:512])
                        v = b.inc("vector", VE)
                        b.wait("tensor", VE, v)
                        b.wait("scalar", VE, v)
                    if DBG_OUT and pass2 and ch == 0 and g == 3:
                        b.wait("scalar", VE, VE.n)
                        b.dma("scalar", scr["pcdbg"].ap().rearrange("p a c -> p (a c)")[:, 0:2048]
                              .rearrange("p (a c) -> p a c", a=16),
                              Ua[:, :, :], ST)
                        b.dma("scalar", scr["pcdbg"].ap().rearrange("p a c -> p (a c)")[:, 2048:4096]
                              .rearrange("p (a c) -> p a c", a=16),
                              S0b[:, :, :], ST)
                        b.wait("scalar", ST)
                    VR.tensor_copy(S0b[:, :, :], S[:, :, :])
                    v_s0 = b.inc("vector", VE)
                    b.wait("tensor", VE, v_s0)
                    b.wait("scalar", SE, SE.n)
                    ve_of[sl] = b.dma("scalar", ofmv[:, :, cs], ofmb[:, sl], OFS[sl])
                    if not pass2:
                        b.wait("scalar", VE, VE.n)
                        b.wait("scalar", SE, SE.n)
                        for nm, tns in (("uv_s", Uv), ("wiT_s", WiT), ("w2T_s", W2T),
                                        ("qh_s", Qh), ("kb_s", Kb)):
                            b.dma("scalar", scr[nm].ap()[ch],
                                  tns[:, sl].rearrange("p a c -> p (a c)"), UVS[sl])
                        uv_store[sl] = UVS[sl].n
                        wr_uv[sl] = VE.n
                        te_qk[sl] = TE.n
                        se_uv[sl] = SE.n
                    else:
                        wr_uv[sl] = VE.n
                        te_qk[sl] = TE.n
                        se_uv[sl] = SE.n

            uv_store = [0, 0]
            VR.memset(S[:, :, :], 0.0)
            v0 = b.inc("vector", VE)
            b.wait("vector", VE, v0)
            VR.tensor_copy(S0b[:, :, :], S[:, :, :])
            b.inc("vector", VE)
            b.wait("tensor", VE, VE.n)
            b.wait("scalar", VE, VE.n)
            sweep(False)
            b.wait("scalar", VE, VE.n)
            b.dma("scalar", scr["cc_in"].ap(), S[:, :, :].rearrange("p a c -> p (a c)"), ST)
            b.wait("gpsimd", ST)
            GP.collective_compute(
                "AllGather", ALU.bypass,
                replica_groups=[[0, 1], [2, 3], [4, 5], [6, 7]],
                ins=[scr["cc_in"].ap()], outs=[scr["cc_out"].ap()],
            )
            b.inc("gpsimd", b.sems["CC"])
            b.wait("gpsimd", b.sems["CC"], b.sems["CC"].n)
            b.dma("gpsimd", s0buf[:, :, :],
                  scr["cc_out"].ap()[0:128].rearrange("p (a c) -> p a c", a=H), CA)
            b.dma("gpsimd", s1buf[:, :, :],
                  scr["cc_out"].ap()[128:256].rearrange("p (a c) -> p a c", a=H), CA)
            b.wait("vector", CA)
            b.wait("vector", OFS[0]); b.wait("vector", OFS[1])
            b.wait("vector", UVS[0]); b.wait("vector", UVS[1])
            VR.tensor_scalar(S[:, :, :], s0buf[:, :, :], mA[:, 0:1], None, ALU.mult)
            VR.tensor_scalar(s1buf[:, :, :], s1buf[:, :, :], mB[:, 0:1], None, ALU.mult)
            VR.tensor_add(S[:, :, :], S[:, :, :], s1buf[:, :, :])
            VR.tensor_copy(S0b[:, :, :], S[:, :, :])
            v = b.inc("vector", VE)
            b.wait("tensor", VE, v)
            b.wait("scalar", VE, v)
            if DBG_OUT:
                b.dma("scalar", scr["scomb_dbg"].ap(),
                      S[:, :, :].rearrange("p a c -> p (a c)"), ST)
                b.wait("scalar", ST)
            if not P3_SWEEP1_ONLY:
                sweep(True)
            b.barrier()

    def owo_phase(hga_res):
        with ExitStack() as ps:
            PE = ps.enter_context
            op_ = PE(nc.sbuf_tensor("op4", [128, NK, 512], BF16))
            x1p = PE(nc.sbuf_tensor("x1p4", [128, NK, 512], F32))
            ap_ = PE(nc.sbuf_tensor("ap4", [128, NK, 512], F32))
            sq4 = PE(nc.sbuf_tensor("sq4", [128, 2, NK, 128], F32))
            ogb = PE(nc.sbuf_tensor("ogb4", [128, NK, 512], BF16))
            sgall = PE(nc.sbuf_tensor("sgall4", [128, NK, 512], BF16))
            sqo = PE(nc.sbuf_tensor("sqo4", [128, 4, 512], F32))
            wgb_s = PE(nc.sbuf_tensor("wgb_s4", [128, D], BF16))
            colw64 = PE(nc.sbuf_tensor("colw644", [128, 64], F32))
            lrow4 = PE(nc.sbuf_tensor("lrow44", [4, 2, 128], F32))
            onw4 = PE(nc.sbuf_tensor("onw44", [4, 4, 128], F32))
            wob = PE(nc.sbuf_tensor("wob4", [128, 2, 512], BF16))
            stat = PE(nc.sbuf_tensor("stat4", [128, 32], F32))
            colw = PE(nc.sbuf_tensor("colw4", [128, 16], F32))
            rowt = PE(nc.sbuf_tensor("rowt4", [16, 128], F32))
            pA = PE(nc.psum_tensor("pA4", [128, 512], F32))
            pB = PE(nc.psum_tensor("pB4", [128, 512], F32))
            pC = PE(nc.psum_tensor("pC4", [128, 512], F32))
            pD = PE(nc.psum_tensor("pD4", [128, 512], F32))
            pE = PE(nc.psum_tensor("pE4", [128, 2048], F32))

            WOS = b.slot_sems("WOS4", 2)
            te_wo = [0, 0]
            te_sq4 = [0, 0]
            b.dma("gpsimd", wgb_s[:, :], ap["wgb"], CA)
            b.dma("sync", onw4[:, :, :], ap["onw4"], LD)
            b.wait("tensor", CA); b.wait("tensor", LD)
            se_g = {}
            ve_og = {}
            te_sqo = [0, 0, 0, 0]

            st_prev = 0
            for ptag in range(4):
                T = 512
                ntile = nt = 4
                b.wait("vector", ST, st_prev)
                b.wait("scalar", ST, st_prev)
                b.wait("sync", TE, TE.n); b.wait("sync", VE, VE.n)
                cs = slice(ptag * 512, (ptag + 1) * 512)
                b.dma("sync", op_[:, :, :], ofmv[:, :, cs], LD)
                b.dma("sync", x1p[:, :, :], x1v[:, :, THALO + ptag * 512:THALO + (ptag + 1) * 512], LD)
                b.wait("scalar", LD); b.wait("tensor", LD); b.wait("vector", LD)
                b.wait("tensor", SE, SE.n)
                b.wait("tensor", VE, VE.n)
                for r in range(ntile):
                    if te_sq4[r % 2]:
                        b.wait("scalar", TE, te_sq4[r % 2])
                    SC.activation(sq4[:, r % 2, :, :], x1p[:, :, r * 128:(r + 1) * 128], AF.Square)
                    s = b.inc("scalar", SE)
                    b.wait("tensor", SE, s)
                    for k in range(NK):
                        TN.matmul(pC[:, r:r + 1], x1p[:, k, r * 128:(r + 1) * 128],
                                  onesColF[:], start=(r == 0 and k == 0), stop=False)
                        TN.matmul(pC[:, 8 + r:9 + r], sq4[:, r % 2, k, :], onesColF[:],
                                  start=False, stop=(r == ntile - 1 and k == NK - 1))
                    te_sq4[r % 2] = b.inc("tensor", TE)
                b.wait("vector", TE, TE.n)
                VR.memset(colw[:, :], 0.0)
                VR.tensor_copy(stat[:, 0:nt], pC[:, 0:nt])
                VR.tensor_copy(stat[:, 8:8 + nt], pC[:, 8:8 + nt])
                VR.tensor_scalar(colw[:, 0:nt], stat[:, 0:nt], 1.0 / D, None, ALU.mult)
                VR.tensor_mul(colw[:, 8:8 + nt], colw[:, 0:nt], colw[:, 0:nt])
                VR.scalar_tensor_tensor(colw[:, 8:8 + nt], stat[:, 8:8 + nt], 1.0 / D,
                                        colw[:, 8:8 + nt], ALU.mult, ALU.subtract)
                v = b.inc("vector", VE)
                b.wait("scalar", VE, v)
                SC.activation(colw[:, 8:8 + nt], colw[:, 8:8 + nt], AF.Sqrt, bias=np.float32(1e-5))
                s = b.inc("scalar", SE)
                b.wait("vector", SE, s)
                VR.reciprocal(colw[:, 8:8 + nt], colw[:, 8:8 + nt])
                VR.tensor_mul(colw[:, 0:nt], colw[:, 0:nt], colw[:, 8:8 + nt])
                VR.tensor_scalar(colw[:, 0:nt], colw[:, 0:nt], -1.0, None, ALU.mult)
                v = b.inc("vector", VE)
                b.wait("tensor", VE, v)
                TN.transpose(pC[0:16, 0:128], colw[:, 0:16], identF[:, :])
                t = b.inc("tensor", TE)
                b.wait("vector", TE, t)
                VR.tensor_copy(rowt[:, :], pC[0:16, 0:128])
                v = b.inc("vector", VE)
                b.wait("tensor", VE, v)
                for r in range(ntile):
                    TN.matmul(pA[:, r * 128:(r + 1) * 128],
                              sel16[:, (8 + r) * 128:(9 + r) * 128], rowt[:, :],
                              start=(r == 0), stop=(r == ntile - 1))
                    TN.matmul(pB[:, r * 128:(r + 1) * 128],
                              sel16[:, r * 128:(r + 1) * 128], rowt[:, :],
                              start=(r == 0), stop=(r == ntile - 1))
                t = b.inc("tensor", TE)
                b.wait("vector", TE, t)
                for k in range(NK):
                    VR.tensor_mul(ap_[:, k, :], x1p[:, k, :], pA[:, 0:512])
                    VR.tensor_add(ap_[:, k, :], ap_[:, k, :], pB[:, 0:512])
                v_a = b.inc("vector", VE)
                b.wait("tensor", VE, v_a)
                for m in range(NK):
                    pX = pA if m % 2 == 0 else pB
                    if m >= 2:
                        b.wait("tensor", SE, se_g[m - 2])
                    TN.matmul(pX[:, 0:512], wgb_s[:, m * 128:(m + 1) * 128],
                              hga_res[:, cs], start=True, stop=True)
                    t = b.inc("tensor", TE)
                    b.wait("scalar", TE, t)
                    SC.activation(sgall[:, m, :], pX[:, 0:512], AF.Silu)
                    se_g[m] = b.inc("scalar", SE)
                for m in range(NK):
                    if m >= 4 and te_sqo[m % 4]:
                        b.wait("scalar", TE, te_sqo[m % 4])
                    SC.activation(sqo[:, m % 4, :], op_[:, m, :], AF.Square)
                    sse = b.inc("scalar", SE)
                    b.wait("tensor", SE, sse)
                    for r in range(4):
                        TN.matmul(pC[:, 4 * m + r:4 * m + r + 1],
                                  sqo[:, m % 4, r * 128:(r + 1) * 128], onesColF[:],
                                  start=(m == 0 and r == 0),
                                  stop=(m == NK - 1 and r == 3))
                    te_rms = te_sqo[m % 4] = b.inc("tensor", TE)
                b.wait("vector", TE, te_rms)
                VR.tensor_copy(colw64[:, :], pC[:, 0:64])
                v = b.inc("vector", VE)
                b.wait("scalar", VE, v)
                SC.activation(colw64[:, :], colw64[:, :], AF.Sqrt,
                              bias=np.float32(1e-6), scale=np.float32(1.0 / 128.0))
                srms = b.inc("scalar", SE)
                b.wait("vector", SE, srms)
                VR.reciprocal(colw64[:, :], colw64[:, :])
                v = b.inc("vector", VE)
                b.wait("tensor", VE, v)
                b.wait("tensor", SE, SE.n)
                for m in range(NK):
                    pX = pA if m % 2 == 0 else pB
                    TN.transpose(pD[0:4, (m % 2) * 128:(m % 2) * 128 + 128],
                                 colw64[:, 4 * m:4 * m + 4], identF[:, :])
                    t = b.inc("tensor", TE)
                    b.wait("vector", TE, t)
                    VR.tensor_copy(lrow4[:, m % 2, :], pD[0:4, (m % 2) * 128:(m % 2) * 128 + 128])
                    vlr = b.inc("vector", VE)
                    b.wait("tensor", VE, vlr)
                    if m >= 2:
                        b.wait("tensor", VE, ve_og[m - 2])
                    for r in range(4):
                        TN.matmul(pX[:, r * 128:(r + 1) * 128],
                                  onw4[:, r, :], lrow4[:, m % 2, :],
                                  start=(r == 0), stop=(r == 3))
                    t = b.inc("tensor", TE)
                    b.wait("vector", TE, t)
                    VR.tensor_mul(ogb[:, m, :], op_[:, m, :], pX[:, 0:512])
                    VR.tensor_mul(ogb[:, m, :], ogb[:, m, :], sgall[:, m, :])
                    ve_og[m] = b.inc("vector", VE)
                b.wait("tensor", VE, VE.n)
                for dq in range(4):
                    for k in range(NK):
                        ksl = k % 2
                        if te_wo[ksl]:
                            b.wait("gpsimd", TE, te_wo[ksl])
                        b.dma("gpsimd", wob[:, ksl, :],
                              ap["wo"][k * 128:(k + 1) * 128, dq * 512:(dq + 1) * 512],
                              WOS[ksl])
                        b.wait("tensor", WOS[ksl], WOS[ksl].n)
                        for dd in range(4):
                            TN.matmul(pE[:, dd * 512:dd * 512 + 512],
                                      wob[:, ksl, dd * 128:(dd + 1) * 128],
                                      ogb[:, k, :], start=(k == 0), stop=(k == NK - 1))
                        te_wo[ksl] = b.inc("tensor", TE)
                    b.wait("vector", TE, TE.n)
                    for dd in range(4):
                        d_ = dq * 4 + dd
                        VR.scalar_tensor_tensor(ap_[:, d_, :], pE[:, dd * 512:dd * 512 + 512],
                                                1.0, ap_[:, d_, :], ALU.mult, ALU.add)
                    ve_pe = b.inc("vector", VE)
                    b.wait("tensor", VE, ve_pe)
                b.wait("scalar", VE, VE.n)
                st_prev = b.dma("scalar", xiv[:, :, cs], ap_[:, :, :], ST)
            b.barrier()

    def mlp_up_phase():
        with ExitStack() as ps:
            PE = ps.enter_context
            h2f = PE(nc.sbuf_tensor("h2f5", [128, NK, TOWN], BF16))
            xip = PE(nc.sbuf_tensor("xip5", [128, NK, 512], F32))
            sq5 = PE(nc.sbuf_tensor("sq5", [128, 2, NK, 128], F32))
            wgb = PE(nc.sbuf_tensor("wgb5", [128, 2, NK, 128], BF16))
            wub = PE(nc.sbuf_tensor("wub5", [128, 2, NK, 128], BF16))
            sgl = PE(nc.sbuf_tensor("sgl5", [128, 2048], BF16))
            mb = PE(nc.sbuf_tensor("mb5", [128, 2, 2048], BF16))
            stat = PE(nc.sbuf_tensor("stat5", [128, 32], F32))
            colw = PE(nc.sbuf_tensor("colw5", [128, 16], F32))
            rowt = PE(nc.sbuf_tensor("rowt5", [16, 128], F32))
            ln2c = PE(nc.sbuf_tensor("ln2c5", [128, 16], F32))
            pG = PE(nc.psum_tensor("pG5", [128, 2048], F32))
            pU = PE(nc.psum_tensor("pU5", [128, 2048], F32))
            WG5 = b.slot_sems("WG5", 2)
            MS5 = b.slot_sems("MS5", 2)
            te_w5 = [0, 0]
            ve_m5 = [0, 0]
            b.dma("sync", ln2c[:, :], ap["ln2_wT"], LD)
            b.wait("vector", LD)
            for ptag in range(4):
                nt = 4
                b.wait("sync", TE, TE.n); b.wait("sync", VE, VE.n)
                b.dma("sync", xip[:, :, :], xiv[:, :, ptag * 512:(ptag + 1) * 512], LD)
                b.wait("scalar", LD); b.wait("tensor", LD); b.wait("vector", LD)
                b.wait("tensor", SE, SE.n); b.wait("tensor", VE, VE.n)
                te_sq5 = {}
                for r in range(4):
                    if r >= 2:
                        b.wait("scalar", TE, te_sq5[r - 2])
                    SC.activation(sq5[:, r % 2, :, :], xip[:, :, r * 128:(r + 1) * 128], AF.Square)
                    s = b.inc("scalar", SE)
                    b.wait("tensor", SE, s)
                    for k in range(NK):
                        TN.matmul(pG[:, 8 + r:9 + r], sq5[:, r % 2, k, :], onesColF[:],
                                  start=(r == 0 and k == 0),
                                  stop=(r == 3 and k == NK - 1))
                    te_sq5[r] = b.inc("tensor", TE)
                b.wait("vector", TE, TE.n)
                VR.memset(colw[:, :], 0.0)
                VR.tensor_copy(stat[:, 8:8 + nt], pG[:, 8:8 + nt])
                v = b.inc("vector", VE)
                b.wait("scalar", VE, v)
                SC.activation(colw[:, 8:8 + nt], stat[:, 8:8 + nt], AF.Sqrt,
                              bias=np.float32(1e-6), scale=np.float32(1.0 / D))
                s = b.inc("scalar", SE)
                b.wait("vector", SE, s)
                VR.reciprocal(colw[:, 8:8 + nt], colw[:, 8:8 + nt])
                v = b.inc("vector", VE)
                b.wait("tensor", VE, v)
                TN.transpose(pG[0:16, 0:128], colw[:, 0:16], identF[:, :])
                t = b.inc("tensor", TE)
                b.wait("vector", TE, t)
                VR.tensor_copy(rowt[:, :], pG[0:16, 0:128])
                v = b.inc("vector", VE)
                b.wait("tensor", VE, v)
                for r in range(4):
                    TN.matmul(pU[:, r * 128:(r + 1) * 128],
                              sel16[:, (8 + r) * 128:(9 + r) * 128], rowt[:, :],
                              start=(r == 0), stop=(r == 3))
                t = b.inc("tensor", TE)
                b.wait("vector", TE, t)
                for k in range(NK):
                    VR.scalar_tensor_tensor(h2f[:, k, ptag * 512:(ptag + 1) * 512],
                                            xip[:, k, :], ln2c[:, k:k + 1],
                                            pU[:, 0:512], ALU.mult, ALU.mult)
                v = b.inc("vector", VE)
                b.wait("tensor", VE, v)
            wgv = ap["w_gate"].rearrange("(k r) i -> r k i", r=128)
            wuv = ap["w_up"].rearrange("(k r) i -> r k i", r=128)
            for i in range(64):
                isl = i % 2
                if te_w5[isl]:
                    b.wait("gpsimd", TE, te_w5[isl])
                b.dma("gpsimd", wgb[:, isl], wgv[:, :, i * 128:(i + 1) * 128], WG5[isl])
                b.dma("gpsimd", wub[:, isl], wuv[:, :, i * 128:(i + 1) * 128], WG5[isl])
                b.wait("tensor", WG5[isl], WG5[isl].n)
                for tg in range(4):
                    for k in range(NK):
                        TN.matmul(pG[:, tg * 512:(tg + 1) * 512], wgb[:, isl, k, :],
                                  h2f[:, k, tg * 512:(tg + 1) * 512],
                                  start=(k == 0), stop=(k == NK - 1))
                for tg in range(4):
                    for k in range(NK):
                        TN.matmul(pU[:, tg * 512:(tg + 1) * 512], wub[:, isl, k, :],
                                  h2f[:, k, tg * 512:(tg + 1) * 512],
                                  start=(k == 0), stop=(k == NK - 1))
                te_w5[isl] = b.inc("tensor", TE)
                b.wait("scalar", TE, te_w5[isl])
                SC.activation(sgl[:, :], pG[:, 0:2048], AF.Silu)
                s = b.inc("scalar", SE)
                b.wait("vector", SE, s)
                if ve_m5[isl]:
                    b.wait("vector", MS5[isl], ve_m5[isl])
                VR.tensor_mul(mb[:, isl, :], sgl[:, :], pU[:, 0:2048])
                v = b.inc("vector", VE)
                b.wait("tensor", VE, v)
                b.wait("scalar", VE, v)
                ve_m5[isl] = b.dma("scalar", scr["mscr"].ap()[i], mb[:, isl, :], MS5[isl])
            b.barrier()

    def mlp_down_phase():
        with ExitStack() as ps:
            PE = ps.enter_context
            mb6 = PE(nc.sbuf_tensor("mb65", [128, 64, 1024], BF16))
            wdb = PE(nc.sbuf_tensor("wdb6", [128, 2, 32, 128], BF16))
            x1t = PE(nc.sbuf_tensor("x1t6", [128, 2, 1024], F32))
            x2b = PE(nc.sbuf_tensor("x2b6", [128, 2, 1024], F32))
            pD1 = PE(nc.psum_tensor("pD16", [128, 1024], F32))
            pD2 = PE(nc.psum_tensor("pD26", [128, 1024], F32))
            MT6 = b.slot_sems("MT6", 1)
            WD6 = b.slot_sems("WD6", 2)
            XT6 = b.slot_sems("XT6", 2)
            X26 = b.slot_sems("X26", 2)
            te_wd = [0, 0]
            ve_add = [0, 0]
            st_x2 = [0, 0]
            wdv = ap["w_down"].rearrange("(i r) d -> r i d", r=128)
            mscv = scr["mscr"].ap()
            for tg in range(2):
                b.wait("sync", TE, TE.n)
                b.dma("sync", mb6[:, :, :],
                      mscv[:, :, tg * 1024:(tg + 1) * 1024].rearrange("i p t -> p i t"),
                      MT6[0])
                b.wait("tensor", MT6[0], MT6[0].n)
                for d_ in range(16):
                    dsl = d_ % 2
                    pDx = pD1 if dsl == 0 else pD2
                    if ve_add[dsl]:
                        b.wait("gpsimd", VE, ve_add[dsl])
                    b.dma("gpsimd", x1t[:, dsl, :],
                          x1v[:, d_, THALO + tg * 1024:THALO + (tg + 1) * 1024], XT6[dsl])
                    b.wait("gpsimd", XT6[dsl], XT6[dsl].n)
                    GP.dma_start(out=x1t[:, dsl, :],
                                 in_=xiv[:, d_, tg * 1024:(tg + 1) * 1024],
                                 accum_op=ALU.add).then_inc(XT6[dsl].h, 16)
                    XT6[dsl].n += 16
                    for ih in range(2):
                        wsl = (2 * d_ + ih) % 2
                        if te_wd[wsl]:
                            b.wait("gpsimd", TE, te_wd[wsl])
                        b.dma("gpsimd", wdb[:, wsl],
                              wdv[:, ih * 32:(ih + 1) * 32, d_ * 128:(d_ + 1) * 128],
                              WD6[wsl])
                        b.wait("tensor", WD6[wsl], WD6[wsl].n)
                        if ih == 0 and ve_add[dsl]:
                            b.wait("tensor", VE, ve_add[dsl])
                        for i in range(32):
                            for gg_ in range(2):
                                TN.matmul(pDx[:, gg_ * 512:(gg_ + 1) * 512],
                                          wdb[:, wsl, i, :],
                                          mb6[:, ih * 32 + i, gg_ * 512:(gg_ + 1) * 512],
                                          start=(ih == 0 and i == 0),
                                          stop=(ih == 1 and i == 31))
                        te_wd[wsl] = b.inc("tensor", TE)
                    b.wait("vector", TE, te_wd[(2 * d_ + 1) % 2])
                    b.wait("vector", XT6[dsl], XT6[dsl].n)
                    if st_x2[dsl]:
                        b.wait("vector", X26[dsl], st_x2[dsl])
                    VR.tensor_add(x2b[:, dsl, :], x1t[:, dsl, :], pDx[:, 0:1024])
                    ve_add[dsl] = b.inc("vector", VE)
                    b.wait("scalar", VE, ve_add[dsl])
                    st_x2[dsl] = b.dma("scalar", x2v[:, d_, tg * 1024:(tg + 1) * 1024],
                                       x2b[:, dsl, :], X26[dsl])
            b.barrier()

    if "p1" in PHASES:
        ffn_phase("p1")
    if "p2" in PHASES or "p3" in PHASES or "p4" in PHASES:
        with ExitStack() as p24:
            g_res = p24.enter_context(nc.sbuf_tensor("g_res", [16, TOWN], F32))
            beta_res = p24.enter_context(nc.sbuf_tensor("beta_res", [16, TOWN], F32))
            hga_res = p24.enter_context(nc.sbuf_tensor("hga_res", [128, TOWN], BF16))
            if "p2" in PHASES:
                proj_phase(g_res, beta_res, hga_res)
            if "p3" in PHASES:
                scan_phase(g_res, beta_res)
            if "p4" in PHASES:
                owo_phase(hga_res)
                if DBG_OUT:
                    b.dma("scalar", scr["g_dbg"].ap(), g_res[:, :], ST)
                    b.dma("scalar", scr["beta_dbg"].ap(), beta_res[:, :], ST)
                    b.dma("scalar", scr["hga_dbg"].ap(), hga_res[:, :], ST)
                    b.wait("scalar", ST)
    if "p5" in PHASES:
        mlp_up_phase()
    if "p6" in PHASES:
        mlp_down_phase()
    if "p7" in PHASES:
        ffn_phase("p7")

    b.wait("gpsimd", ST)
    for name, out_ap in dbg_outs.items():
        b.dma("gpsimd", out_ap, scr[name].ap(), CA)
    b.wait("gpsimd", CA)
    b.wait("scalar", ST)

    b.finalize_hazards()
    with nc.Block() as block:
        block.sync(lambda e: b.r["sync"].replay(e))
        block.scalar(lambda e: b.r["scalar"].replay(e))
        block.tensor(lambda e: b.r["tensor"].replay(e))
        block.vector(lambda e: b.r["vector"].replay(e))
        block.gpsimd(lambda e: b.r["gpsimd"].replay(e))

    stack.close()
    return nc


def _consts():
    c = {}
    c["identF"] = np.eye(128, dtype=np.float32)
    c["identB"] = np.eye(128, dtype=np.float32).astype(ml_dtypes.bfloat16)
    c["onesColF"] = np.ones((128, 1), np.float32)
    c["onesRowF"] = np.ones((1, 128), np.float32)
    c["ones16"] = np.ones((16, CCH), np.float32)
    s = np.arange(128)[:, None]
    t = np.arange(128)[None, :]
    c["mUIT"] = np.where(t <= s, 0.0, -1e9).astype(ml_dtypes.bfloat16)
    c["mLST"] = np.where(s < t, 0.0, -1e9).astype(ml_dtypes.bfloat16)
    sm = np.zeros((16, 4, 512), np.float32)
    for g in range(4):
        for j in range(4):
            sm[4 * g + j, g, 128 * j:128 * (j + 1)] = 1.0
    c["selm4"] = sm
    c["Ib4"] = np.tile(np.eye(128, dtype=np.float32), (1, 4)).astype(ml_dtypes.bfloat16)
    c["twoIb4"] = np.tile(2.0 * np.eye(128, dtype=np.float32), (1, 4)).copy()
    e8 = np.zeros((8, 128), np.float32)
    for e in range(8):
        e8[e, 16 * e:16 * e + 16] = 1.0
    c["E8"] = e8.astype(ml_dtypes.bfloat16)
    c["sel16"] = np.kron(np.eye(16, dtype=np.float32), np.ones((1, 128), np.float32))
    sel4 = np.zeros((4, 4, 128), np.float32)
    for r in range(4):
        sel4[r, r, :] = 1.0
    c["sel4"] = sel4
    return c


_NC_CACHE = {}


def kernel(**inputs):
    key = (tuple(PHASES), tuple(DBG_OUT))
    if key not in _NC_CACHE:
        _NC_CACHE[key] = build_program()
    nc = _NC_CACHE[key]

    x = np.ascontiguousarray(np.asarray(inputs["L0_attn_x"], np.float32))
    Bsz, T, _ = x.shape
    common = _consts()
    ren = {
        "wq": "L0_attn_wq", "wk": "L0_attn_wk", "wv": "L0_attn_wv", "wo": "L0_attn_wo",
        "wfa": "L0_attn_wfa", "wfb": "L0_attn_wfb", "wb": "L0_attn_wb",
        "wga": "L0_attn_wga", "wgb": "L0_attn_wgb",
        "w_gate": "L0_attn_w_gate", "w_up": "L0_attn_w_up", "w_down": "L0_attn_w_down",
    }
    for k, src in ren.items():
        common[k] = np.ascontiguousarray(np.asarray(inputs[src], np.float32))
    common["a_log"] = np.asarray(inputs["L0_attn_a_log"], np.float32).reshape(16, 1).copy()
    common["dt_bias"] = np.asarray(inputs["L0_attn_dt_bias"], np.float32).reshape(16, 1).copy()
    common["o_norm_w"] = np.asarray(inputs["L0_attn_o_norm_w"], np.float32).reshape(1, 128).copy()
    onw4 = np.zeros((4, 4, 128), np.float32)
    for r in range(4):
        onw4[r, r, :] = common["o_norm_w"][0]
    common["onw4"] = onw4
    common["ln1_wT"] = np.ascontiguousarray(
        np.asarray(inputs["L0_attn_ln1_w"], np.float32).reshape(16, 128).T)
    common["ln2_wT"] = np.ascontiguousarray(
        np.asarray(inputs["L0_attn_ln2_w"], np.float32).reshape(16, 128).T)
    for cn, src in (("conv_qT", "L0_attn_conv_q"), ("conv_kT", "L0_attn_conv_k"),
                    ("conv_vT", "L0_attn_conv_v")):
        cw = np.asarray(inputs[src], np.float32)
        common[cn] = np.ascontiguousarray(
            cw.reshape(4, 16, 128).transpose(2, 1, 0).reshape(128, 64))
    common["fwT"] = np.ascontiguousarray(np.asarray(inputs["L0_ffn_weight"], np.float32).T)
    common["gate_wT"] = np.ascontiguousarray(np.asarray(inputs["L0_ffn_gate_w"], np.float32).T)
    common["A_wT"] = np.ascontiguousarray(
        np.asarray(inputs["L0_ffn_A_w"], np.float32).reshape(128, D).T)
    common["B_wRT"] = np.ascontiguousarray(
        np.asarray(inputs["L0_ffn_B_w"], np.float32).transpose(0, 2, 1).reshape(128, D))

    in_maps = []
    for c in range(8):
        bidx, half = c >> 1, c & 1
        m = dict(common)
        m["x_own"] = np.ascontiguousarray(x[bidx, half * TOWN:(half + 1) * TOWN])
        m["x_halo"] = np.ascontiguousarray(
            x[bidx, TOWN - THALO:TOWN] if half == 1 else x[bidx, 0:THALO])
        m["halo_mask"] = np.full((128, 1), float(half), np.float32)
        m["mA"] = np.full((128, 1), float(half), np.float32)
        m["mB"] = np.zeros((128, 1), np.float32)
        in_maps.append(m)

    res = run_bass_kernel_spmd(nc, in_maps, core_ids=list(range(8)))
    out = np.empty((Bsz, T, D), np.float32)
    for c in range(8):
        bidx, half = c >> 1, c & 1
        out[bidx, half * TOWN:(half + 1) * TOWN] = res.results[c]["y"]
    if DBG_OUT:
        kernel.dbg = [{name: r["dbg_" + name] for name in DBG_OUT} for r in res.results]
    return out
